# version 02
# baseline (speedup 1.0000x reference)
import contextlib
import math
import numpy as np
import ml_dtypes

import concourse.bass as bass
import concourse.mybir as mybir
from concourse.bass_utils import run_bass_kernel_spmd

F32 = mybir.dt.float32
BF16 = mybir.dt.bfloat16
AF = mybir.ActivationFunctionType
ALU = mybir.AluOpType
AX = mybir.AxisListType
BF = ml_dtypes.bfloat16

B, N, D = 4, 8192, 1024
NCORES = 8
T_OWN = 4096
HALO = 1664
TC = T_OWN + HALO
OFFSETS = sorted(set(range(0, 33)) | {48, 64, 96, 128, 192, 256, 384, 512, 768, 1024, 1536})
SLOT_H, SLOT_HD = 16, 64
SLOT_BLOCKS = [0, 1, 2, 3, 4, 6, 8, 12]
FFN_H, FFN_HD = 8, 64
WIN = 512
HID = 2730
HIDP = 2816
NG = TC // 512


def cdiv(a, b):
    return (a + b - 1) // b


class Sem:

    def __init__(self, es, nc, name):
        self.h = es.enter_context(nc.semaphore(name))
        self.n = 0

    def inc(self, k=1):
        self.n += k
        return self.n


class Rec:

    def __init__(self):
        self.ops = {"sync": [], "tensor": [], "vector": [], "scalar": [], "gpsimd": []}

    def add(self, engine, fn, waits=(), incs=()):
        for sem, v in waits:
            rec_n = getattr(sem, "recorded", 0)
            assert v <= rec_n, (
                f"wait on {sem.h} for {v} but only {rec_n} incs recorded so far "
                f"(engine {engine}, op #{len(self.ops[engine])})")
        self.ops[engine].append((fn, list(waits), list(incs)))
        for sem, amount in incs:
            sem.recorded = getattr(sem, "recorded", 0) + amount

    @staticmethod
    def check(ctx):
        for name, sem in ctx.sems.items():
            rec_n = getattr(sem, "recorded", 0)
            assert rec_n == sem.n, (
                f"sem {name}: recorded incs {rec_n} != counter {sem.n}")

    def replay(self, engine_name, barrier=()):
        def runner(eng):
            first = True
            for fn, waits, incs in self.ops[engine_name]:
                if first:
                    for sem, val in barrier:
                        eng.wait_ge(sem.h, val)
                    first = False
                for sem, val in waits:
                    eng.wait_ge(sem.h, val)
                inst = fn(eng)
                for sem, val in incs:
                    inst.then_inc(sem.h, val)
        return runner


def finish(nc, rec, es, ctx=None, barrier=()):
    if ctx is not None:
        Rec.check(ctx)
    block = es.enter_context(nc.Block())
    for name in ("sync", "tensor", "vector", "scalar", "gpsimd"):
        if rec.ops[name]:
            getattr(block, name)(rec.replay(name, barrier))


CONSTS = {
    "one": 1.0, "neg1": -1.0, "inv1024": 1.0 / 1024.0, "eps": 1e-5,
    "tiny": 1e-30, "eighth": 0.125, "half": 0.5, "clip": 1e-12, "clip2": 1e-24,
}


class Ctx:

    def __init__(self, name, nc=None, sem_es=None, sems=None):
        self.nc = nc if nc is not None else bass.Bass("TRN2", target_bir_lowering=False)
        self.prefix = name + "_" if nc is not None else ""
        self.es = contextlib.ExitStack()
        self.sem_es = sem_es if sem_es is not None else self.es
        self.rec = Rec()
        self.sems = sems if sems is not None else {}
        self._sb = {}

    def sem(self, name):
        if name not in self.sems:
            self.sems[name] = Sem(self.sem_es, self.nc, name)
        return self.sems[name]

    def sbuf(self, name, shape, dtype):
        t = self.es.enter_context(self.nc.sbuf_tensor(self.prefix + name, list(shape), dtype))
        self._sb[name] = t
        return t

    def psum(self, name, shape, dtype=F32):
        return self.es.enter_context(self.nc.psum_tensor(self.prefix + name, list(shape), dtype))

    def final_sems(self):
        return [(s, s.n) for s in self.sems.values() if s.n > 0]

    def setup_consts(self):
        names = list(CONSTS)
        self.cst = self.sbuf("csts", [128, len(names)], F32)
        self.cidx = {n: i for i, n in enumerate(names)}
        s = self.sem("cst_ready")
        for n in names:
            i = self.cidx[n]
            v = CONSTS[n]
            self.rec.add("vector", lambda e, i=i, v=v: e.memset(self.cst[:, i:i + 1], v),
                         incs=[(s, 1)])
        self.cst_target = s.inc(len(names))

    def c(self, name):
        i = self.cidx[name]
        return self.cst[:, i:i + 1]


def build_identity_bf16(ctx, ident_h=None):
    if ident_h is None:
        ident_h = ctx.nc.dram_tensor("ident", [128, 128], F32, kind="ExternalInput").ap()
    idn = ctx.sbuf("idn", [128, 128], BF16)
    s = ctx.sem("ident_rdy")
    ctx.rec.add("gpsimd", lambda e: e.dma_start(out=idn[:], in_=ident_h),
                incs=[(s, 16)])
    s.inc(16)
    return idn


def emit_ln(ctx, io):
    nc = ctx.nc
    rec = ctx.rec
    x_h = io["x"]
    xn_tm_h = io["xn_tm"]
    xn_fm_h = io["xn_fm"]
    if True:
        ctx.setup_consts()
        idn = build_identity_bf16(ctx, io.get("ident"))
        rec = ctx.rec
        xt = [ctx.sbuf(f"xt{i}", [128, D], F32) for i in range(2)]
        xnt = [ctx.sbuf(f"xnt{i}", [128, D], BF16) for i in range(2)]
        fmt = [ctx.sbuf(f"fmt{i}", [128, D], BF16) for i in range(2)]
        stats = ctx.sbuf("stats", [128, 8], F32)
        scratch = ctx.sbuf("scr", [128, D], F32)
        tp = [ctx.psum(f"tp{i}", [128, 512], BF16) for i in range(2)]

        ld = ctx.sem("ld"); st = ctx.sem("st"); vdone = ctx.sem("vd")
        sdone = ctx.sem("sd"); tdone = ctx.sem("td"); ev = ctx.sem("ev")

        ntiles = TC // 128
        ttr_done = [0] * ntiles
        aff_done = [0] * ntiles
        xn_cons = [0] * ntiles
        xn_tr = [0] * ntiles
        fm_st = [0] * ntiles
        ev_done = [[0] * ntiles, [0] * ntiles]

        for i in range(ntiles):
            p = i % 2
            x_sb, xn_sb, fm_sb = xt[p], xnt[p], fmt[p]
            w = []
            if i >= 2:
                w = [(vdone, ttr_done[i - 2]), (sdone, aff_done[i - 2])]
            rec.add("sync", lambda e, i=i, x_sb=x_sb: e.dma_start(
                out=x_sb[:], in_=x_h[i * 128:(i + 1) * 128, :]), waits=w, incs=[(ld, 16)])
            ldv = ld.inc(16)

            rec.add("vector", lambda e, x_sb=x_sb: e.reduce_sum(
                stats[:, 0:1], x_sb[:], AX.X), waits=[(ld, ldv), (ctx.sem("cst_ready"), ctx.cst_target)])
            rec.add("scalar", lambda e, x_sb=x_sb: e.activation(
                scratch[:], x_sb[:], AF.Square), waits=[(ld, ldv)], incs=[(sdone, 1)])
            s_sq = sdone.inc()
            rec.add("vector", lambda e: e.reduce_sum(
                stats[:, 1:2], scratch[:], AX.X), waits=[(sdone, s_sq)], incs=[(vdone, 1)])
            ttr_done[i] = vdone.inc()
            rec.add("vector", lambda e: e.tensor_scalar(
                stats[:, 2:3], stats[:, 0:1], ctx.c("inv1024"), None, ALU.mult), incs=[(vdone, 1)])
            v_mu = vdone.inc()
            rec.add("vector", lambda e: e.tensor_scalar(
                stats[:, 3:4], stats[:, 1:2], ctx.c("inv1024"), None, ALU.mult))
            rec.add("vector", lambda e: e.scalar_tensor_tensor(
                stats[:, 4:5], stats[:, 2:3], stats[:, 2:3], stats[:, 3:4],
                ALU.mult, ALU.subtract), waits=[(vdone, v_mu)], incs=[(vdone, 1)])
            v_t4 = vdone.inc()
            rec.add("scalar", lambda e: e.activation(
                stats[:, 5:6], stats[:, 4:5], AF.Sqrt, bias=ctx.c("eps"),
                scale=ctx.c("neg1")), waits=[(vdone, v_t4)], incs=[(sdone, 1)])
            s_std = sdone.inc()
            rec.add("vector", lambda e: e.reciprocal(
                stats[:, 6:7], stats[:, 5:6]), waits=[(sdone, s_std)], incs=[(vdone, 1)])
            v_rec = vdone.inc()
            rec.add("vector", lambda e: e.tensor_scalar(
                stats[:, 7:8], stats[:, 2:3], stats[:, 6:7], ctx.c("neg1"),
                ALU.mult, ALU.mult), waits=[(vdone, v_rec)], incs=[(vdone, 1)])
            v_mb = vdone.inc()
            w = [(vdone, v_mb)]
            if i >= 2:
                w += [(st, xn_cons[i - 2]), (tdone, xn_tr[i - 2])]
            rec.add("scalar", lambda e, x_sb=x_sb, xn_sb=xn_sb: e.activation(
                xn_sb[:], x_sb[:], AF.Identity, bias=stats[:, 7:8], scale=stats[:, 6:7]),
                waits=w, incs=[(sdone, 1)])
            s_aff = sdone.inc()
            aff_done[i] = s_aff
            rec.add("sync", lambda e, i=i, xn_sb=xn_sb: e.dma_start(
                out=xn_tm_h[i * 128:(i + 1) * 128, :], in_=xn_sb[:]),
                waits=[(sdone, s_aff)], incs=[(st, 16)])
            xn_cons[i] = st.inc(16)
            for half in range(2):
                pb = tp[half]
                for j in range(4):
                    c = half * 4 + j
                    wv = []
                    if j == 0:
                        wv.append((sdone, s_aff))
                        if i >= 1:
                            wv.append((ev, ev_done[half][i - 1]))
                    rec.add("tensor", lambda e, xn_sb=xn_sb, c=c, pb=pb, j=j: e.transpose(
                        pb[:, j * 128:(j + 1) * 128], xn_sb[:, c * 128:(c + 1) * 128], idn[:]),
                        waits=wv, incs=[(tdone, 1)] if j == 3 else [])
                td = tdone.inc()
                wv = [(tdone, td)]
                if i >= 2 and half == 0:
                    wv.append((st, fm_st[i - 2]))
                rec.add("vector", lambda e, pb=pb, fm_sb=fm_sb, half=half: e.tensor_copy(
                    fm_sb[:, half * 512:(half + 1) * 512], pb[:]),
                    waits=wv, incs=[(ev, 1)])
                ev_done[half][i] = ev.inc()
            xn_tr[i] = tdone.n
            rec.add("sync", lambda e, i=i, fm_sb=fm_sb: e.dma_start(
                out=xn_fm_h[:, i * 128:(i + 1) * 128].rearrange("(c f) t -> f c t", f=128),
                in_=fm_sb[:].rearrange("f (c t) -> f c t", t=128)),
                waits=[(ev, ev_done[1][i])], incs=[(st, 16)])
            fm_st[i] = st.inc(16)
        rec.add("sync", lambda e: e.nop(), waits=[(st, st.n)])

def build_ln_phase():
    ctx = Ctx("emit_ln")
    nc = ctx.nc
    nc = ctx.nc
    x_h = nc.dram_tensor("x", [TC, D], F32, kind="ExternalInput").ap()
    xn_tm_h = nc.dram_tensor("xn_tm", [TC, D], BF16, kind="ExternalOutput").ap()
    xn_fm_h = nc.dram_tensor("xn_fm", [D, TC], BF16, kind="ExternalOutput").ap()

    io = {
        "x": x_h,
        "xn_tm": xn_tm_h,
        "xn_fm": xn_fm_h,
    }
    with ctx.es:
        emit_ln(ctx, io)
        finish(ctx.nc, ctx.rec, ctx.es, ctx)
    return ctx.nc


def _ttiles(total, span=512):
    out = []
    t = 0
    while t < total:
        s = min(span, total - t)
        out.append((t, s))
        t += s
    return out


def emit_slot_qkv(ctx, io):
    nc = ctx.nc
    rec = ctx.rec
    xn_fm_h = io["xn_fm"]
    w_h = io["Wqkv"]
    b_h = io["bqkv"]
    q_fm_h = io["q_fm"]
    k_fm_h = io["k_fm"]
    v65_h = io["v65"]
    if True:
        ctx.setup_consts()
        rec = ctx.rec
        wq = ctx.sbuf("wq", [128, 8, 3 * D], BF16)
        bqc = ctx.sbuf("bqc", [128, 24], F32)
        bqrow = ctx.sbuf("bqrow", [1, D], F32)
        ones_row = ctx.sbuf("ones_row", [1, 128], F32)
        xn_sb = [ctx.sbuf(f"xn{i}", [128, 8, 512], BF16) for i in range(2)]
        stage = [ctx.sbuf(f"stg{i}", [128, 512], BF16) for i in range(2)]
        v_sb = [ctx.sbuf(f"vsb{i}", [128, SLOT_H, 65], BF16) for i in range(2)]
        ps = [ctx.psum(f"ps{i}", [128, 512]) for i in range(2)]

        ld = ctx.sem("ld"); wrdy = ctx.sem("wrdy"); st = ctx.sem("st")
        td = ctx.sem("td"); ev = ctx.sem("ev")

        rec.add("gpsimd", lambda e: e.dma_start(
            out=wq[:], in_=w_h.rearrange("(c p) e -> p c e", p=128)), incs=[(wrdy, 16)])
        rec.add("sync", lambda e: e.dma_start(
            out=bqc[:], in_=b_h.rearrange("(c p) -> p c", p=128),
            allow_slow_non_contiguous=True), incs=[(wrdy, 16)])
        rec.add("sync", lambda e: e.dma_start(
            out=bqrow[:], in_=b_h[2048:3072].rearrange("(o e) -> o e", o=1)), incs=[(wrdy, 16)])
        wv = wrdy.inc(48)
        rec.add("vector", lambda e: e.memset(ones_row[:], 1.0))
        rec.add("vector", lambda e: e.tensor_scalar(
            wq[:, :, 0:1024], wq[:, :, 0:1024], ctx.c("eighth"), None, ALU.mult),
            waits=[(wrdy, wv), (ctx.sem("cst_ready"), ctx.cst_target)])
        rec.add("vector", lambda e: e.tensor_scalar(
            bqc[:, 0:8], bqc[:, 0:8], ctx.c("eighth"), None, ALU.mult), incs=[(wrdy, 1)])
        wv = wrdy.inc()

        tt = _ttiles(TC, 512)
        ev_after = {}
        ld_after = {}
        for it, (t0, sp) in enumerate(tt):
            p = it % 2
            xs = xn_sb[p]
            w = [(wrdy, wv)]
            if it >= 2:
                w.append((td, ld_after[it - 2]))
            rec.add("sync", lambda e, t0=t0, sp=sp, xs=xs: e.dma_start(
                out=xs[:, :, 0:sp],
                in_=xn_fm_h[:, t0:t0 + sp].rearrange("(c p) t -> p c t", p=128)),
                waits=w, incs=[(ld, 16)])
            lv = ld.inc(16)
            for m in range(16):
                pp = ps[m % 2]
                wm = [(ld, lv)]
                if it > 0 or m >= 2:
                    wm.append((ev, ev_after[(it, m - 2) if m >= 2 else (it - 1, 14 + m)]))
                for c in range(8):
                    rec.add("tensor", lambda e, pp=pp, m=m, c=c, xs=xs, sp=sp: e.matmul(
                        pp[:, 0:sp], wq[:, c, m * 128:(m + 1) * 128], xs[:, c, 0:sp],
                        start=(c == 0), stop=(c == 7)),
                        waits=wm if c == 0 else [], incs=[(td, 1)] if c == 7 else [])
                tv = td.inc()
                if m == 15:
                    ld_after[it] = tv
                stg = stage[m % 2]
                rec.add("vector", lambda e, pp=pp, m=m, stg=stg, sp=sp: e.tensor_scalar(
                    stg[:, 0:sp], pp[:, 0:sp], bqc[:, m:m + 1], None, ALU.add),
                    waits=[(td, tv)] + ([(st, ev_after.get(("st", it, m - 2),
                           ev_after.get(("st", it - 1, 14 + m), 0)))] if (it > 0 or m >= 2) else []),
                    incs=[(ev, 1)])
                evv = ev.inc()
                ev_after[(it, m)] = evv
                dst = q_fm_h if m < 8 else k_fm_h
                mm = m % 8
                rec.add("sync", lambda e, dst=dst, mm=mm, t0=t0, sp=sp, stg=stg: e.dma_start(
                    out=dst[mm * 128:(mm + 1) * 128, t0:t0 + sp], in_=stg[:, 0:sp]),
                    waits=[(ev, evv)], incs=[(st, 16)])
                ev_after[("st", it, m)] = st.inc(16)

        for i in range(TC // 128):
            p = i % 2
            xs, vs = xn_sb[p], v_sb[p]
            w = []
            if i >= 2:
                w.append((td, ld_after[("v", i - 2)]))
            else:
                w.append((td, ld_after[len(_ttiles(TC, 512)) - 1]))
            rec.add("sync", lambda e, i=i, xs=xs: e.dma_start(
                out=xs[:, :, 0:128],
                in_=xn_fm_h[:, i * 128:(i + 1) * 128].rearrange("(c p) t -> p c t", p=128)),
                waits=w, incs=[(ld, 16)])
            lv = ld.inc(16)
            if i < 2:
                rec.add("vector", lambda e, vs=vs: e.memset(vs[:, :, 64:65], 1.0))
            for half in range(2):
                pp = ps[half]
                wm = [(ld, lv)]
                if i > 0:
                    wm.append((ev, ev_after[("v", i - 1, half)]))
                else:
                    last_it = len(_ttiles(TC, 512)) - 1
                    wm.append((ev, ev_after[(last_it, 14 + half)]))
                for c in range(8):
                    rec.add("tensor", lambda e, pp=pp, c=c, half=half, xs=xs: e.matmul(
                        pp[:], xs[:, c, 0:128], wq[:, c, 2048 + half * 512: 2048 + (half + 1) * 512],
                        start=(c == 0), stop=False), waits=wm if c == 0 else [])
                rec.add("tensor", lambda e, pp=pp, half=half: e.matmul(
                    pp[:], ones_row[:], bqrow[:, half * 512:(half + 1) * 512],
                    start=False, stop=True), incs=[(td, 1)])
                tv = td.inc()
                if half == 1:
                    ld_after[("v", i)] = tv
                wv3 = [(td, tv)]
                if i >= 2:
                    wv3.append((st, ev_after[("vst", i - 2)]))
                rec.add("vector", lambda e, pp=pp, vs=vs, half=half: e.tensor_copy(
                    vs[:, half * 8:(half + 1) * 8, 0:64],
                    pp[:].rearrange("p (h e) -> p h e", e=64)), waits=wv3, incs=[(ev, 1)])
                ev_after[("v", i, half)] = ev.inc()
            rec.add("sync", lambda e, i=i, vs=vs: e.dma_start(
                out=v65_h[i * 128:(i + 1) * 128, :], in_=vs[:].rearrange("p h e -> p (h e)")),
                waits=[(ev, ev_after[("v", i, 1)])], incs=[(st, 16)])
            ev_after[("vst", i)] = st.inc(16)
        rec.add("sync", lambda e: e.nop(), waits=[(st, st.n)])

def build_slot_qkv_phase():
    ctx = Ctx("emit_slot_qkv")
    nc = ctx.nc
    nc = ctx.nc
    xn_fm_h = nc.dram_tensor("xn_fm", [D, TC], BF16, kind="ExternalInput").ap()
    w_h = nc.dram_tensor("Wqkv", [D, 3 * D], F32, kind="ExternalInput").ap()
    b_h = nc.dram_tensor("bqkv", [3 * D], F32, kind="ExternalInput").ap()
    q_fm_h = nc.dram_tensor("q_fm", [D, TC], BF16, kind="ExternalOutput").ap()
    k_fm_h = nc.dram_tensor("k_fm", [D, TC], BF16, kind="ExternalOutput").ap()
    v65_h = nc.dram_tensor("v65", [TC, SLOT_H * 65], BF16, kind="ExternalOutput").ap()

    io = {
        "xn_fm": xn_fm_h,
        "Wqkv": w_h,
        "bqkv": b_h,
        "q_fm": q_fm_h,
        "k_fm": k_fm_h,
        "v65": v65_h,
    }
    with ctx.es:
        emit_slot_qkv(ctx, io)
        finish(ctx.nc, ctx.rec, ctx.es, ctx)
    return ctx.nc


NPOS_NEAR = 128 * 384
NPOS = NPOS_NEAR + 4096
FAR_KS = [3, 4, 6, 8, 12]


def make_slot_mconst():
    oi = {d: i for i, d in enumerate(OFFSETS)}
    M = np.zeros((44, NPOS), dtype=np.float32)
    j = np.arange(128)[:, None]
    c = np.arange(128)[None, :]
    for kidx in range(3):
        delta = 128 * kidx + c - j
        for d, i in oi.items():
            jj, cc = np.nonzero(delta == d)
            M[i, jj * 384 + kidx * 128 + cc] = 1.0
    for f, k in enumerate(FAR_KS):
        M[oi[128 * k], NPOS_NEAR + f] = 1.0
    return M.astype(BF)


def emit_slot_attn(ctx, io, dbg_tile=None, dbg_head=0):
    nc = ctx.nc
    rec = ctx.rec
    q_h = io["q_fm"]
    k_h = io["k_fm"]
    v_h = io["v65"]
    pb_h = io["pos_bias"]
    mc_h = io["mconst"]
    msk_h = io["kvmask"]
    flat_h = io["flat"]
    ball_h = io["ball_scratch"]
    dbgS_h = io.get("dbgS")
    dbgq_h = io.get("dbgq")
    dbgk_h = io.get("dbgk")
    dbgE_h = io.get("dbgE")
    dbgG_h = io.get("dbgG")
    dbgO_h = io.get("dbgO")
    dbgB_h = io.get("dbgB")
    if True:
        ctx.setup_consts()
        idn = build_identity_bf16(ctx, io.get("ident"))
        rec = ctx.rec
        ld = ctx.sem("ld"); st = ctx.sem("st"); td = ctx.sem("td")
        vd = ctx.sem("vd"); sd = ctx.sem("sd"); ev = ctx.sem("ev")
        gld = ctx.sem("gld")

        pb_sb = ctx.sbuf("pb", [44, 16], F32)
        epb = ctx.sbuf("epb", [44, 16], BF16)
        MCW = 4096
        mc_sb = [ctx.sbuf(f"mc{j}", [44, MCW], BF16) for j in range(2)]
        bst = ctx.sbuf("bst", [16, 512], F32)
        bps = ctx.psum("bps", [16, 512])
        rec.add("sync", lambda e: e.dma_start(out=pb_sb[:], in_=pb_h), incs=[(ld, 16)])
        lv0 = ld.inc(16)
        rec.add("scalar", lambda e: e.activation(epb[:], pb_sb[:], AF.Exp),
                waits=[(ld, lv0)], incs=[(sd, 1)])
        sd_epb = sd.inc()
        mm_done = {}
        for j in range(NPOS // MCW):
            mcb = mc_sb[j % 2]
            w = [(td, mm_done[j - 2])] if j >= 2 else []
            rec.add("gpsimd", lambda e, j=j, mcb=mcb: e.dma_start(
                out=mcb[:], in_=mc_h[:, j * MCW:(j + 1) * MCW]), waits=w, incs=[(gld, 16)])
            lvj = gld.inc(16)
            for i2 in range(MCW // 512):
                i = j * (MCW // 512) + i2
                rec.add("tensor", lambda e, i2=i2, mcb=mcb: e.matmul(
                    bps[:], epb[:], mcb[:, i2 * 512:(i2 + 1) * 512], start=True, stop=True),
                    waits=[(sd, sd_epb), (gld, lvj)] + ([(ev, ctx.sems["ev"].n)] if i >= 1 else []),
                    incs=[(td, 1)])
                tv = td.inc()
                mm_done[j] = tv
                rec.add("vector", lambda e: e.tensor_copy(bst[:], bps[:]),
                        waits=[(td, tv)] + ([(st, ctx.sems["st"].n)] if i >= 1 else []),
                        incs=[(ev, 1)])
                evv = ev.inc()
                rec.add("sync", lambda e, i=i: e.dma_start(
                    out=ball_h[:, i * 512:(i + 1) * 512], in_=bst[:]),
                    waits=[(ev, evv)], incs=[(st, 16)])
                st.inc(16)
        bnear = ctx.sbuf("bnear", [128, 16, 384], BF16)
        pbf = ctx.sbuf("pbf", [128, 16, len(FAR_KS)], F32)
        wb = [(st, st.n)]
        for h in range(16):
            rec.add("gpsimd", lambda e, h=h: e.dma_start(
                out=bnear[:, h, :],
                in_=ball_h[h, 0:NPOS_NEAR].rearrange("(j c) -> j c", j=128)),
                waits=wb if h == 0 else [], incs=[(gld, 16)])
            gld.inc(16)
        rec.add("sync", lambda e: e.dma_start(
            out=pbf[:],
            in_=ball_h[:, NPOS_NEAR:NPOS_NEAR + len(FAR_KS)].unsqueeze(0).broadcast_to(
                [128, 16, len(FAR_KS)])), waits=wb, incs=[(ld, 16)])
        ld.inc(16)
        mask_sb = ctx.sbuf("masks", [128, TC // 128], F32)
        rec.add("sync", lambda e: e.dma_start(
            out=mask_sb[:], in_=msk_h.rearrange("(i p) -> p i", p=128),
            allow_slow_non_contiguous=True), incs=[(ld, 16)])
        prep_ld = ld.inc(16)
        prep_gld = gld.n

        q_sb = [ctx.sbuf(f"qsb{i}", [64, 16, 128], BF16) for i in range(2)]
        k_sb = [ctx.sbuf(f"ksb{i}", [64, 16, 8 * 128], BF16) for i in range(2)]
        v_sb = [ctx.sbuf(f"vsb{i}", [128, 8, SLOT_H * 65], BF16) for i in range(2)]
        E_sb = [ctx.sbuf(f"esb{i}", [128, 1024], BF16) for i in range(2)]
        G_sb = [ctx.sbuf(f"gsb{i}", [128, 1024], BF16) for i in range(2)]
        o_stage = ctx.sbuf("ostg", [128, SLOT_H, 65], F32)
        zcol = ctx.sbuf("zcol", [128, 2 * SLOT_H], F32)
        tmpc = ctx.sbuf("tmpc", [128, 8], F32)
        flat_sb = [ctx.sbuf(f"flt{i}", [128, D], BF16) for i in range(2)]
        ps_e = [ctx.psum(f"pse{i}", [128, 1024]) for i in range(2)]
        o_ps = [ctx.psum(f"ops{i}", [128, 512]) for i in range(2)]

        tiles = list(range(HALO // 128, TC // 128))
        dbg_barrier = [0]
        tile_mm_done = {}
        tile_pv_done = {}
        exp_done = {}
        g_done = {}
        oev_done = {}
        flat_st = {}
        for ti, i in enumerate(tiles):
            p = ti % 2
            qs, ks, vs = q_sb[p], k_sb[p], v_sb[p]
            blocks = [i - k for k in SLOT_BLOCKS]
            w = [(ld, prep_ld)] if ti == 0 else []
            if ti >= 2:
                w = [(td, tile_pv_done[ti - 2])]
            rec.add("sync", lambda e, i=i, qs=qs: e.dma_start(
                out=qs[:], in_=q_h[:, i * 128:(i + 1) * 128].rearrange(
                    "(h p) t -> p h t", p=64)), waits=w, incs=[(ld, 16)])
            for kidx, bu in enumerate(blocks):
                rec.add("sync", lambda e, bu=bu, ks=ks, kidx=kidx: e.dma_start(
                    out=ks[:, :, kidx * 128:(kidx + 1) * 128],
                    in_=k_h[:, bu * 128:(bu + 1) * 128].rearrange(
                        "(h p) t -> p h t", p=64)), incs=[(ld, 16)])
                wg2 = w if kidx == 0 else []
                rec.add("gpsimd", lambda e, bu=bu, vs=vs, kidx=kidx: e.dma_start(
                    out=vs[:, kidx, :], in_=v_h[bu * 128:(bu + 1) * 128, :]),
                    waits=wg2, incs=[(gld, 16)])
            lv = ld.inc(16 * 9)
            lvg = gld.inc(16 * 8)
            for h in range(16):
                hp = h % 2
                pe = ps_e[hp]
            
                wm = [(ld, lv), (gld, lvg)]
                if (ti, h - 2) in exp_done:
                    wm.append((sd, exp_done[(ti, h - 2)]))
                elif ti > 0 and (ti - 1, 14 + h) in exp_done:
                    wm.append((sd, exp_done[(ti - 1, 14 + h)]))
                for kidx in range(8):
                    rec.add("tensor", lambda e, pe=pe, kidx=kidx, ks=ks, qs=qs, h=h: e.matmul(
                        pe[:, kidx * 128:(kidx + 1) * 128],
                        ks[:, h, kidx * 128:(kidx + 1) * 128],
                        qs[:, h, :], start=True, stop=True),
                        waits=wm if kidx == 0 else [], incs=[(td, 1)] if kidx == 7 else [])
                t_mm = td.inc()
                if h == 15:
                    tile_mm_done[ti] = t_mm
                Eb = E_sb[hp]
                we = [(td, t_mm)]
                if dbg_barrier[0]:
                    we.append((st, dbg_barrier[0]))
                if (ti, h - 2) in g_done:
                    we.append((vd, g_done[(ti, h - 2)]))
                elif ti > 0 and (ti - 1, 14 + h) in g_done:
                    we.append((vd, g_done[(ti - 1, 14 + h)]))
                rec.add("scalar", lambda e, pe=pe, Eb=Eb: e.activation(
                    Eb[:], pe[:], AF.Exp), waits=we, incs=[(sd, 1)])
                exp_done[(ti, h)] = sd.inc()
                Gb = G_sb[hp]
                wg = [(sd, exp_done[(ti, h)])]
                if (ti, h - 2) in tile_pv_done:
                    wg.append((td, tile_pv_done[(ti, h - 2)]))
                elif ti > 0 and (ti - 1, 14 + h) in tile_pv_done:
                    wg.append((td, tile_pv_done[(ti - 1, 14 + h)]))
                gops = []
                gops.append((lambda e, Eb=Eb, Gb=Gb, h=h: e.tensor_tensor(
                    Gb[:, 0:384], Eb[:, 0:384], bnear[:, h, :], ALU.mult), wg))
                for kidx in range(3):
                    bu = blocks[kidx]
                    if bu < HALO // 128:
                        gops.append((lambda e, Gb=Gb, kidx=kidx, bu=bu: e.tensor_scalar(
                            Gb[:, kidx * 128:(kidx + 1) * 128],
                            Gb[:, kidx * 128:(kidx + 1) * 128],
                            mask_sb[:, bu:bu + 1], None, ALU.mult), []))
                for f, kidx in enumerate(range(3, 8)):
                    bu = blocks[kidx]
                    if bu < HALO // 128:
                        rec.add("vector", lambda e, f=f, h=h, bu=bu: e.tensor_scalar(
                            tmpc[:, f:f + 1], pbf[:, h, f:f + 1], mask_sb[:, bu:bu + 1],
                            None, ALU.mult), waits=wg, incs=[(vd, 1)])
                        vv = vd.inc()
                        gops.append((lambda e, Gb=Gb, Eb=Eb, kidx=kidx, f=f: e.scalar_tensor_tensor(
                            Gb[:, kidx * 128:(kidx + 1) * 128],
                            Eb[:, kidx * 128:(kidx + 1) * 128],
                            tmpc[:, f:f + 1], idn[:], ALU.mult, ALU.mult),
                            [(vd, vv)]))
                    else:
                        gops.append((lambda e, Gb=Gb, Eb=Eb, kidx=kidx, f=f, h=h: e.scalar_tensor_tensor(
                            Gb[:, kidx * 128:(kidx + 1) * 128],
                            Eb[:, kidx * 128:(kidx + 1) * 128],
                            pbf[:, h, f:f + 1], idn[:], ALU.mult, ALU.mult), []))
                for oi_, (fn_, w_) in enumerate(gops):
                    last = oi_ == len(gops) - 1
                    rec.add("vector", fn_, waits=w_, incs=[(vd, 1)] if last else [])
                g_done[(ti, h)] = vd.inc()
                if dbg_tile is not None and i == dbg_tile and h == dbg_head:
                    scrS = ctx.sbuf("scrS", [128, 1024], F32)
                    rec.add("vector", lambda e, pe=pe: e.tensor_copy(scrS[:], pe[:]),
                            waits=[(td, t_mm)], incs=[(vd, 1)])
                    v_ps = vd.inc()
                    rec.add("sync", lambda e: e.dma_start(out=dbgS_h, in_=scrS[:]),
                            waits=[(vd, v_ps)], incs=[(st, 16)])
                    st.inc(16)
                    rec.add("sync", lambda e, qs=qs: e.dma_start(out=dbgq_h, in_=qs[:]),
                            waits=[(td, t_mm)], incs=[(st, 16)])
                    st.inc(16)
                    rec.add("sync", lambda e, ks=ks: e.dma_start(out=dbgk_h, in_=ks[:]),
                            waits=[(td, t_mm)], incs=[(st, 16)])
                    st.inc(16)
                    rec.add("sync", lambda e, Eb=Eb: e.dma_start(out=dbgE_h, in_=Eb[:]),
                            waits=[(vd, g_done[(ti, h)])], incs=[(st, 16)])
                    st.inc(16)
                    rec.add("sync", lambda e, Gb=Gb: e.dma_start(out=dbgG_h, in_=Gb[:]),
                            waits=[(vd, g_done[(ti, h)])], incs=[(st, 16)])
                    st.inc(16)
                    rec.add("sync", lambda e: e.dma_start(out=dbgB_h, in_=bnear[:]),
                            incs=[(st, 16)])
                    st.inc(16)
                    dbg_barrier[0] = st.n
                po = o_ps[hp]
                wp = [(vd, g_done[(ti, h)])]
                if (ti, h - 2) in oev_done:
                    wp.append((ev, oev_done[(ti, h - 2)]))
                elif ti > 0 and (ti - 1, 14 + h) in oev_done:
                    wp.append((ev, oev_done[(ti - 1, 14 + h)]))
                for kidx in range(8):
                    rec.add("tensor", lambda e, po=po, Gb=Gb, kidx=kidx, vs=vs, h=h: e.matmul(
                        po[:, 0:65], Gb[:, kidx * 128:(kidx + 1) * 128],
                        vs[:, kidx, 65 * h:65 * h + 65],
                        start=(kidx == 0), stop=(kidx == 7)),
                        waits=wp if kidx == 0 else [], incs=[(td, 1)] if kidx == 7 else [])
                tile_pv_done[(ti, h)] = td.inc()
                if h == 15:
                    tile_pv_done[ti] = tile_pv_done[(ti, h)]
                rec.add("vector", lambda e, po=po, h=h: e.tensor_copy(
                    o_stage[:, h, :], po[:, 0:65]),
                    waits=[(td, tile_pv_done[(ti, h)])], incs=[(ev, 1)])
                oev_done[(ti, h)] = ev.inc()
            if dbg_tile is not None and i == dbg_tile:
                rec.add("sync", lambda e: e.dma_start(
                    out=dbgO_h, in_=o_stage[:].rearrange("p h e -> p (h e)")),
                    waits=[(ev, oev_done[(ti, 15)])], incs=[(st, 16)])
                st.inc(16)
                dbg_barrier[0] = st.n
            fb = flat_sb[p]
            wn = [(ev, oev_done[(ti, 15)])]
            if ti >= 2:
                wn.append((st, flat_st[ti - 2]))
            rec.add("vector", lambda e: e.tensor_scalar(
                zcol[:, 0:16], o_stage[:, :, 64], ctx.c("tiny"), None, ALU.max),
                waits=wn, incs=[(vd, 1)])
            v_z = vd.inc()
            rec.add("vector", lambda e: e.reciprocal(
                zcol[:, 16:32], zcol[:, 0:16]), waits=[(vd, v_z)], incs=[(vd, 1)])
            v_rz = vd.inc()
            for h in range(16):
                rec.add("vector", lambda e, fb=fb, h=h: e.tensor_scalar(
                    fb[:, h * 64:(h + 1) * 64], o_stage[:, h, 0:64],
                    zcol[:, 16 + h:17 + h], None, ALU.mult),
                    waits=[(vd, v_rz)] if h == 0 else [],
                    incs=[(vd, 1)] if h == 15 else [])
            v_fl = vd.inc()
            rec.add("sync", lambda e, i=i, fb=fb: e.dma_start(
                out=flat_h[i * 128:(i + 1) * 128, :], in_=fb[:]),
                waits=[(vd, v_fl)], incs=[(st, 16)])
            flat_st[ti] = st.inc(16)
        rec.add("sync", lambda e: e.nop(), waits=[(st, st.n)])

def build_slot_attn_phase(dbg_tile=None, dbg_head=0):
    ctx = Ctx("emit_slot_attn")
    nc = ctx.nc
    nc = ctx.nc
    q_h = nc.dram_tensor("q_fm", [D, TC], BF16, kind="ExternalInput").ap()
    k_h = nc.dram_tensor("k_fm", [D, TC], BF16, kind="ExternalInput").ap()
    v_h = nc.dram_tensor("v65", [TC, SLOT_H * 65], BF16, kind="ExternalInput").ap()
    pb_h = nc.dram_tensor("pos_bias", [44, 16], F32, kind="ExternalInput").ap()
    mc_h = nc.dram_tensor("mconst", [44, NPOS], BF16, kind="ExternalInput").ap()
    msk_h = nc.dram_tensor("kvmask", [TC], F32, kind="ExternalInput").ap()
    flat_h = nc.dram_tensor("flat", [TC, D], BF16, kind="ExternalOutput").ap()
    ball_h = nc.dram_tensor("ball_scratch", [16, NPOS], F32, kind="Internal").ap()
    if dbg_tile is not None:
        dbgS_h = nc.dram_tensor("dbgS", [128, 1024], F32, kind="ExternalOutput").ap()
        dbgq_h = nc.dram_tensor("dbgq", [64, 16, 128], BF16, kind="ExternalOutput").ap()
        dbgk_h = nc.dram_tensor("dbgk", [64, 16, 1024], BF16, kind="ExternalOutput").ap()
        dbgE_h = nc.dram_tensor("dbgE", [128, 1024], BF16, kind="ExternalOutput").ap()
        dbgG_h = nc.dram_tensor("dbgG", [128, 1024], BF16, kind="ExternalOutput").ap()
        dbgO_h = nc.dram_tensor("dbgO", [128, SLOT_H * 65], F32, kind="ExternalOutput").ap()
        dbgB_h = nc.dram_tensor("dbgB", [128, 16, 384], BF16, kind="ExternalOutput").ap()

    io = {
        "q_fm": q_h,
        "k_fm": k_h,
        "v65": v_h,
        "pos_bias": pb_h,
        "mconst": mc_h,
        "kvmask": msk_h,
        "flat": flat_h,
        "ball_scratch": ball_h,
    }
    if dbg_tile is not None:
        io.update({"dbgS": dbgS_h, "dbgq": dbgq_h, "dbgk": dbgk_h,
                   "dbgE": dbgE_h, "dbgG": dbgG_h, "dbgO": dbgO_h, "dbgB": dbgB_h})
    with ctx.es:
        emit_slot_attn(ctx, io, dbg_tile=dbg_tile, dbg_head=dbg_head)
        finish(ctx.nc, ctx.rec, ctx.es, ctx)
    return ctx.nc


def emit_gate(ctx, io):
    nc = ctx.nc
    rec = ctx.rec
    xn_fm_h = io["xn_fm"]
    flat_h = io["flat"]
    wg_h = io["Wgate"]
    bg_h = io["bgate"]
    wo_h = io["Wout"]
    bo_h = io["bout"]
    s_h = io["s_tm"]
    if True:
        ctx.setup_consts()
        idn = build_identity_bf16(ctx, io.get("ident"))
        rec = ctx.rec
        ld = ctx.sem("ld"); st = ctx.sem("st"); td = ctx.sem("td")
        vd = ctx.sem("vd"); sd = ctx.sem("sd"); ev = ctx.sem("ev")

        wg = ctx.sbuf("wg", [128, 8, D], BF16)
        wo = ctx.sbuf("wo", [128, 8, D], BF16)
        bgrow = ctx.sbuf("bgrow", [1, D], F32)
        borow = ctx.sbuf("borow", [1, D], F32)
        ones_row = ctx.sbuf("ones_row", [1, 128], F32)
        rec.add("gpsimd", lambda e: e.dma_start(
            out=wg[:], in_=wg_h.rearrange("(c p) e -> p c e", p=128)), incs=[(ld, 16)])
        rec.add("gpsimd", lambda e: e.dma_start(
            out=wo[:], in_=wo_h.rearrange("(c p) e -> p c e", p=128)), incs=[(ld, 16)])
        rec.add("sync", lambda e: e.dma_start(
            out=bgrow[:], in_=bg_h.rearrange("(o e) -> o e", o=1)), incs=[(ld, 16)])
        rec.add("sync", lambda e: e.dma_start(
            out=borow[:], in_=bo_h.rearrange("(o e) -> o e", o=1)), incs=[(ld, 16)])
        rec.add("vector", lambda e: e.memset(ones_row[:], 1.0), incs=[(vd, 1)])
        prep_ld = ld.inc(64)
        prep_vd = vd.inc()

        xn_sb = [ctx.sbuf(f"xn{i}", [128, 8, 128], BF16) for i in range(2)]
        fl_sb = [ctx.sbuf(f"fl{i}", [128, D], BF16) for i in range(2)]
        gt_sb = [ctx.sbuf(f"gt{i}", [128, D], BF16) for i in range(2)]
        fg_sb = [ctx.sbuf(f"fg{i}", [128, 8, 128], BF16) for i in range(2)]
        s_sb = [ctx.sbuf(f"ssb{i}", [128, D], BF16) for i in range(2)]
        gps = [ctx.psum(f"gps{i}", [128, 512]) for i in range(2)]
        tps = ctx.psum("tps", [128, 1024], BF16)
        sps = [ctx.psum(f"sps{i}", [128, 512]) for i in range(2)]

        tiles = list(range(HALO // 128, TC // 128))
        trk = {}
        for ti, i in enumerate(tiles):
            p = ti % 2
            xs, fl, gt, fg, ss = xn_sb[p], fl_sb[p], gt_sb[p], fg_sb[p], s_sb[p]
            w = [(ld, prep_ld)] if ti == 0 else []
            if ti >= 2:
                w = [(td, trk[("smm", ti - 2)])]
            rec.add("sync", lambda e, i=i, xs=xs: e.dma_start(
                out=xs[:], in_=xn_fm_h[:, i * 128:(i + 1) * 128].rearrange(
                    "(c p) t -> p c t", p=128)), waits=w, incs=[(ld, 16)])
            rec.add("sync", lambda e, i=i, fl=fl: e.dma_start(
                out=fl[:], in_=flat_h[i * 128:(i + 1) * 128, :]), incs=[(ld, 16)])
            lv = ld.inc(32)
            for half in range(2):
                pg = gps[half]
                wm = [(ld, lv)]
                if ti >= 1:
                    wm.append((sd, trk[("sig", ti - 1, half)]))
                for c in range(8):
                    rec.add("tensor", lambda e, pg=pg, xs=xs, c=c, half=half: e.matmul(
                        pg[:], xs[:, c, :], wg[:, c, half * 512:(half + 1) * 512],
                        start=(c == 0), stop=False), waits=wm if c == 0 else [])
                rec.add("tensor", lambda e, pg=pg, half=half: e.matmul(
                    pg[:], ones_row[:], bgrow[:, half * 512:(half + 1) * 512],
                    start=False, stop=True), incs=[(td, 1)])
                tv = td.inc()
                rec.add("scalar", lambda e, pg=pg, gt=gt, half=half: e.activation(
                    gt[:, half * 512:(half + 1) * 512], pg[:], AF.Sigmoid),
                    waits=[(td, tv)], incs=[(sd, 1)])
                trk[("sig", ti, half)] = sd.inc()
            wfg = [(sd, trk[("sig", ti, 1)])]
            if ti >= 1:
                wfg.append((td, trk[("tp", ti - 1)]))
            rec.add("vector", lambda e, fl=fl, gt=gt, fg=fg: e.tensor_tensor(
                fg[:].rearrange("p c t -> p (c t)"), fl[:], gt[:], ALU.mult),
                waits=wfg, incs=[(vd, 1)])
            v_fg = vd.inc()
            wtr = [(vd, v_fg)]
            if ti >= 1:
                wtr.append((ev, trk[("fgev", ti - 1)]))
            for c in range(8):
                rec.add("tensor", lambda e, fg=fg, c=c: e.transpose(
                    tps[:, c * 128:(c + 1) * 128], fg[:, c, :], idn[:]),
                    waits=wtr if c == 0 else [], incs=[(td, 1)] if c == 7 else [])
            trk[("tp", ti)] = td.inc()
            rec.add("vector", lambda e, fg=fg: e.tensor_copy(
                fg[:].rearrange("p c t -> p (c t)"), tps[:]),
                waits=[(td, trk[("tp", ti)])], incs=[(ev, 1)])
            trk[("fgev", ti)] = ev.inc()
            for half in range(2):
                pq = sps[half]
                wm = [(ev, trk[("fgev", ti)])]
                if ti >= 1:
                    wm.append((ev, trk[("sev", ti - 1, half)]))
                for c in range(8):
                    rec.add("tensor", lambda e, pq=pq, fg=fg, c=c, half=half: e.matmul(
                        pq[:], fg[:, c, :], wo[:, c, half * 512:(half + 1) * 512],
                        start=(c == 0), stop=False), waits=wm if c == 0 else [])
                rec.add("tensor", lambda e, pq=pq, half=half: e.matmul(
                    pq[:], ones_row[:], borow[:, half * 512:(half + 1) * 512],
                    start=False, stop=True), incs=[(td, 1)])
                tvs = td.inc()
                if half == 1:
                    trk[("smm", ti)] = tvs
                wv = [(td, tvs)]
                if ti >= 2:
                    wv.append((st, trk[("sst", ti - 2)]))
                rec.add("vector", lambda e, pq=pq, ss=ss, half=half: e.tensor_copy(
                    ss[:, half * 512:(half + 1) * 512], pq[:]),
                    waits=wv, incs=[(ev, 1)])
                trk[("sev", ti, half)] = ev.inc()
            rec.add("sync", lambda e, i=i, ss=ss: e.dma_start(
                out=s_h[i * 128:(i + 1) * 128, :], in_=ss[:]),
                waits=[(ev, trk[("sev", ti, 1)])], incs=[(st, 16)])
            trk[("sst", ti)] = st.inc(16)
        rec.add("sync", lambda e: e.nop(), waits=[(st, st.n)])

def build_gate_phase():
    ctx = Ctx("emit_gate")
    nc = ctx.nc
    nc = ctx.nc
    xn_fm_h = nc.dram_tensor("xn_fm", [D, TC], BF16, kind="ExternalInput").ap()
    flat_h = nc.dram_tensor("flat", [TC, D], BF16, kind="ExternalInput").ap()
    wg_h = nc.dram_tensor("Wgate", [D, D], F32, kind="ExternalInput").ap()
    bg_h = nc.dram_tensor("bgate", [D], F32, kind="ExternalInput").ap()
    wo_h = nc.dram_tensor("Wout", [D, D], F32, kind="ExternalInput").ap()
    bo_h = nc.dram_tensor("bout", [D], F32, kind="ExternalInput").ap()
    s_h = nc.dram_tensor("s_tm", [TC, D], BF16, kind="ExternalOutput").ap()

    io = {
        "xn_fm": xn_fm_h,
        "flat": flat_h,
        "Wgate": wg_h,
        "bgate": bg_h,
        "Wout": wo_h,
        "bout": bo_h,
        "s_tm": s_h,
    }
    with ctx.es:
        emit_gate(ctx, io)
        finish(ctx.nc, ctx.rec, ctx.es, ctx)
    return ctx.nc


def make_rotary_tables(t0):
    t = np.arange(t0 - HALO, t0 + T_OWN, dtype=np.float64)
    half = np.arange(0, 64, 2, dtype=np.float64)
    inv_freq = 1.0 / 10000.0 ** (half / 64.0)
    fr = t[:, None] * inv_freq[None, :]
    fr = np.concatenate([fr, fr], -1)
    base = (half + 0.4 * 64) / (1.4 * 64)
    power = (t - N // 2) / float(WIN // 2)
    scale = base[None, :] ** power[:, None]
    scale = np.concatenate([scale, scale], -1)
    sign = np.concatenate([-np.ones(32), np.ones(32)])[None, :]
    cq = (np.cos(fr) * scale * 8.0).astype(np.float32)
    sq = (np.sin(fr) * scale * 8.0 * sign).astype(np.float32)
    ck = (np.cos(fr) / scale).astype(np.float32)
    sk = (np.sin(fr) / scale * sign).astype(np.float32)
    return cq, sq, ck, sk


FQ_T0 = 4
FA_T0 = 8


def emit_ffn_qkv(ctx, io):
    nc = ctx.nc
    rec = ctx.rec
    tab_h = {n: io[n] for n in ("cq", "sq", "ck", "sk")}
    xin_h = io["xin_fm"]
    w_h = io["wqkv"]
    qs_h = io["q_scale"]
    ks_h = io["k_scale"]
    q_out = io["q_fm"]
    k_out = io["k_fm"]
    v_out = io["v65"]
    if True:
        ctx.setup_consts()
        idn = build_identity_bf16(ctx, io.get("ident"))
        rec = ctx.rec
        ld = ctx.sem("ld"); gld = ctx.sem("gld"); st = ctx.sem("st")
        td = ctx.sem("td"); vd = ctx.sem("vd"); sd = ctx.sem("sd"); ev = ctx.sem("ev")

        wq = ctx.sbuf("wq", [128, 8, 1536], BF16)
        scl = ctx.sbuf("scl", [128, 4, 64], F32)
        rec.add("gpsimd", lambda e: e.dma_start(
            out=wq[:], in_=w_h.rearrange("(c p) e -> p c e", p=128)), incs=[(gld, 16)])
        for j, hh in enumerate((qs_h, ks_h)):
            rec.add("sync", lambda e, j=j, hh=hh: e.dma_start(
                out=scl[:, 2 * j, :], in_=hh.unsqueeze(0).broadcast_to([128, 64])),
                incs=[(ld, 16)])
            rec.add("sync", lambda e, j=j, hh=hh: e.dma_start(
                out=scl[:, 2 * j + 1, 0:32],
                in_=hh[32:64].unsqueeze(0).broadcast_to([128, 32])), incs=[(ld, 16)])
            rec.add("sync", lambda e, j=j, hh=hh: e.dma_start(
                out=scl[:, 2 * j + 1, 32:64],
                in_=hh[0:32].unsqueeze(0).broadcast_to([128, 32])), incs=[(ld, 16)])
        prep_ld = ld.inc(16 * 6)
        w_gld = gld.inc(16)

        xs = [ctx.sbuf(f"xs{i}", [128, 8, 128], BF16) for i in range(2)]
        tabs = [ctx.sbuf(f"tab{i}", [128, 4, 64], F32) for i in range(2)]
        fold = ctx.sbuf("fold", [128, 4, 64], F32)
        qkt = [ctx.sbuf(f"qkt{i}", [128, 2, 512], BF16) for i in range(2)]
        v_sb = [ctx.sbuf(f"vsb{i}", [128, FFN_H, 65], BF16) for i in range(2)]
        rns = ctx.sbuf("rns", [128, 2, 8], F32)
        scr = ctx.sbuf("scr", [128, 512], F32)
        tmp1 = ctx.sbuf("tmp1", [128, 512], F32)
        tmp2 = ctx.sbuf("tmp2", [128, 512], F32)
        fmst = [ctx.sbuf(f"fmst{i}", [128, 512], BF16) for i in range(2)]
        ps = [ctx.psum(f"ps{i}", [128, 512]) for i in range(3)]
        tp = ctx.psum("tp", [128, 512], BF16)

        trk = {}
        tiles = list(range(FQ_T0, TC // 128))
        for ti, i in enumerate(tiles):
            p = ti % 2
            x_sb, tb, qk, vs = xs[p], tabs[p], qkt[p], v_sb[p]
            w = [(ld, prep_ld)] if ti == 0 else []
            if ti >= 2:
                w = [(td, trk[("mmv", ti - 2)]), (vd, trk[("fold", ti - 2)])]
            rec.add("sync", lambda e, i=i, x_sb=x_sb: e.dma_start(
                out=x_sb[:, 0:4, :],
                in_=xin_h[0:512, i * 128:(i + 1) * 128].rearrange(
                    "(c p) t -> p c t", p=128)), waits=w, incs=[(ld, 16)])
            rec.add("sync", lambda e, i=i, x_sb=x_sb: e.dma_start(
                out=x_sb[:, 4:8, :],
                in_=xin_h[512:1024, i * 128 - 1:(i + 1) * 128 - 1].rearrange(
                    "(c p) t -> p c t", p=128)), incs=[(ld, 16)])
            for j, nmn in enumerate(("cq", "sq", "ck", "sk")):
                rec.add("sync", lambda e, i=i, j=j, hh=tab_h[nmn], tb=tb: e.dma_start(
                    out=tb[:, j, :], in_=hh[i * 128:(i + 1) * 128, :]), incs=[(ld, 16)])
            lv = ld.inc(16 * 6)
            wf = [(ld, lv)]
            if ti >= 1:
                wf.append((vd, trk[("rotdone", ti - 1)]))
            for j in range(4):
                rec.add("vector", lambda e, j=j, tb=tb: e.tensor_tensor(
                    fold[:, j, :], tb[:, j, :], scl[:, j, :], ALU.mult),
                    waits=wf if j == 0 else [], incs=[(vd, 1)] if j == 3 else [])
            trk[("fold", ti)] = vd.inc()
            for part in range(3):
                pp = ps[part]
                wm = [(ld, lv), (gld, w_gld)]
                if ti >= 1:
                    wm.append(trk[("pfree", ti - 1, part)])
                for c in range(8):
                    rec.add("tensor", lambda e, pp=pp, x_sb=x_sb, c=c, part=part: e.matmul(
                        pp[:], x_sb[:, c, :], wq[:, c, part * 512:(part + 1) * 512],
                        start=(c == 0), stop=(c == 7)),
                        waits=wm if c == 0 else [], incs=[(td, 1)] if c == 7 else [])
                tv = td.inc()
                if part == 2:
                    trk[("mmv", ti)] = tv
                    if ti < 2:
                        rec.add("vector", lambda e, vs=vs: e.memset(vs[:, :, 64:65], 1.0))
                    wv = [(td, tv)]
                    if ti >= 2:
                        wv.append((st, trk[("vst", ti - 2)]))
                    rec.add("vector", lambda e, pp=pp, vs=vs: e.tensor_copy(
                        vs[:, :, 0:64], pp[:].rearrange("p (h d) -> p h d", d=64)),
                        waits=wv, incs=[(ev, 1)])
                    evv = ev.inc()
                    trk[("pfree", ti, part)] = (ev, evv)
                    rec.add("sync", lambda e, i=i, vs=vs: e.dma_start(
                        out=v_out[i * 128:(i + 1) * 128, :],
                        in_=vs[:].rearrange("p h e -> p (h e)")),
                        waits=[(ev, evv)], incs=[(st, 16)])
                    trk[("vst", ti)] = st.inc(16)
                    continue
                wv = [(td, tv)]
                rec.add("scalar", lambda e, pp=pp: e.activation(
                    scr[:], pp[:], AF.Square), waits=wv, incs=[(sd, 1)])
                s_sq = sd.inc()
                rec.add("vector", lambda e, part=part: e.reduce_sum(
                    rns[:, part, :], scr[:].rearrange("p (h d) -> p h d", d=64),
                    AX.X), waits=[(sd, s_sq)])
                rec.add("vector", lambda e, part=part: e.tensor_scalar(
                    rns[:, part, :], rns[:, part, :], ctx.c("clip2"), None,
                    ALU.max), incs=[(vd, 1)])
                v_ss = vd.inc()
                rec.add("scalar", lambda e, part=part: e.activation(
                    rns[:, part, :], rns[:, part, :], AF.Sqrt),
                    waits=[(vd, v_ss)], incs=[(sd, 1)])
                s_rt = sd.inc()
                rec.add("vector", lambda e, part=part: e.reciprocal(
                    rns[:, part, :], rns[:, part, :]), waits=[(sd, s_rt)], incs=[(vd, 1)])
                v_rn = vd.inc()
                rec.add("vector", lambda e, pp=pp, part=part: e.tensor_tensor(
                    tmp1[:].rearrange("p (h d) -> p h d", d=64),
                    pp[:].rearrange("p (h d) -> p h d", d=64),
                    rns[:, part, :].unsqueeze(2).broadcast_to([128, 8, 64]),
                    ALU.mult), waits=[(vd, v_rn)], incs=[(vd, 1)])
                trk[("pfree", ti, part)] = (vd, vd.inc())
                so = 2 * part + 1
                for hside in range(2):
                    dst = (slice(0, 32), slice(32, 64))[hside]
                    srcs = (slice(32, 64), slice(0, 32))[hside]
                    rec.add("vector", lambda e, dst=dst, srcs=srcs, so=so: e.tensor_tensor(
                        tmp2[:].rearrange("p (h d) -> p h d", d=64)[:, :, dst],
                        tmp1[:].rearrange("p (h d) -> p h d", d=64)[:, :, srcs],
                        fold[:, so, dst].unsqueeze(1).broadcast_to([128, 8, 32]),
                        ALU.mult))
                rec.add("vector", lambda e, so=so: e.tensor_tensor(
                    scr[:].rearrange("p (h d) -> p h d", d=64),
                    tmp1[:].rearrange("p (h d) -> p h d", d=64),
                    fold[:, so - 1, :].unsqueeze(1).broadcast_to([128, 8, 64]),
                    ALU.mult))
                wv2 = []
                if ti >= 2:
                    wv2.append((td, trk[("tpd", ti - 2, part)]))
                rec.add("vector", lambda e, qk=qk, part=part: e.tensor_tensor(
                    qk[:, part, :], scr[:], tmp2[:], ALU.add),
                    waits=wv2, incs=[(vd, 1)])
                v_rot = vd.inc()
                if part == 1:
                    trk[("rotdone", ti)] = v_rot
                wt = [(vd, v_rot)]
                if (ti, part) != (0, 0):
                    prevkey = (ti, part - 1) if part == 1 else (ti - 1, 1)
                    wt.append((ev, trk[("fmev", prevkey)]))
                for c in range(4):
                    rec.add("tensor", lambda e, qk=qk, part=part, c=c: e.transpose(
                        tp[:, c * 128:(c + 1) * 128], qk[:, part, c * 128:(c + 1) * 128],
                        idn[:]), waits=wt if c == 0 else [],
                        incs=[(td, 1)] if c == 3 else [])
                trk[("tpd", ti, part)] = td.inc()
                fs = fmst[part % 2]
                wv3 = [(td, trk[("tpd", ti, part)])]
                if ti >= 1:
                    wv3.append((st, trk[("fmst", ti - 1, part)]))
                rec.add("vector", lambda e, fs=fs: e.tensor_copy(fs[:], tp[:]),
                        waits=wv3, incs=[(ev, 1)])
                trk[("fmev", (ti, part))] = ev.inc()
                dst_h = (q_out, k_out)[part]
                rec.add("sync", lambda e, dst_h=dst_h, i=i, fs=fs: e.dma_start(
                    out=dst_h[:, i * 128:(i + 1) * 128].rearrange(
                        "(c f) t -> f c t", f=128),
                    in_=fs[:].rearrange("f (c t) -> f c t", t=128)),
                    waits=[(ev, trk[("fmev", (ti, part))])], incs=[(st, 16)])
                trk[("fmst", ti, part)] = st.inc(16)
        rec.add("sync", lambda e: e.nop(), waits=[(st, st.n)])

def build_ffn_qkv_phase():
    ctx = Ctx("emit_ffn_qkv")
    nc = ctx.nc
    nc = ctx.nc
    xin_h = nc.dram_tensor("xin_fm", [D, TC], BF16, kind="ExternalInput").ap()
    w_h = nc.dram_tensor("wqkv", [D, 1536], F32, kind="ExternalInput").ap()
    qs_h = nc.dram_tensor("q_scale", [64], F32, kind="ExternalInput").ap()
    ks_h = nc.dram_tensor("k_scale", [64], F32, kind="ExternalInput").ap()
    tab_h = {n: nc.dram_tensor(n, [TC, 64], F32, kind="ExternalInput").ap()
             for n in ("cq", "sq", "ck", "sk")}
    q_out = nc.dram_tensor("q_fm", [512, TC], BF16, kind="ExternalOutput").ap()
    k_out = nc.dram_tensor("k_fm", [512, TC], BF16, kind="ExternalOutput").ap()
    v_out = nc.dram_tensor("v65", [TC, FFN_H * 65], BF16, kind="ExternalOutput").ap()

    io = {
        "xin_fm": xin_h,
        "wqkv": w_h,
        "q_scale": qs_h,
        "k_scale": ks_h,
        "q_fm": q_out,
        "k_fm": k_out,
        "v65": v_out,
        **tab_h,
    }
    with ctx.es:
        emit_ffn_qkv(ctx, io)
        finish(ctx.nc, ctx.rec, ctx.es, ctx)
    return ctx.nc


def make_w01():
    j = np.arange(128)[:, None]
    c = np.arange(128)[None, :]
    a = (c <= j).astype(np.float32)
    b = (c >= j).astype(np.float32)
    return a.astype(BF), b.astype(BF)


def emit_ffn_attn(ctx, io, dbg_tile=None, dbg_head=0):
    nc = ctx.nc
    rec = ctx.rec
    q_h = io["q_fm"]
    k_h = io["k_fm"]
    v_h = io["v65"]
    w01a_h = io["w01a"]
    w01b_h = io["w01b"]
    msk_h = io["kvmask"]
    o_h = io["o_tm"]
    dbgE_h = io.get("dbgE")
    dbgG_h = io.get("dbgG")
    dbgO_h = io.get("dbgO")
    if True:
        ctx.setup_consts()
        rec = ctx.rec
        ld = ctx.sem("ld"); gld = ctx.sem("gld"); st = ctx.sem("st")
        td = ctx.sem("td"); vd = ctx.sem("vd"); sd = ctx.sem("sd"); ev = ctx.sem("ev")

        w01a = ctx.sbuf("w01a_sb", [128, 128], BF16)
        w01b = ctx.sbuf("w01b_sb", [128, 128], BF16)
        mask_sb = ctx.sbuf("masks", [128, TC // 128], F32)
        rec.add("sync", lambda e: e.dma_start(out=w01a[:], in_=w01a_h), incs=[(ld, 16)])
        rec.add("sync", lambda e: e.dma_start(out=w01b[:], in_=w01b_h), incs=[(ld, 16)])
        rec.add("sync", lambda e: e.dma_start(
            out=mask_sb[:], in_=msk_h.rearrange("(i p) -> p i", p=128),
            allow_slow_non_contiguous=True), incs=[(ld, 16)])
        prep_ld = ld.inc(48)

        q_sb = [ctx.sbuf(f"qsb{i}", [64, 8, 128], BF16) for i in range(2)]
        k_sb = [ctx.sbuf(f"ksb{i}", [64, 8, 640], BF16) for i in range(2)]
        v_sb = [ctx.sbuf(f"vsb{i}", [128, 5, FFN_H * 65], BF16) for i in range(2)]
        E_sb = [ctx.sbuf(f"esb{i}", [128, 640], BF16) for i in range(2)]
        G_sb = [ctx.sbuf(f"gsb{i}", [128, 640], BF16) for i in range(2)]
        o_stage = ctx.sbuf("ostg", [128, FFN_H, 65], F32)
        zcol = ctx.sbuf("zcol", [128, 2 * FFN_H], F32)
        o_tm = [ctx.sbuf(f"otm{i}", [128, 512], BF16) for i in range(2)]
        ps_e = [ctx.psum(f"pse{i}", [128, 1024]) for i in range(2)]
        o_ps = [ctx.psum(f"ops{i}", [128, 512]) for i in range(2)]

        tiles = list(range(FA_T0, TC // 128))
        dbgbar = [0]
        exp_done = {}; g_done = {}; pv_done = {}; oev_done = {}; o_st = {}
        for ti, i in enumerate(tiles):
            p = ti % 2
            qs, ks, vs = q_sb[p], k_sb[p], v_sb[p]
            blocks = [i - 4 + j for j in range(5)]
            w = [(ld, prep_ld)] if ti == 0 else []
            if ti >= 2:
                w = [(td, pv_done[ti - 2])]
            rec.add("sync", lambda e, i=i, qs=qs: e.dma_start(
                out=qs[:], in_=q_h[:, i * 128:(i + 1) * 128].rearrange(
                    "(h p) t -> p h t", p=64)), waits=w, incs=[(ld, 16)])
            rec.add("sync", lambda e, i=i, ks=ks: e.dma_start(
                out=ks[:], in_=k_h[:, (i - 4) * 128:(i + 1) * 128].rearrange(
                    "(h p) t -> p h t", p=64)), incs=[(ld, 16)])
            lv = ld.inc(32)
            wg2 = list(w)
            rec.add("gpsimd", lambda e, i=i, vs=vs: e.dma_start(
                out=vs[:], in_=v_h[(i - 4) * 128:(i + 1) * 128, :].rearrange(
                    "(b p) e -> p b e", p=128)), waits=wg2, incs=[(gld, 16)])
            lvg = gld.inc(16)
            for h in range(FFN_H):
                hp = h % 2
                pe = ps_e[hp]
                wm = [(ld, lv), (gld, lvg)]
                key = (ti, h - 2) if h >= 2 else (ti - 1, FFN_H - 2 + h)
                if key in exp_done:
                    wm.append((sd, exp_done[key]))
                for j in range(5):
                    rec.add("tensor", lambda e, pe=pe, j=j, ks=ks, qs=qs, h=h: e.matmul(
                        pe[:, j * 128:(j + 1) * 128], ks[:, h, j * 128:(j + 1) * 128],
                        qs[:, h, :], start=True, stop=True),
                        waits=wm if j == 0 else [], incs=[(td, 1)] if j == 4 else [])
                t_mm = td.inc()
                Eb = E_sb[hp]
                we = [(td, t_mm)]
                if key in g_done:
                    we.append((vd, g_done[key]))
                if dbgbar[0]:
                    we.append((st, dbgbar[0]))
                rec.add("scalar", lambda e, pe=pe, Eb=Eb: e.activation(
                    Eb[:], pe[:, 0:640], AF.Exp), waits=we, incs=[(sd, 1)])
                exp_done[(ti, h)] = sd.inc()
                Gb = G_sb[hp]
                wg = [(sd, exp_done[(ti, h)])]
                if key in pv_done:
                    wg.append((td, pv_done[key]))
                use_g = [False] * 5
                gops = []
                gops.append((lambda e, Eb=Eb, Gb=Gb: e.tensor_tensor(
                    Gb[:, 0:128], Eb[:, 0:128], w01a[:], ALU.mult), wg))
                use_g[0] = True
                gops.append((lambda e, Eb=Eb, Gb=Gb: e.tensor_tensor(
                    Gb[:, 512:640], Eb[:, 512:640], w01b[:], ALU.mult), []))
                use_g[4] = True
                for j, bu in enumerate(blocks):
                    if bu < HALO // 128:
                        if j in (0, 4):
                            gops.append((lambda e, Gb=Gb, j=j, bu=bu: e.tensor_scalar(
                                Gb[:, j * 128:(j + 1) * 128],
                                Gb[:, j * 128:(j + 1) * 128],
                                mask_sb[:, bu:bu + 1], None, ALU.mult), []))
                        else:
                            gops.append((lambda e, Gb=Gb, Eb=Eb, j=j, bu=bu: e.tensor_scalar(
                                Gb[:, j * 128:(j + 1) * 128],
                                Eb[:, j * 128:(j + 1) * 128],
                                mask_sb[:, bu:bu + 1], None, ALU.mult), []))
                            use_g[j] = True
                for oi_, (fn_, w_) in enumerate(gops):
                    rec.add("vector", fn_, waits=w_,
                            incs=[(vd, 1)] if oi_ == len(gops) - 1 else [])
                g_done[(ti, h)] = vd.inc()
                po = o_ps[hp]
                wp = [(vd, g_done[(ti, h)])]
                if key in oev_done:
                    wp.append((ev, oev_done[key]))
                for j in range(5):
                    src = Gb if use_g[j] else Eb
                    rec.add("tensor", lambda e, po=po, src=src, j=j, vs=vs, h=h: e.matmul(
                        po[:, 0:65], src[:, j * 128:(j + 1) * 128],
                        vs[:, j, 65 * h:65 * h + 65],
                        start=(j == 0), stop=(j == 4)),
                        waits=wp if j == 0 else [], incs=[(td, 1)] if j == 4 else [])
                pv_done[(ti, h)] = td.inc()
                if h == FFN_H - 1:
                    pv_done[ti] = pv_done[(ti, h)]
                if dbg_tile is not None and i == dbg_tile and h == dbg_head:
                    rec.add("sync", lambda e, Eb=Eb: e.dma_start(out=dbgE_h, in_=Eb[:]),
                            waits=[(vd, g_done[(ti, h)])], incs=[(st, 16)])
                    st.inc(16)
                    rec.add("sync", lambda e, Gb=Gb: e.dma_start(out=dbgG_h, in_=Gb[:]),
                            waits=[(vd, g_done[(ti, h)])], incs=[(st, 16)])
                    dbgbar[0] = st.inc(16)
                rec.add("vector", lambda e, po=po, h=h: e.tensor_copy(
                    o_stage[:, h, :], po[:, 0:65]),
                    waits=[(td, pv_done[(ti, h)])], incs=[(ev, 1)])
                oev_done[(ti, h)] = ev.inc()
            if dbg_tile is not None and i == dbg_tile:
                rec.add("sync", lambda e: e.dma_start(
                    out=dbgO_h, in_=o_stage[:].rearrange("p h e -> p (h e)")),
                    waits=[(ev, oev_done[(ti, FFN_H - 1)])], incs=[(st, 16)])
                dbgbar[0] = st.inc(16)
            ob = o_tm[p]
            wn = [(ev, oev_done[(ti, FFN_H - 1)])]
            if ti >= 2:
                wn.append((st, o_st[ti - 2]))
            rec.add("vector", lambda e: e.tensor_scalar(
                zcol[:, 0:8], o_stage[:, :, 64], ctx.c("tiny"), None, ALU.max),
                waits=wn, incs=[(vd, 1)])
            v_z = vd.inc()
            rec.add("vector", lambda e: e.reciprocal(
                zcol[:, 8:16], zcol[:, 0:8]), waits=[(vd, v_z)], incs=[(vd, 1)])
            v_rz = vd.inc()
            for h in range(FFN_H):
                rec.add("vector", lambda e, ob=ob, h=h: e.tensor_scalar(
                    ob[:, h * 64:(h + 1) * 64], o_stage[:, h, 0:64],
                    zcol[:, 8 + h:9 + h], None, ALU.mult),
                    waits=[(vd, v_rz)] if h == 0 else [],
                    incs=[(vd, 1)] if h == FFN_H - 1 else [])
            v_fl = vd.inc()
            rec.add("sync", lambda e, i=i, ob=ob: e.dma_start(
                out=o_h[i * 128:(i + 1) * 128, :], in_=ob[:]),
                waits=[(vd, v_fl)], incs=[(st, 16)])
            o_st[ti] = st.inc(16)
        rec.add("sync", lambda e: e.nop(), waits=[(st, st.n)])

def build_ffn_attn_phase(dbg_tile=None, dbg_head=0):
    ctx = Ctx("emit_ffn_attn")
    nc = ctx.nc
    nc = ctx.nc
    q_h = nc.dram_tensor("q_fm", [512, TC], BF16, kind="ExternalInput").ap()
    k_h = nc.dram_tensor("k_fm", [512, TC], BF16, kind="ExternalInput").ap()
    v_h = nc.dram_tensor("v65", [TC, FFN_H * 65], BF16, kind="ExternalInput").ap()
    w01a_h = nc.dram_tensor("w01a", [128, 128], BF16, kind="ExternalInput").ap()
    w01b_h = nc.dram_tensor("w01b", [128, 128], BF16, kind="ExternalInput").ap()
    msk_h = nc.dram_tensor("kvmask", [TC], F32, kind="ExternalInput").ap()
    o_h = nc.dram_tensor("o_tm", [TC, 512], BF16, kind="ExternalOutput").ap()
    if dbg_tile is not None:
        dbgE_h = nc.dram_tensor("dbgE", [128, 640], BF16, kind="ExternalOutput").ap()
        dbgG_h = nc.dram_tensor("dbgG", [128, 640], BF16, kind="ExternalOutput").ap()
        dbgO_h = nc.dram_tensor("dbgO", [128, FFN_H * 65], F32, kind="ExternalOutput").ap()

    io = {
        "q_fm": q_h,
        "k_fm": k_h,
        "v65": v_h,
        "w01a": w01a_h,
        "w01b": w01b_h,
        "kvmask": msk_h,
        "o_tm": o_h,
    }
    if dbg_tile is not None:
        io.update({"dbgE": dbgE_h, "dbgG": dbgG_h, "dbgO": dbgO_h})
    with ctx.es:
        emit_ffn_attn(ctx, io, dbg_tile=dbg_tile, dbg_head=dbg_head)
        finish(ctx.nc, ctx.rec, ctx.es, ctx)
    return ctx.nc


def emit_ffn_wo(ctx, io):
    nc = ctx.nc
    rec = ctx.rec
    o_h = io["o_tm"]
    wo_h = io["wo"]
    xr_h = io["xres_tm"]
    msk_h = io["kvmask"]
    x1_h = io["x1_tm"]
    if True:
        ctx.setup_consts()
        idn = build_identity_bf16(ctx, io.get("ident"))
        rec = ctx.rec
        ld = ctx.sem("ld"); gld = ctx.sem("gld"); st = ctx.sem("st")
        td = ctx.sem("td"); vd = ctx.sem("vd"); ev = ctx.sem("ev")

        wo = ctx.sbuf("wo_sb", [128, 4, D], BF16)
        mask_sb = ctx.sbuf("masks", [128, TC // 128], F32)
        rec.add("gpsimd", lambda e: e.dma_start(
            out=wo[:], in_=wo_h.rearrange("(c p) e -> p c e", p=128)), incs=[(gld, 16)])
        w_gld = gld.inc(16)
        rec.add("sync", lambda e: e.dma_start(
            out=mask_sb[:], in_=msk_h.rearrange("(i p) -> p i", p=128),
            allow_slow_non_contiguous=True), incs=[(ld, 16)])
        prep_ld = ld.inc(16)

        o_sb = [ctx.sbuf(f"osb{i}", [128, 512], BF16) for i in range(2)]
        xr_sb = [ctx.sbuf(f"xrsb{i}", [128, D], BF16) for i in range(2)]
        ofm = [ctx.sbuf(f"ofm{i}", [128, 4, 128], BF16) for i in range(2)]
        x1s = [ctx.sbuf(f"x1s{i}", [128, D], BF16) for i in range(2)]
        tp = ctx.psum("tp", [128, 512], BF16)
        ps = [ctx.psum(f"ps{i}", [128, 512]) for i in range(2)]

        trk = {}
        tiles = list(range(FA_T0, TC // 128))
        for ti, i in enumerate(tiles):
            p = ti % 2
            ob, xr, of, x1 = o_sb[p], xr_sb[p], ofm[p], x1s[p]
            w = [(ld, prep_ld)] if ti == 0 else []
            if ti >= 2:
                w = [(td, trk[("tp", ti - 2)])]
            rec.add("sync", lambda e, i=i, ob=ob: e.dma_start(
                out=ob[:], in_=o_h[i * 128:(i + 1) * 128, :]), waits=w, incs=[(ld, 16)])
            rec.add("sync", lambda e, i=i, xr=xr: e.dma_start(
                out=xr[:], in_=xr_h[i * 128:(i + 1) * 128, :]), incs=[(ld, 16)])
            lv = ld.inc(32)
            wt = [(ld, lv)]
            if ti >= 1:
                wt.append((ev, trk[("ofev", ti - 1)]))
            for c in range(4):
                rec.add("tensor", lambda e, ob=ob, c=c: e.transpose(
                    tp[:, c * 128:(c + 1) * 128], ob[:, c * 128:(c + 1) * 128], idn[:]),
                    waits=wt if c == 0 else [], incs=[(td, 1)] if c == 3 else [])
            trk[("tp", ti)] = td.inc()
            rec.add("vector", lambda e, of=of: e.tensor_copy(
                of[:].rearrange("p c t -> p (c t)"), tp[:]),
                waits=[(td, trk[("tp", ti)])], incs=[(ev, 1)])
            trk[("ofev", ti)] = ev.inc()
            for half in range(2):
                pp = ps[half]
                wm = [(ev, trk[("ofev", ti)]), (gld, w_gld)]
                if ti >= 1:
                    wm.append((ev, trk[("x1ev", ti - 1, half)]))
                for c in range(4):
                    rec.add("tensor", lambda e, pp=pp, of=of, c=c, half=half: e.matmul(
                        pp[:], of[:, c, :], wo[:, c, half * 512:(half + 1) * 512],
                        start=(c == 0), stop=(c == 3)),
                        waits=wm if c == 0 else [], incs=[(td, 1)] if c == 3 else [])
                tv = td.inc()
                wv = [(td, tv)]
                if ti >= 2:
                    wv.append((st, trk[("x1st", ti - 2)]))
                rec.add("vector", lambda e, pp=pp, x1=x1, xr=xr, half=half: e.scalar_tensor_tensor(
                    x1[:, half * 512:(half + 1) * 512], pp[:], ctx.c("one"),
                    xr[:, half * 512:(half + 1) * 512], ALU.mult, ALU.add),
                    waits=wv, incs=[(ev, 1)] if i >= HALO // 128 else [])
                if i < HALO // 128:
                    rec.add("vector", lambda e, x1=x1, half=half, i=i: e.tensor_scalar(
                        x1[:, half * 512:(half + 1) * 512],
                        x1[:, half * 512:(half + 1) * 512],
                        mask_sb[:, i:i + 1], None, ALU.mult), incs=[(ev, 1)])
                trk[("x1ev", ti, half)] = ev.inc()
            rec.add("sync", lambda e, i=i, x1=x1: e.dma_start(
                out=x1_h[i * 128:(i + 1) * 128, :], in_=x1[:]),
                waits=[(ev, trk[("x1ev", ti, 1)])], incs=[(st, 16)])
            trk[("x1st", ti)] = st.inc(16)
        rec.add("sync", lambda e: e.nop(), waits=[(st, st.n)])

def build_ffn_wo_phase():
    ctx = Ctx("emit_ffn_wo")
    nc = ctx.nc
    nc = ctx.nc
    o_h = nc.dram_tensor("o_tm", [TC, 512], BF16, kind="ExternalInput").ap()
    wo_h = nc.dram_tensor("wo", [512, D], F32, kind="ExternalInput").ap()
    xr_h = nc.dram_tensor("xres_tm", [TC, D], BF16, kind="ExternalInput").ap()
    msk_h = nc.dram_tensor("kvmask", [TC], F32, kind="ExternalInput").ap()
    x1_h = nc.dram_tensor("x1_tm", [TC, D], BF16, kind="ExternalOutput").ap()

    io = {
        "o_tm": o_h,
        "wo": wo_h,
        "xres_tm": xr_h,
        "kvmask": msk_h,
        "x1_tm": x1_h,
    }
    with ctx.es:
        emit_ffn_wo(ctx, io)
        finish(ctx.nc, ctx.rec, ctx.es, ctx)
    return ctx.nc


def emit_ffn_mid(ctx, io, fm_out, dbg_group=None):
    nc = ctx.nc
    rec = ctx.rec
    NCH = 2 * HIDP // 128
    NMID = HIDP // 128
    x1_h = io["x1_tm"]
    lg_h = io["ln_g"]
    lb_h = io["ln_b"]
    w1_h = io["w1p"]
    b1_h = io["b1p"]
    w2_h = io["w2p"]
    b2_h = io["b2"]
    msk_h = io["kvmask"]
    x2_h = io["x2_tm"]
    x2f_h = io.get("x2_fm")
    dbgh_h = io.get("dbgH")
    dbgm_h = io.get("dbgM")
    if True:
        ctx.setup_consts()
        idn = build_identity_bf16(ctx, io.get("ident"))
        rec = ctx.rec
        ld = ctx.sem("ld"); gld = ctx.sem("gld"); st = ctx.sem("st")
        td = ctx.sem("td"); vd = ctx.sem("vd"); sd = ctx.sem("sd"); ev = ctx.sem("ev")

        w1 = ctx.sbuf("w1_sb", [128, 8, 2 * HIDP], BF16)
        w2 = ctx.sbuf("w2_sb", [128, NMID, D], BF16)
        cols = ctx.sbuf("cols_sb", [128, 8 + 8 + NCH], F32)
        b2row = ctx.sbuf("b2row", [1, D], F32)
        ones_row = ctx.sbuf("ones_row2", [1, 128], F32)
        mask_sb = ctx.sbuf("masks", [128, TC // 128], F32)
        rec.add("gpsimd", lambda e: e.dma_start(
            out=w1[:], in_=w1_h.rearrange("(c p) e -> p c e", p=128)), incs=[(gld, 16)])
        rec.add("gpsimd", lambda e: e.dma_start(
            out=w2[:], in_=w2_h.rearrange("(c p) e -> p c e", p=128)), incs=[(gld, 16)])
        w_gld = gld.inc(32)
        rec.add("sync", lambda e: e.dma_start(
            out=cols[:, 0:8], in_=lg_h.rearrange("(c p) -> p c", p=128),
            allow_slow_non_contiguous=True), incs=[(ld, 16)])
        rec.add("sync", lambda e: e.dma_start(
            out=cols[:, 8:16], in_=lb_h.rearrange("(c p) -> p c", p=128),
            allow_slow_non_contiguous=True), incs=[(ld, 16)])
        rec.add("sync", lambda e: e.dma_start(
            out=cols[:, 16:16 + NCH], in_=b1_h.rearrange("(c p) -> p c", p=128),
            allow_slow_non_contiguous=True), incs=[(ld, 16)])
        rec.add("sync", lambda e: e.dma_start(
            out=b2row[:], in_=b2_h.rearrange("(o e) -> o e", o=1)), incs=[(ld, 16)])
        rec.add("sync", lambda e: e.dma_start(
            out=mask_sb[:], in_=msk_h.rearrange("(i p) -> p i", p=128),
            allow_slow_non_contiguous=True), incs=[(ld, 16)])
        rec.add("vector", lambda e: e.memset(ones_row[:], 1.0), incs=[(vd, 1)])
        prep_ld = ld.inc(16 * 5)
        prep_vd = vd.inc()

        x1r = [ctx.sbuf(f"x1r{i}", [128, D], BF16) for i in range(4)]
        h2b = [ctx.sbuf(f"h2b{i}", [128, 512], BF16) for i in range(2)]
        h2n = ctx.sbuf("h2n", [128, D], BF16)
        h2nf = ctx.sbuf("h2nf", [128, 8, 512], BF16)
        mid = ctx.sbuf("mid_sb", [128, NMID, 512], BF16)
        Gg = ctx.sbuf("Gg", [128, 512], BF16)
        stats = ctx.sbuf("stats2", [128, 12], F32)
        scr = ctx.sbuf("scr2", [128, D], F32)
        x2s = [ctx.sbuf(f"x2s{i}", [128, D], BF16) for i in range(2)]
        fmst = ctx.sbuf("fmst2", [128, D], BF16)
        tp = ctx.psum("tp2", [128, 512], BF16)
        hp1 = [ctx.psum(f"hp{i}", [128, 512]) for i in range(2)]
        xp = [ctx.psum(f"xp{i}", [128, 512]) for i in range(2)]

        trk = {}
        ntile = TC // 128
        groups = []
        t = FA_T0 * 128
        while t < TC:
            spn = min(512, TC - t)
            groups.append((t, spn))
            t += spn
        for gi, (g0, spn) in enumerate(groups):
            nt = spn // 128
            for tt in range(nt):
                i = g0 // 128 + tt
                p = i % 2
                xr, hb = x1r[i % 4], h2b[p]
                w = [(ld, prep_ld), (vd, prep_vd)] if gi == 0 and tt == 0 else []
                if ("x2ev", i - 4, 1) in trk:
                    w = [(ev, trk[("x2ev", i - 4, 1)])]
                elif (i - 2) * 128 >= FA_T0 * 128:
                    w = [(vd, trk[("h2n", i - 2)])]
                if ("sh2n", i - 2) in trk:
                    w.append((sd, trk[("sh2n", i - 2)]))
                rec.add("sync", lambda e, i=i, xr=xr: e.dma_start(
                    out=xr[:], in_=x1_h[i * 128:(i + 1) * 128, :]),
                    waits=w, incs=[(ld, 16)])
                rec.add("sync", lambda e, i=i, hb=hb: e.dma_start(
                    out=hb[:], in_=x1_h[i * 128 - 1:(i + 1) * 128 - 1, 512:1024]),
                    incs=[(ld, 16)])
                lv = ld.inc(32)
                rec.add("vector", lambda e, xr=xr: e.reduce_sum(
                    stats[:, 0:1], xr[:, 0:512], AX.X), waits=[(ld, lv)])
                rec.add("vector", lambda e, hb=hb: e.reduce_sum(
                    stats[:, 1:2], hb[:], AX.X))
                rec.add("scalar", lambda e, xr=xr: e.activation(
                    scr[:, 0:512], xr[:, 0:512], AF.Square), waits=[(ld, lv)])
                rec.add("scalar", lambda e, hb=hb: e.activation(
                    scr[:, 512:1024], hb[:], AF.Square), incs=[(sd, 1)])
                s_sq = sd.inc()
                rec.add("vector", lambda e: e.reduce_sum(
                    stats[:, 2:3], scr[:], AX.X), waits=[(sd, s_sq)])
                rec.add("vector", lambda e: e.tensor_tensor(
                    stats[:, 3:4], stats[:, 0:1], stats[:, 1:2], ALU.add), incs=[(vd, 1)])
                v_s = vd.inc()
                rec.add("vector", lambda e: e.tensor_scalar(
                    stats[:, 4:5], stats[:, 3:4], ctx.c("inv1024"), None, ALU.mult))
                rec.add("vector", lambda e: e.tensor_scalar(
                    stats[:, 5:6], stats[:, 2:3], ctx.c("inv1024"), None, ALU.mult),
                    incs=[(vd, 1)])
                v_m = vd.inc()
                rec.add("vector", lambda e: e.scalar_tensor_tensor(
                    stats[:, 6:7], stats[:, 4:5], stats[:, 4:5], stats[:, 5:6],
                    ALU.mult, ALU.subtract), waits=[(vd, v_m)], incs=[(vd, 1)])
                v_t = vd.inc()
                rec.add("scalar", lambda e: e.activation(
                    stats[:, 7:8], stats[:, 6:7], AF.Sqrt, bias=ctx.c("eps"),
                    scale=ctx.c("neg1")), waits=[(vd, v_t)], incs=[(sd, 1)])
                s_std = sd.inc()
                rec.add("vector", lambda e: e.reciprocal(
                    stats[:, 8:9], stats[:, 7:8]), waits=[(sd, s_std)], incs=[(vd, 1)])
                v_rs = vd.inc()
                rec.add("vector", lambda e: e.tensor_scalar(
                    stats[:, 9:10], stats[:, 4:5], stats[:, 8:9], ctx.c("neg1"),
                    ALU.mult, ALU.mult), waits=[(vd, v_rs)], incs=[(vd, 1)])
                v_mb = vd.inc()
                wa = [(vd, v_mb)]
                if (i - 1) * 128 >= FA_T0 * 128:
                    wa.append((td, trk[("h2tp", i - 1)]))
                rec.add("scalar", lambda e, xr=xr: e.activation(
                    h2n[:, 0:512], xr[:, 0:512], AF.Identity, bias=stats[:, 9:10],
                    scale=stats[:, 8:9]), waits=wa)
                rec.add("scalar", lambda e, hb=hb: e.activation(
                    h2n[:, 512:1024], hb[:], AF.Identity, bias=stats[:, 9:10],
                    scale=stats[:, 8:9]), incs=[(sd, 1)])
                s_h2n = sd.inc()
                trk[("sh2n", i)] = s_h2n
                for c in range(8):
                    wt = [(sd, s_h2n)] if c % 4 == 0 else []
                    if c % 4 == 0 and (gi > 0 or tt > 0 or c > 0):
                        wt.append((ev, ctx.sems["ev"].n))
                    rec.add("tensor", lambda e, c=c: e.transpose(
                        tp[:, (c % 4) * 128:((c % 4) + 1) * 128],
                        h2n[:, c * 128:(c + 1) * 128], idn[:]),
                        waits=wt, incs=[(td, 1)] if c % 4 == 3 else [])
                    if c % 4 == 3:
                        tv = td.inc()
                        for cc in range(c - 3, c + 1):
                            lastcc = cc == c
                            rec.add("vector", lambda e, cc=cc, tt=tt: e.scalar_tensor_tensor(
                                h2nf[:, cc, tt * 128:(tt + 1) * 128],
                                tp[:, (cc % 4) * 128:((cc % 4) + 1) * 128],
                                cols[:, cc:cc + 1],
                                cols[:, 8 + cc:9 + cc].broadcast_to([128, 128]),
                                ALU.mult, ALU.add),
                                waits=[(td, tv)], incs=[(ev, 1)])
                            ev.inc()
                trk[("h2tp", i)] = td.n
                rec.add("vector", lambda e: e.tensor_copy(
                    stats[:, 10:11], stats[:, 8:9]), incs=[(vd, 1)])
                trk[("h2n", i)] = vd.inc()
            h2nf_ev = ctx.sems["ev"].n
            for m in range(NMID):
                pg = hp1[0]
                pa = hp1[1]
                wmg = [(ev, h2nf_ev), (gld, w_gld)]
                if (gi, m - 1) in trk:
                    wmg.append((sd, trk[(gi, m - 1)]))
                    wmg.append((vd, trk[(gi, m - 1, "a")]))
                elif gi > 0:
                    wmg.append((sd, trk[(gi - 1, NMID - 1)]))
                    wmg.append((vd, trk[(gi - 1, NMID - 1, "a")]))
                for c in range(8):
                    rec.add("tensor", lambda e, pg=pg, c=c, m=m, spn=spn: e.matmul(
                        pg[:, 0:spn], w1[:, c, (NMID + m) * 128:(NMID + m + 1) * 128],
                        h2nf[:, c, 0:spn], start=(c == 0), stop=(c == 7)),
                        waits=wmg if c == 0 else [], incs=[(td, 1)] if c == 7 else [])
                tvg = td.inc()
                rec.add("scalar", lambda e, pg=pg, m=m, spn=spn: e.activation(
                    Gg[:, 0:spn], pg[:, 0:spn], AF.Gelu,
                    bias=cols[:, 16 + NMID + m:17 + NMID + m]),
                    waits=[(td, tvg)], incs=[(sd, 1)])
                trk[(gi, m)] = sd.inc()
                wma = [(td, tvg)]
                if (gi, m - 1, "a") in trk:
                    wma.append((vd, trk[(gi, m - 1, "a")]))
                elif gi > 0:
                    wma.append((vd, trk[(gi - 1, NMID - 1, "a")]))
                for c in range(8):
                    rec.add("tensor", lambda e, pa=pa, c=c, m=m, spn=spn: e.matmul(
                        pa[:, 0:spn], w1[:, c, m * 128:(m + 1) * 128],
                        h2nf[:, c, 0:spn], start=(c == 0), stop=(c == 7)),
                        waits=wma if c == 0 else [], incs=[(td, 1)] if c == 7 else [])
                tva = td.inc()
                rec.add("vector", lambda e, pa=pa, m=m, spn=spn: e.scalar_tensor_tensor(
                    mid[:, m, 0:spn], pa[:, 0:spn], cols[:, 16 + m:17 + m],
                    Gg[:, 0:spn], ALU.add, ALU.mult),
                    waits=[(td, tva), (sd, trk[(gi, m)])], incs=[(vd, 1)])
                trk[(gi, m, "a")] = vd.inc()
            mid_vd = trk[(gi, NMID - 1, "a")]
            if dbg_group is not None and gi == dbg_group:
                rec.add("sync", lambda e: e.dma_start(
                    out=dbgh_h, in_=h2nf[:].rearrange("p c t -> p (c t)")),
                    waits=[(vd, mid_vd)], incs=[(st, 16)])
                st.inc(16)
                rec.add("sync", lambda e: e.dma_start(
                    out=dbgm_h, in_=mid[:].rearrange("p c t -> p (c t)")),
                    waits=[(vd, mid_vd)], incs=[(st, 16)])
                st.inc(16)
            for tt in range(nt):
                i = g0 // 128 + tt
                p = i % 2
                x2b = x2s[p]
                for half in range(2):
                    pp = xp[half]
                    wm = [(vd, mid_vd)]
                    if ("x2ev", i - 2, half) in trk:
                        wm.append((ev, trk[("x2ev", i - 2, half)]))
                    if fm_out and ("xf", i - 2) in trk:
                        wm.append((st, trk[("xf", i - 2)]))
                    for m in range(NMID):
                        rec.add("tensor", lambda e, pp=pp, m=m, tt=tt, half=half: e.matmul(
                            pp[:], mid[:, m, tt * 128:(tt + 1) * 128],
                            w2[:, m, half * 512:(half + 1) * 512],
                            start=(m == 0), stop=False),
                            waits=wm if m == 0 else [])
                    rec.add("tensor", lambda e, pp=pp, half=half: e.matmul(
                        pp[:], ones_row[:], b2row[:, half * 512:(half + 1) * 512],
                        start=False, stop=True), incs=[(td, 1)])
                    tvx = td.inc()
                    wv = [(td, tvx)]
                    if ("x2st", i - 2) in trk:
                        wv.append((st, trk[("x2st", i - 2)]))
                    xr = x1r[i % 4]
                    rec.add("vector", lambda e, pp=pp, x2b=x2b, xr=xr, half=half: e.scalar_tensor_tensor(
                        x2b[:, half * 512:(half + 1) * 512], pp[:], ctx.c("one"),
                        xr[:, half * 512:(half + 1) * 512], ALU.mult, ALU.add),
                        waits=wv, incs=[(ev, 1)] if i >= HALO // 128 else [])
                    if i < HALO // 128:
                        rec.add("vector", lambda e, x2b=x2b, half=half, i=i: e.tensor_scalar(
                            x2b[:, half * 512:(half + 1) * 512],
                            x2b[:, half * 512:(half + 1) * 512],
                            mask_sb[:, i:i + 1], None, ALU.mult), incs=[(ev, 1)])
                    trk[("x2ev", i, half)] = ev.inc()
                rec.add("sync", lambda e, i=i, x2b=x2b: e.dma_start(
                    out=x2_h[i * 128:(i + 1) * 128, :], in_=x2b[:]),
                    waits=[(ev, trk[("x2ev", i, 1)])], incs=[(st, 16)])
                trk[("x2st", i)] = st.inc(16)
                if fm_out:
                    for c in range(8):
                        wt = []
                        if c % 4 == 0:
                            wt = [(ev, trk[("x2ev", i, 1)]), (ev, ctx.sems["ev"].n)]
                        rec.add("tensor", lambda e, x2b=x2b, c=c: e.transpose(
                            tp[:, (c % 4) * 128:((c % 4) + 1) * 128],
                            x2b[:, c * 128:(c + 1) * 128], idn[:]),
                            waits=wt, incs=[(td, 1)] if c % 4 == 3 else [])
                        if c % 4 == 3:
                            tvf = td.inc()
                            wvf = [(td, tvf)]
                            if ("xf", i - 1) in trk:
                                wvf.append((st, trk[("xf", i - 1)]))
                            rec.add("vector", lambda e, c=c: e.tensor_copy(
                                fmst[:, (c - 3) * 128:(c + 1) * 128],
                                tp[:]), waits=wvf, incs=[(ev, 1)])
                            ev.inc()
                    fm_ev = ctx.sems["ev"].n
                    rec.add("sync", lambda e, i=i: e.dma_start(
                        out=x2f_h[:, i * 128:(i + 1) * 128].rearrange(
                            "(c f) t -> f c t", f=128),
                        in_=fmst[:].rearrange("f (c t) -> f c t", t=128)),
                        waits=[(ev, fm_ev)], incs=[(st, 16)])
                    trk[("xf", i)] = st.inc(16)
            trk[("gx2", gi)] = trk[("x2ev", g0 // 128 + nt - 1, 1)]
        rec.add("sync", lambda e: e.nop(), waits=[(st, st.n)])

def build_ffn_mid_phase(fm_out, dbg_group=None):
    ctx = Ctx("emit_ffn_mid")
    nc = ctx.nc
    nc = ctx.nc
    x1_h = nc.dram_tensor("x1_tm", [TC, D], BF16, kind="ExternalInput").ap()
    lg_h = nc.dram_tensor("ln_g", [D], F32, kind="ExternalInput").ap()
    lb_h = nc.dram_tensor("ln_b", [D], F32, kind="ExternalInput").ap()
    w1_h = nc.dram_tensor("w1p", [D, 2 * HIDP], F32, kind="ExternalInput").ap()
    b1_h = nc.dram_tensor("b1p", [2 * HIDP], F32, kind="ExternalInput").ap()
    w2_h = nc.dram_tensor("w2p", [HIDP, D], F32, kind="ExternalInput").ap()
    b2_h = nc.dram_tensor("b2", [D], F32, kind="ExternalInput").ap()
    msk_h = nc.dram_tensor("kvmask", [TC], F32, kind="ExternalInput").ap()
    x2_h = nc.dram_tensor("x2_tm", [TC, D], BF16, kind="ExternalOutput").ap()
    if fm_out:
        x2f_h = nc.dram_tensor("x2_fm", [D, TC], BF16, kind="ExternalOutput").ap()
    if dbg_group is not None:
        dbgh_h = nc.dram_tensor("dbgH", [128, 8 * 512], BF16, kind="ExternalOutput").ap()
        dbgm_h = nc.dram_tensor("dbgM", [128, 22 * 512], BF16, kind="ExternalOutput").ap()

    NCH = 2 * HIDP // 128
    NMID = HIDP // 128
    io = {
        "x1_tm": x1_h,
        "ln_g": lg_h,
        "ln_b": lb_h,
        "w1p": w1_h,
        "b1p": b1_h,
        "w2p": w2_h,
        "b2": b2_h,
        "kvmask": msk_h,
        "x2_tm": x2_h,
    }
    if fm_out:
        io["x2_fm"] = x2f_h
    if dbg_group is not None:
        io.update({"dbgH": dbgh_h, "dbgM": dbgm_h})
    with ctx.es:
        emit_ffn_mid(ctx, io, fm_out=fm_out, dbg_group=dbg_group)
        finish(ctx.nc, ctx.rec, ctx.es, ctx)
    return ctx.nc


def emit_combine(ctx, io):
    nc = ctx.nc
    rec = ctx.rec
    x_h = io["x"]
    s_h = io["s_tm"]
    f_h = io["f_tm"]
    o_h = io["out"]
    if True:
        ctx.setup_consts()
        rec = ctx.rec
        ld = ctx.sem("ld"); st = ctx.sem("st")
        vd = ctx.sem("vd"); sd = ctx.sem("sd")
        xb = [ctx.sbuf(f"xb{i}", [128, D], F32) for i in range(2)]
        sb = [ctx.sbuf(f"sb{i}", [128, D], BF16) for i in range(2)]
        fb = [ctx.sbuf(f"fb{i}", [128, D], BF16) for i in range(2)]
        gb = ctx.sbuf("gb", [128, D], BF16)
        tb = ctx.sbuf("tb", [128, D], F32)
        ob = [ctx.sbuf(f"ob{i}", [128, D], F32) for i in range(2)]
        trk = {}
        ntile = T_OWN // 128
        for ti in range(ntile):
            i = HALO // 128 + ti
            p = ti % 2
            w = []
            if ti >= 2:
                w = [(vd, trk[("mul", ti - 2)])]
            rec.add("sync", lambda e, i=i, p=p: e.dma_start(
                out=xb[p][:], in_=x_h[i * 128:(i + 1) * 128, :]), waits=w, incs=[(ld, 16)])
            rec.add("sync", lambda e, i=i, p=p: e.dma_start(
                out=sb[p][:], in_=s_h[i * 128:(i + 1) * 128, :]), incs=[(ld, 16)])
            rec.add("sync", lambda e, i=i, p=p: e.dma_start(
                out=fb[p][:], in_=f_h[i * 128:(i + 1) * 128, :]), incs=[(ld, 16)])
            lv = ld.inc(48)
            ws = [(ld, lv)]
            if ti >= 1:
                ws.append((vd, trk[("mul", ti - 1)]))
            rec.add("scalar", lambda e, p=p: e.activation(
                gb[:], fb[p][:], AF.Silu), waits=ws, incs=[(sd, 1)])
            s_g = sd.inc()
            wv = [(sd, s_g)]
            if ti >= 1:
                wv.append((vd, trk[("add", ti - 1)]))
            rec.add("vector", lambda e, p=p: e.tensor_tensor(
                tb[:], sb[p][:], gb[:], ALU.mult), waits=wv, incs=[(vd, 1)])
            trk[("mul", ti)] = vd.inc()
            wv2 = []
            if ti >= 2:
                wv2 = [(st, trk[("ost", ti - 2)])]
            rec.add("vector", lambda e, p=p: e.tensor_tensor(
                ob[p][:], xb[p][:], tb[:], ALU.add), waits=wv2, incs=[(vd, 1)])
            trk[("add", ti)] = vd.inc()
            rec.add("sync", lambda e, ti=ti, p=p: e.dma_start(
                out=o_h[ti * 128:(ti + 1) * 128, :], in_=ob[p][:]),
                waits=[(vd, trk[("add", ti)])], incs=[(st, 16)])
            trk[("ost", ti)] = st.inc(16)
        rec.add("sync", lambda e: e.nop(), waits=[(st, st.n)])

def build_combine_phase():
    ctx = Ctx("emit_combine")
    nc = ctx.nc
    nc = ctx.nc
    x_h = nc.dram_tensor("x", [TC, D], F32, kind="ExternalInput").ap()
    s_h = nc.dram_tensor("s_tm", [TC, D], BF16, kind="ExternalInput").ap()
    f_h = nc.dram_tensor("f_tm", [TC, D], BF16, kind="ExternalInput").ap()
    o_h = nc.dram_tensor("out", [T_OWN, D], F32, kind="ExternalOutput").ap()

    io = {
        "x": x_h,
        "s_tm": s_h,
        "f_tm": f_h,
        "out": o_h,
    }
    with ctx.es:
        emit_combine(ctx, io)
        finish(ctx.nc, ctx.rec, ctx.es, ctx)
    return ctx.nc


def _pad_slice(full, lo, hi, axis=0, dtype=None):
    n = full.shape[axis]
    lo2, hi2 = max(lo, 0), min(hi, n)
    sl = [slice(None)] * full.ndim
    sl[axis] = slice(lo2, hi2)
    part = full[tuple(sl)]
    pads = [(0, 0)] * full.ndim
    pads[axis] = (lo2 - lo, hi - hi2)
    out = np.pad(part, pads) if (lo2 - lo or hi - hi2) else part
    return out.astype(dtype) if dtype is not None else out


def _core_ranges():
    return [(c // 2, (c % 2) * T_OWN) for c in range(NCORES)]


_IDENT = np.eye(128, dtype=np.float32)


def run_ln(x_full):
    nc = build_ln_phase()
    in_maps = []
    for b, t0 in _core_ranges():
        xs = _pad_slice(x_full[b], t0 - HALO, t0 + T_OWN, axis=0).astype(np.float32)
        in_maps.append({"x": np.ascontiguousarray(xs), "ident": _IDENT})
    res = run_bass_kernel_spmd(nc, in_maps, core_ids=list(range(NCORES)))
    return [r["xn_tm"] for r in res.results], [r["xn_fm"] for r in res.results]


_MCONST = None


def run_slot(xn_fm_list, inputs):
    global _MCONST
    if _MCONST is None:
        _MCONST = make_slot_mconst()
    nc = build_slot_qkv_phase()
    im = [{"xn_fm": xn_fm_list[c], "Wqkv": inputs["L0_attn_Wqkv"],
           "bqkv": inputs["L0_attn_bqkv"]} for c in range(NCORES)]
    res = run_bass_kernel_spmd(nc, im, core_ids=list(range(NCORES)))
    q = [r["q_fm"] for r in res.results]
    k = [r["k_fm"] for r in res.results]
    v = [r["v65"] for r in res.results]
    nc = build_slot_attn_phase()
    masks = []
    for b, t0 in _core_ranges():
        m = np.zeros(TC, dtype=np.float32)
        m[max(0, HALO - t0):] = 1.0
        masks.append(m)
    im = [{"q_fm": q[c], "k_fm": k[c], "v65": v[c],
           "pos_bias": inputs["L0_attn_pos_bias"], "mconst": _MCONST,
           "kvmask": masks[c], "ident": _IDENT} for c in range(NCORES)]
    res = run_bass_kernel_spmd(nc, im, core_ids=list(range(NCORES)))
    flat = [r["flat"] for r in res.results]
    nc = build_gate_phase()
    im = [{"xn_fm": xn_fm_list[c], "flat": flat[c],
           "Wgate": inputs["L0_attn_Wgate"], "bgate": inputs["L0_attn_bgate"],
           "Wout": inputs["L0_attn_Wout"], "bout": inputs["L0_attn_bout"],
           "ident": _IDENT} for c in range(NCORES)]
    res = run_bass_kernel_spmd(nc, im, core_ids=list(range(NCORES)))
    s = [r["s_tm"] for r in res.results]
    return q, k, v, flat, s


def _prep_ffn_weights(inputs, l):
    w1 = np.asarray(inputs["L0_ffn_w1"][l])
    b1 = np.asarray(inputs["L0_ffn_b1"][l])
    w2 = np.asarray(inputs["L0_ffn_w2"][l])
    w1p = np.zeros((D, 2 * HIDP), np.float32)
    w1p[:, 0:HID] = w1[:, 0:HID]
    w1p[:, HIDP:HIDP + HID] = w1[:, HID:2 * HID]
    b1p = np.zeros(2 * HIDP, np.float32)
    b1p[0:HID] = b1[0:HID]
    b1p[HIDP:HIDP + HID] = b1[HID:2 * HID]
    w2p = np.zeros((HIDP, D), np.float32)
    w2p[0:HID] = w2
    return w1p, b1p, w2p


def _masks():
    ms = []
    for b, t0 in _core_ranges():
        m = np.zeros(TC, dtype=np.float32)
        m[max(0, HALO - t0):] = 1.0
        ms.append(m)
    return ms


def _spmd(nc, im):
    return run_bass_kernel_spmd(nc, im, core_ids=list(range(NCORES))).results


def run_ffn_layer(inputs, l, xin_fm, xres_tm, fm_out):
    masks = _masks()
    tabs = [make_rotary_tables(t0) for _, t0 in _core_ranges()]
    nc = build_ffn_qkv_phase()
    im = [{"xin_fm": xin_fm[c], "wqkv": np.asarray(inputs["L0_ffn_wqkv"][l]),
           "q_scale": np.asarray(inputs["L0_ffn_q_scale"][l]),
           "k_scale": np.asarray(inputs["L0_ffn_k_scale"][l]),
           "cq": tabs[c][0], "sq": tabs[c][1], "ck": tabs[c][2], "sk": tabs[c][3],
           "ident": _IDENT} for c in range(NCORES)]
    r = _spmd(nc, im)
    q = [x["q_fm"] for x in r]; k = [x["k_fm"] for x in r]; v = [x["v65"] for x in r]

    w01a, w01b = make_w01()
    nc = build_ffn_attn_phase()
    im = [{"q_fm": q[c], "k_fm": k[c], "v65": v[c], "w01a": w01a, "w01b": w01b,
           "kvmask": masks[c]} for c in range(NCORES)]
    r = _spmd(nc, im)
    o = [x["o_tm"] for x in r]

    nc = build_ffn_wo_phase()
    im = [{"o_tm": o[c], "wo": np.asarray(inputs["L0_ffn_wo"][l]),
           "xres_tm": xres_tm[c], "kvmask": masks[c], "ident": _IDENT}
          for c in range(NCORES)]
    r = _spmd(nc, im)
    x1 = [x["x1_tm"] for x in r]

    w1p, b1p, w2p = _prep_ffn_weights(inputs, l)
    nc = build_ffn_mid_phase(fm_out)
    im = [{"x1_tm": x1[c], "ln_g": np.asarray(inputs["L0_ffn_ln_g"][l]),
           "ln_b": np.asarray(inputs["L0_ffn_ln_b"][l]), "w1p": w1p, "b1p": b1p,
           "w2p": w2p, "b2": np.asarray(inputs["L0_ffn_b2"][l]),
           "kvmask": masks[c], "ident": _IDENT} for c in range(NCORES)]
    r = _spmd(nc, im)
    x2 = [x["x2_tm"] for x in r]
    x2f = [x["x2_fm"] for x in r] if fm_out else None
    return x1, x2, x2f


def build_mega():
    nc = bass.Bass("TRN2", target_bir_lowering=False)
    EI, IN, EO = "ExternalInput", "Internal", "ExternalOutput"

    def d(nm, shp, dt, kind):
        return nc.dram_tensor(nm, list(shp), dt, kind=kind).ap()

    g = {}
    g["x"] = d("x", [TC, D], F32, EI)
    g["ident"] = d("ident", [128, 128], F32, EI)
    g["kvmask"] = d("kvmask", [TC], F32, EI)
    g["Wqkv"] = d("Wqkv", [D, 3 * D], F32, EI)
    g["bqkv"] = d("bqkv", [3 * D], F32, EI)
    g["Wgate"] = d("Wgate", [D, D], F32, EI)
    g["bgate"] = d("bgate", [D], F32, EI)
    g["Wout"] = d("Wout", [D, D], F32, EI)
    g["bout"] = d("bout", [D], F32, EI)
    g["pos_bias"] = d("pos_bias", [44, 16], F32, EI)
    g["mconst"] = d("mconst", [44, NPOS], BF16, EI)
    g["w01a"] = d("w01a", [128, 128], BF16, EI)
    g["w01b"] = d("w01b", [128, 128], BF16, EI)
    for n in ("cq", "sq", "ck", "sk"):
        g[n] = d(n, [TC, 64], F32, EI)
    for l in range(2):
        g[f"wqkv{l}"] = d(f"wqkv{l}", [D, 1536], F32, EI)
        g[f"q_scale{l}"] = d(f"q_scale{l}", [64], F32, EI)
        g[f"k_scale{l}"] = d(f"k_scale{l}", [64], F32, EI)
        g[f"wo{l}"] = d(f"wo{l}", [512, D], F32, EI)
        g[f"ln_g{l}"] = d(f"ln_g{l}", [D], F32, EI)
        g[f"ln_b{l}"] = d(f"ln_b{l}", [D], F32, EI)
        g[f"w1p{l}"] = d(f"w1p{l}", [D, 2 * HIDP], F32, EI)
        g[f"b1p{l}"] = d(f"b1p{l}", [2 * HIDP], F32, EI)
        g[f"w2p{l}"] = d(f"w2p{l}", [HIDP, D], F32, EI)
        g[f"b2{l}"] = d(f"b2{l}", [D], F32, EI)
    g["xn_tm"] = d("xn_tm", [TC, D], BF16, IN)
    g["xn_fm"] = d("xn_fm", [D, TC], BF16, IN)
    g["sq_fm"] = d("sq_fm", [D, TC], BF16, IN)
    g["sk_fm"] = d("sk_fm", [D, TC], BF16, IN)
    g["sv65"] = d("sv65", [TC, SLOT_H * 65], BF16, IN)
    g["ball"] = d("ball", [16, NPOS], F32, IN)
    g["flat"] = d("flat", [TC, D], BF16, IN)
    g["s_tm"] = d("s_tm", [TC, D], BF16, IN)
    for l in range(2):
        g[f"fq{l}"] = d(f"fq{l}", [512, TC], BF16, IN)
        g[f"fk{l}"] = d(f"fk{l}", [512, TC], BF16, IN)
        g[f"fv{l}"] = d(f"fv{l}", [TC, FFN_H * 65], BF16, IN)
        g[f"fo{l}"] = d(f"fo{l}", [TC, 512], BF16, IN)
        g[f"fx1{l}"] = d(f"fx1{l}", [TC, D], BF16, IN)
        g[f"fx2{l}"] = d(f"fx2{l}", [TC, D], BF16, IN)
    g["x2_fm"] = d("x2_fm", [D, TC], BF16, IN)
    g["out"] = d("out", [T_OWN, D], F32, EO)

    mega_es = contextlib.ExitStack()
    barrier = [()]

    shared_sems = {}

    def phase(name, emitfn, io, *args, **kw):
        ctx = Ctx(name, nc, sem_es=mega_es, sems=shared_sems)
        with ctx.es:
            emitfn(ctx, io, *args, **kw)
            finish(nc, ctx.rec, ctx.es, ctx, barrier[0])
        barrier[0] = ctx.final_sems()

    with mega_es:
        phase("ln", emit_ln, {"x": g["x"], "xn_tm": g["xn_tm"], "xn_fm": g["xn_fm"],
                              "ident": g["ident"]})
        phase("sqkv", emit_slot_qkv, {"xn_fm": g["xn_fm"], "Wqkv": g["Wqkv"],
                                      "bqkv": g["bqkv"], "q_fm": g["sq_fm"],
                                      "k_fm": g["sk_fm"], "v65": g["sv65"]})
        phase("sattn", emit_slot_attn, {"q_fm": g["sq_fm"], "k_fm": g["sk_fm"],
                                        "v65": g["sv65"], "pos_bias": g["pos_bias"],
                                        "mconst": g["mconst"], "kvmask": g["kvmask"],
                                        "flat": g["flat"], "ball_scratch": g["ball"],
                                        "ident": g["ident"]})
        phase("gate", emit_gate, {"xn_fm": g["xn_fm"], "flat": g["flat"],
                                  "Wgate": g["Wgate"], "bgate": g["bgate"],
                                  "Wout": g["Wout"], "bout": g["bout"],
                                  "s_tm": g["s_tm"], "ident": g["ident"]})
        for l in range(2):
            xin_fm = g["xn_fm"] if l == 0 else g["x2_fm"]
            xres = g["xn_tm"] if l == 0 else g["fx2" + "0"]
            phase(f"fqkv{l}", emit_ffn_qkv,
                  {"xin_fm": xin_fm, "wqkv": g[f"wqkv{l}"],
                   "q_scale": g[f"q_scale{l}"], "k_scale": g[f"k_scale{l}"],
                   "cq": g["cq"], "sq": g["sq"], "ck": g["ck"], "sk": g["sk"],
                   "q_fm": g[f"fq{l}"], "k_fm": g[f"fk{l}"], "v65": g[f"fv{l}"],
                   "ident": g["ident"]})
            phase(f"fattn{l}", emit_ffn_attn,
                  {"q_fm": g[f"fq{l}"], "k_fm": g[f"fk{l}"], "v65": g[f"fv{l}"],
                   "w01a": g["w01a"], "w01b": g["w01b"], "kvmask": g["kvmask"],
                   "o_tm": g[f"fo{l}"]})
            phase(f"fwo{l}", emit_ffn_wo,
                  {"o_tm": g[f"fo{l}"], "wo": g[f"wo{l}"], "xres_tm": xres,
                   "kvmask": g["kvmask"], "x1_tm": g[f"fx1{l}"], "ident": g["ident"]})
            io_mid = {"x1_tm": g[f"fx1{l}"], "ln_g": g[f"ln_g{l}"],
                      "ln_b": g[f"ln_b{l}"], "w1p": g[f"w1p{l}"], "b1p": g[f"b1p{l}"],
                      "w2p": g[f"w2p{l}"], "b2": g[f"b2{l}"], "kvmask": g["kvmask"],
                      "x2_tm": g[f"fx2{l}"], "ident": g["ident"]}
            if l == 0:
                io_mid["x2_fm"] = g["x2_fm"]
            phase(f"fmid{l}", emit_ffn_mid, io_mid, l == 0)
        phase("comb", emit_combine, {"x": g["x"], "s_tm": g["s_tm"],
                                     "f_tm": g["fx21"], "out": g["out"]})
    return nc


_MEGA_NC = None


def kernel(**inputs):
    global _MEGA_NC, _MCONST
    if _MCONST is None:
        _MCONST = make_slot_mconst()
    if _MEGA_NC is None:
        _MEGA_NC = build_mega()
    x_full = np.asarray(inputs["L0_attn_x"])
    masks = _masks()
    w01a, w01b = make_w01()
    preps = [_prep_ffn_weights(inputs, l) for l in range(2)]
    in_maps = []
    for c, (b, t0) in enumerate(_core_ranges()):
        xs = _pad_slice(x_full[b], t0 - HALO, t0 + T_OWN, axis=0).astype(np.float32)
        cq, sq, ck, sk = make_rotary_tables(t0)
        im = {"x": np.ascontiguousarray(xs), "ident": _IDENT, "kvmask": masks[c],
              "Wqkv": np.asarray(inputs["L0_attn_Wqkv"]),
              "bqkv": np.asarray(inputs["L0_attn_bqkv"]),
              "Wgate": np.asarray(inputs["L0_attn_Wgate"]),
              "bgate": np.asarray(inputs["L0_attn_bgate"]),
              "Wout": np.asarray(inputs["L0_attn_Wout"]),
              "bout": np.asarray(inputs["L0_attn_bout"]),
              "pos_bias": np.asarray(inputs["L0_attn_pos_bias"]),
              "mconst": _MCONST, "w01a": w01a, "w01b": w01b,
              "cq": cq, "sq": sq, "ck": ck, "sk": sk}
        for l in range(2):
            im[f"wqkv{l}"] = np.asarray(inputs["L0_ffn_wqkv"][l])
            im[f"q_scale{l}"] = np.asarray(inputs["L0_ffn_q_scale"][l])
            im[f"k_scale{l}"] = np.asarray(inputs["L0_ffn_k_scale"][l])
            im[f"wo{l}"] = np.asarray(inputs["L0_ffn_wo"][l])
            im[f"ln_g{l}"] = np.asarray(inputs["L0_ffn_ln_g"][l])
            im[f"ln_b{l}"] = np.asarray(inputs["L0_ffn_ln_b"][l])
            im[f"w1p{l}"], im[f"b1p{l}"], im[f"w2p{l}"] = preps[l]
            im[f"b2{l}"] = np.asarray(inputs["L0_ffn_b2"][l])
        in_maps.append(im)
    res = run_bass_kernel_spmd(_MEGA_NC, in_maps, core_ids=list(range(NCORES)))
    out = np.empty((B, N, D), np.float32)
    for c, (b, t0) in enumerate(_core_ranges()):
        out[b, t0:t0 + T_OWN] = res.results[c]["out"]
    return out


def kernel_unfused(**inputs):
    x_full = np.asarray(inputs["L0_attn_x"])
    xn_tm, xn_fm = run_ln(x_full)
    _, _, _, _, s = run_slot(xn_fm, inputs)
    _, x2, x2f = run_ffn_layer(inputs, 0, xn_fm, xn_tm, True)
    x2p, f, _ = run_ffn_layer(inputs, 1, x2f, x2, False)
    nc = build_combine_phase()
    im = []
    for c, (b, t0) in enumerate(_core_ranges()):
        xs = _pad_slice(x_full[b], t0 - HALO, t0 + T_OWN, axis=0).astype(np.float32)
        im.append({"x": np.ascontiguousarray(xs), "s_tm": s[c], "f_tm": f[c]})
    r = _spmd(nc, im)
    out = np.empty((B, N, D), np.float32)
    for c, (b, t0) in enumerate(_core_ranges()):
        out[b, t0:t0 + T_OWN] = r[c]["out"]
    return out


# version 05
# speedup vs baseline: 1.2356x; 1.2356x over previous
import contextlib
import math
import numpy as np
import ml_dtypes

import concourse.bass as bass
import concourse.mybir as mybir
from concourse.bass_utils import run_bass_kernel_spmd

F32 = mybir.dt.float32
BF16 = mybir.dt.bfloat16
AF = mybir.ActivationFunctionType
ALU = mybir.AluOpType
AX = mybir.AxisListType
BF = ml_dtypes.bfloat16

B, N, D = 4, 8192, 1024
NCORES = 8
T_OWN = 4096
HALO = 1664
TC = T_OWN + HALO
OFFSETS = sorted(set(range(0, 33)) | {48, 64, 96, 128, 192, 256, 384, 512, 768, 1024, 1536})
SLOT_H, SLOT_HD = 16, 64
SLOT_BLOCKS = [0, 1, 2, 3, 4, 6, 8, 12]
FFN_H, FFN_HD = 8, 64
WIN = 512
HID = 2730
HIDP = 2816
NG = TC // 512


def cdiv(a, b):
    return (a + b - 1) // b


class Sem:

    def __init__(self, es, nc, name):
        self.h = es.enter_context(nc.semaphore(name))
        self.n = 0

    def inc(self, k=1):
        self.n += k
        return self.n


class Rec:

    def __init__(self):
        self.ops = {"sync": [], "tensor": [], "vector": [], "scalar": [], "gpsimd": []}

    def add(self, engine, fn, waits=(), incs=()):
        for sem, v in waits:
            rec_n = getattr(sem, "recorded", 0)
            assert v <= rec_n, (
                f"wait on {sem.h} for {v} but only {rec_n} incs recorded so far "
                f"(engine {engine}, op #{len(self.ops[engine])})")
        self.ops[engine].append((fn, list(waits), list(incs)))
        for sem, amount in incs:
            sem.recorded = getattr(sem, "recorded", 0) + amount

    @staticmethod
    def check(ctx):
        for name, sem in ctx.sems.items():
            rec_n = getattr(sem, "recorded", 0)
            assert rec_n == sem.n, (
                f"sem {name}: recorded incs {rec_n} != counter {sem.n}")

    def replay(self, engine_name, barrier=()):
        def runner(eng):
            first = True
            for fn, waits, incs in self.ops[engine_name]:
                if first:
                    for sem, val in barrier:
                        eng.wait_ge(sem.h, val)
                    first = False
                for sem, val in waits:
                    eng.wait_ge(sem.h, val)
                inst = fn(eng)
                for sem, val in incs:
                    inst.then_inc(sem.h, val)
        return runner


def finish(nc, rec, es, ctx=None, barrier=()):
    if ctx is not None:
        Rec.check(ctx)
    block = es.enter_context(nc.Block())
    for name in ("sync", "tensor", "vector", "scalar", "gpsimd"):
        if rec.ops[name]:
            getattr(block, name)(rec.replay(name, barrier))


CONSTS = {
    "one": 1.0, "neg1": -1.0, "inv1024": 1.0 / 1024.0, "eps": 1e-5,
    "tiny": 1e-30, "eighth": 0.125, "half": 0.5, "clip": 1e-12, "clip2": 1e-24,
}


class Ctx:

    def __init__(self, name, nc=None, sem_es=None, sems=None):
        self.nc = nc if nc is not None else bass.Bass("TRN2", target_bir_lowering=False)
        self.prefix = name + "_" if nc is not None else ""
        self.es = contextlib.ExitStack()
        self.sem_es = sem_es if sem_es is not None else self.es
        self.rec = Rec()
        self.sems = sems if sems is not None else {}
        self._sb = {}

    def sem(self, name):
        if name not in self.sems:
            self.sems[name] = Sem(self.sem_es, self.nc, name)
        return self.sems[name]

    def sbuf(self, name, shape, dtype):
        t = self.es.enter_context(self.nc.sbuf_tensor(self.prefix + name, list(shape), dtype))
        self._sb[name] = t
        return t

    def psum(self, name, shape, dtype=F32):
        return self.es.enter_context(self.nc.psum_tensor(self.prefix + name, list(shape), dtype))

    def final_sems(self):
        return [(s, s.n) for s in self.sems.values() if s.n > 0]

    def setup_consts(self):
        names = list(CONSTS)
        self.cst = self.sbuf("csts", [128, len(names)], F32)
        self.cidx = {n: i for i, n in enumerate(names)}
        s = self.sem("cst_ready")
        for n in names:
            i = self.cidx[n]
            v = CONSTS[n]
            self.rec.add("vector", lambda e, i=i, v=v: e.memset(self.cst[:, i:i + 1], v),
                         incs=[(s, 1)])
        self.cst_target = s.inc(len(names))

    def c(self, name):
        i = self.cidx[name]
        return self.cst[:, i:i + 1]


def build_identity_bf16(ctx, ident_h=None):
    if ident_h is None:
        ident_h = ctx.nc.dram_tensor("ident", [128, 128], F32, kind="ExternalInput").ap()
    idn = ctx.sbuf("idn", [128, 128], BF16)
    s = ctx.sem("ident_rdy")
    ctx.rec.add("gpsimd", lambda e: e.dma_start(out=idn[:], in_=ident_h),
                incs=[(s, 16)])
    s.inc(16)
    return idn


def emit_ln(ctx, io):
    nc = ctx.nc
    rec = ctx.rec
    x_h = io["x"]
    xn_tm_h = io["xn_tm"]
    xn_fm_h = io["xn_fm"]
    if True:
        ctx.setup_consts()
        idn = build_identity_bf16(ctx, io.get("ident"))
        rec = ctx.rec
        xt = [ctx.sbuf(f"xt{i}", [128, D], F32) for i in range(2)]
        xnt = [ctx.sbuf(f"xnt{i}", [128, D], BF16) for i in range(2)]
        fmt = [ctx.sbuf(f"fmt{i}", [128, D], BF16) for i in range(2)]
        stats = ctx.sbuf("stats", [128, 8], F32)
        scratch = ctx.sbuf("scr", [128, D], F32)
        tp = [ctx.psum(f"tp{i}", [128, 512], BF16) for i in range(2)]

        ld = ctx.sem("ld"); st = ctx.sem("st"); vdone = ctx.sem("vd")
        sdone = ctx.sem("sd"); tdone = ctx.sem("td"); ev = ctx.sem("ev")

        ntiles = TC // 128
        ttr_done = [0] * ntiles
        aff_done = [0] * ntiles
        xn_cons = [0] * ntiles
        xn_tr = [0] * ntiles
        fm_st = [0] * ntiles
        ev_done = [[0] * ntiles, [0] * ntiles]

        for i in range(ntiles):
            p = i % 2
            x_sb, xn_sb, fm_sb = xt[p], xnt[p], fmt[p]
            w = []
            if i >= 2:
                w = [(vdone, ttr_done[i - 2]), (sdone, aff_done[i - 2])]
            rec.add("sync", lambda e, i=i, x_sb=x_sb: e.dma_start(
                out=x_sb[:], in_=x_h[i * 128:(i + 1) * 128, :]), waits=w, incs=[(ld, 16)])
            ldv = ld.inc(16)

            rec.add("vector", lambda e, x_sb=x_sb: e.reduce_sum(
                stats[:, 0:1], x_sb[:], AX.X), waits=[(ld, ldv), (ctx.sem("cst_ready"), ctx.cst_target)])
            rec.add("scalar", lambda e, x_sb=x_sb: e.activation(
                scratch[:], x_sb[:], AF.Square), waits=[(ld, ldv)], incs=[(sdone, 1)])
            s_sq = sdone.inc()
            rec.add("vector", lambda e: e.reduce_sum(
                stats[:, 1:2], scratch[:], AX.X), waits=[(sdone, s_sq)], incs=[(vdone, 1)])
            ttr_done[i] = vdone.inc()
            rec.add("vector", lambda e: e.tensor_scalar(
                stats[:, 2:3], stats[:, 0:1], ctx.c("inv1024"), None, ALU.mult), incs=[(vdone, 1)])
            v_mu = vdone.inc()
            rec.add("vector", lambda e: e.tensor_scalar(
                stats[:, 3:4], stats[:, 1:2], ctx.c("inv1024"), None, ALU.mult))
            rec.add("vector", lambda e: e.scalar_tensor_tensor(
                stats[:, 4:5], stats[:, 2:3], stats[:, 2:3], stats[:, 3:4],
                ALU.mult, ALU.subtract), waits=[(vdone, v_mu)], incs=[(vdone, 1)])
            v_t4 = vdone.inc()
            rec.add("scalar", lambda e: e.activation(
                stats[:, 5:6], stats[:, 4:5], AF.Sqrt, bias=ctx.c("eps"),
                scale=ctx.c("neg1")), waits=[(vdone, v_t4)], incs=[(sdone, 1)])
            s_std = sdone.inc()
            rec.add("vector", lambda e: e.reciprocal(
                stats[:, 6:7], stats[:, 5:6]), waits=[(sdone, s_std)], incs=[(vdone, 1)])
            v_rec = vdone.inc()
            rec.add("vector", lambda e: e.tensor_scalar(
                stats[:, 7:8], stats[:, 2:3], stats[:, 6:7], ctx.c("neg1"),
                ALU.mult, ALU.mult), waits=[(vdone, v_rec)], incs=[(vdone, 1)])
            v_mb = vdone.inc()
            w = [(vdone, v_mb)]
            if i >= 2:
                w += [(st, xn_cons[i - 2]), (tdone, xn_tr[i - 2])]
            rec.add("scalar", lambda e, x_sb=x_sb, xn_sb=xn_sb: e.activation(
                xn_sb[:], x_sb[:], AF.Identity, bias=stats[:, 7:8], scale=stats[:, 6:7]),
                waits=w, incs=[(sdone, 1)])
            s_aff = sdone.inc()
            aff_done[i] = s_aff
            rec.add("sync", lambda e, i=i, xn_sb=xn_sb: e.dma_start(
                out=xn_tm_h[i * 128:(i + 1) * 128, :], in_=xn_sb[:]),
                waits=[(sdone, s_aff)], incs=[(st, 16)])
            xn_cons[i] = st.inc(16)
            for half in range(2):
                pb = tp[half]
                for j in range(4):
                    c = half * 4 + j
                    wv = []
                    if j == 0:
                        wv.append((sdone, s_aff))
                        if i >= 1:
                            wv.append((ev, ev_done[half][i - 1]))
                    rec.add("tensor", lambda e, xn_sb=xn_sb, c=c, pb=pb, j=j: e.transpose(
                        pb[:, j * 128:(j + 1) * 128], xn_sb[:, c * 128:(c + 1) * 128], idn[:]),
                        waits=wv, incs=[(tdone, 1)] if j == 3 else [])
                td = tdone.inc()
                wv = [(tdone, td)]
                if i >= 2 and half == 0:
                    wv.append((st, fm_st[i - 2]))
                rec.add("vector", lambda e, pb=pb, fm_sb=fm_sb, half=half: e.tensor_copy(
                    fm_sb[:, half * 512:(half + 1) * 512], pb[:]),
                    waits=wv, incs=[(ev, 1)])
                ev_done[half][i] = ev.inc()
            xn_tr[i] = tdone.n
            rec.add("sync", lambda e, i=i, fm_sb=fm_sb: e.dma_start(
                out=xn_fm_h[:, i * 128:(i + 1) * 128].rearrange("(c f) t -> f c t", f=128),
                in_=fm_sb[:].rearrange("f (c t) -> f c t", t=128)),
                waits=[(ev, ev_done[1][i])], incs=[(st, 16)])
            fm_st[i] = st.inc(16)
        rec.add("sync", lambda e: e.nop(), waits=[(st, st.n)])

def build_ln_phase():
    ctx = Ctx("emit_ln")
    nc = ctx.nc
    nc = ctx.nc
    x_h = nc.dram_tensor("x", [TC, D], F32, kind="ExternalInput").ap()
    xn_tm_h = nc.dram_tensor("xn_tm", [TC, D], BF16, kind="ExternalOutput").ap()
    xn_fm_h = nc.dram_tensor("xn_fm", [D, TC], BF16, kind="ExternalOutput").ap()

    io = {
        "x": x_h,
        "xn_tm": xn_tm_h,
        "xn_fm": xn_fm_h,
    }
    with ctx.es:
        emit_ln(ctx, io)
        finish(ctx.nc, ctx.rec, ctx.es, ctx)
    return ctx.nc


def _ttiles(total, span=512):
    out = []
    t = 0
    while t < total:
        s = min(span, total - t)
        out.append((t, s))
        t += s
    return out


def emit_slot_qkv(ctx, io):
    nc = ctx.nc
    rec = ctx.rec
    xn_fm_h = io["xn_fm"]
    w_h = io["Wqkv"]
    b_h = io["bqkv"]
    q_fm_h = io["q_fm"]
    k_fm_h = io["k_fm"]
    v65_h = io["v65"]
    if True:
        ctx.setup_consts()
        rec = ctx.rec
        wq = ctx.sbuf("wq", [128, 8, 3 * D], BF16)
        bqc = ctx.sbuf("bqc", [128, 24], F32)
        bqrow = ctx.sbuf("bqrow", [1, D], F32)
        ones_row = ctx.sbuf("ones_row", [1, 128], F32)
        xn_sb = [ctx.sbuf(f"xn{i}", [128, 8, 512], BF16) for i in range(2)]
        stage = [ctx.sbuf(f"stg{i}", [128, 512], BF16) for i in range(2)]
        v_sb = [ctx.sbuf(f"vsb{i}", [128, SLOT_H, 65], BF16) for i in range(2)]
        ps = [ctx.psum(f"ps{i}", [128, 512]) for i in range(2)]

        ld = ctx.sem("ld"); wrdy = ctx.sem("wrdy"); st = ctx.sem("st")
        td = ctx.sem("td"); ev = ctx.sem("ev")

        rec.add("gpsimd", lambda e: e.dma_start(
            out=wq[:], in_=w_h.rearrange("(c p) e -> p c e", p=128)), incs=[(wrdy, 16)])
        rec.add("sync", lambda e: e.dma_start(
            out=bqc[:], in_=b_h.rearrange("(c p) -> p c", p=128),
            allow_slow_non_contiguous=True), incs=[(wrdy, 16)])
        rec.add("sync", lambda e: e.dma_start(
            out=bqrow[:], in_=b_h[2048:3072].rearrange("(o e) -> o e", o=1)), incs=[(wrdy, 16)])
        wv = wrdy.inc(48)
        rec.add("vector", lambda e: e.memset(ones_row[:], 1.0))
        rec.add("vector", lambda e: e.tensor_scalar(
            wq[:, :, 0:1024], wq[:, :, 0:1024], ctx.c("eighth"), None, ALU.mult),
            waits=[(wrdy, wv), (ctx.sem("cst_ready"), ctx.cst_target)])
        rec.add("vector", lambda e: e.tensor_scalar(
            bqc[:, 0:8], bqc[:, 0:8], ctx.c("eighth"), None, ALU.mult), incs=[(wrdy, 1)])
        wv = wrdy.inc()

        tt = _ttiles(TC, 512)
        ev_after = {}
        ld_after = {}
        for it, (t0, sp) in enumerate(tt):
            p = it % 2
            xs = xn_sb[p]
            w = [(wrdy, wv)]
            if it >= 2:
                w.append((td, ld_after[it - 2]))
            rec.add("sync", lambda e, t0=t0, sp=sp, xs=xs: e.dma_start(
                out=xs[:, :, 0:sp],
                in_=xn_fm_h[:, t0:t0 + sp].rearrange("(c p) t -> p c t", p=128)),
                waits=w, incs=[(ld, 16)])
            lv = ld.inc(16)
            for m in range(16):
                pp = ps[m % 2]
                wm = [(ld, lv)]
                if it > 0 or m >= 2:
                    wm.append((ev, ev_after[(it, m - 2) if m >= 2 else (it - 1, 14 + m)]))
                for c in range(8):
                    rec.add("tensor", lambda e, pp=pp, m=m, c=c, xs=xs, sp=sp: e.matmul(
                        pp[:, 0:sp], wq[:, c, m * 128:(m + 1) * 128], xs[:, c, 0:sp],
                        start=(c == 0), stop=(c == 7)),
                        waits=wm if c == 0 else [], incs=[(td, 1)] if c == 7 else [])
                tv = td.inc()
                if m == 15:
                    ld_after[it] = tv
                stg = stage[m % 2]
                rec.add("vector", lambda e, pp=pp, m=m, stg=stg, sp=sp: e.tensor_scalar(
                    stg[:, 0:sp], pp[:, 0:sp], bqc[:, m:m + 1], None, ALU.add),
                    waits=[(td, tv)] + ([(st, ev_after.get(("st", it, m - 2),
                           ev_after.get(("st", it - 1, 14 + m), 0)))] if (it > 0 or m >= 2) else []),
                    incs=[(ev, 1)])
                evv = ev.inc()
                ev_after[(it, m)] = evv
                dst = q_fm_h if m < 8 else k_fm_h
                mm = m % 8
                rec.add("sync", lambda e, dst=dst, mm=mm, t0=t0, sp=sp, stg=stg: e.dma_start(
                    out=dst[mm * 128:(mm + 1) * 128, t0:t0 + sp], in_=stg[:, 0:sp]),
                    waits=[(ev, evv)], incs=[(st, 16)])
                ev_after[("st", it, m)] = st.inc(16)

        for i in range(TC // 128):
            p = i % 2
            xs, vs = xn_sb[p], v_sb[p]
            w = []
            if i >= 2:
                w.append((td, ld_after[("v", i - 2)]))
            else:
                w.append((td, ld_after[len(_ttiles(TC, 512)) - 1]))
            rec.add("sync", lambda e, i=i, xs=xs: e.dma_start(
                out=xs[:, :, 0:128],
                in_=xn_fm_h[:, i * 128:(i + 1) * 128].rearrange("(c p) t -> p c t", p=128)),
                waits=w, incs=[(ld, 16)])
            lv = ld.inc(16)
            if i < 2:
                rec.add("vector", lambda e, vs=vs: e.memset(vs[:, :, 64:65], 1.0))
            for half in range(2):
                pp = ps[half]
                wm = [(ld, lv)]
                if i > 0:
                    wm.append((ev, ev_after[("v", i - 1, half)]))
                else:
                    last_it = len(_ttiles(TC, 512)) - 1
                    wm.append((ev, ev_after[(last_it, 14 + half)]))
                for c in range(8):
                    rec.add("tensor", lambda e, pp=pp, c=c, half=half, xs=xs: e.matmul(
                        pp[:], xs[:, c, 0:128], wq[:, c, 2048 + half * 512: 2048 + (half + 1) * 512],
                        start=(c == 0), stop=False), waits=wm if c == 0 else [])
                rec.add("tensor", lambda e, pp=pp, half=half: e.matmul(
                    pp[:], ones_row[:], bqrow[:, half * 512:(half + 1) * 512],
                    start=False, stop=True), incs=[(td, 1)])
                tv = td.inc()
                if half == 1:
                    ld_after[("v", i)] = tv
                wv3 = [(td, tv)]
                if i >= 2:
                    wv3.append((st, ev_after[("vst", i - 2)]))
                rec.add("vector", lambda e, pp=pp, vs=vs, half=half: e.tensor_copy(
                    vs[:, half * 8:(half + 1) * 8, 0:64],
                    pp[:].rearrange("p (h e) -> p h e", e=64)), waits=wv3, incs=[(ev, 1)])
                ev_after[("v", i, half)] = ev.inc()
            rec.add("sync", lambda e, i=i, vs=vs: e.dma_start(
                out=v65_h[i * 128:(i + 1) * 128, :], in_=vs[:].rearrange("p h e -> p (h e)")),
                waits=[(ev, ev_after[("v", i, 1)])], incs=[(st, 16)])
            ev_after[("vst", i)] = st.inc(16)
        rec.add("sync", lambda e: e.nop(), waits=[(st, st.n)])

def build_slot_qkv_phase():
    ctx = Ctx("emit_slot_qkv")
    nc = ctx.nc
    nc = ctx.nc
    xn_fm_h = nc.dram_tensor("xn_fm", [D, TC], BF16, kind="ExternalInput").ap()
    w_h = nc.dram_tensor("Wqkv", [D, 3 * D], F32, kind="ExternalInput").ap()
    b_h = nc.dram_tensor("bqkv", [3 * D], F32, kind="ExternalInput").ap()
    q_fm_h = nc.dram_tensor("q_fm", [D, TC], BF16, kind="ExternalOutput").ap()
    k_fm_h = nc.dram_tensor("k_fm", [D, TC], BF16, kind="ExternalOutput").ap()
    v65_h = nc.dram_tensor("v65", [TC, SLOT_H * 65], BF16, kind="ExternalOutput").ap()

    io = {
        "xn_fm": xn_fm_h,
        "Wqkv": w_h,
        "bqkv": b_h,
        "q_fm": q_fm_h,
        "k_fm": k_fm_h,
        "v65": v65_h,
    }
    with ctx.es:
        emit_slot_qkv(ctx, io)
        finish(ctx.nc, ctx.rec, ctx.es, ctx)
    return ctx.nc


NPOS_NEAR = 128 * 384
NPOS = NPOS_NEAR + 4096
FAR_KS = [3, 4, 6, 8, 12]


def make_slot_mconst():
    oi = {d: i for i, d in enumerate(OFFSETS)}
    M = np.zeros((44, NPOS), dtype=np.float32)
    j = np.arange(128)[:, None]
    c = np.arange(128)[None, :]
    for kidx in range(3):
        delta = 128 * kidx + c - j
        for d, i in oi.items():
            jj, cc = np.nonzero(delta == d)
            M[i, jj * 384 + kidx * 128 + cc] = 1.0
    for f, k in enumerate(FAR_KS):
        M[oi[128 * k], NPOS_NEAR + f] = 1.0
    return M.astype(BF)


def emit_slot_attn(ctx, io, dbg_tile=None, dbg_head=0):
    nc = ctx.nc
    rec = ctx.rec
    q_h = io["q_fm"]
    k_h = io["k_fm"]
    v_h = io["v65"]
    pb_h = io["pos_bias"]
    mc_h = io["mconst"]
    msk_h = io["kvmask"]
    flat_h = io["flat"]
    ball_h = io["ball_scratch"]
    dbgS_h = io.get("dbgS")
    dbgq_h = io.get("dbgq")
    dbgk_h = io.get("dbgk")
    dbgE_h = io.get("dbgE")
    dbgG_h = io.get("dbgG")
    dbgO_h = io.get("dbgO")
    dbgB_h = io.get("dbgB")
    if True:
        ctx.setup_consts()
        idn = build_identity_bf16(ctx, io.get("ident"))
        rec = ctx.rec
        ld = ctx.sem("ld"); st = ctx.sem("st"); td = ctx.sem("td")
        vd = ctx.sem("vd"); sd = ctx.sem("sd"); ev = ctx.sem("ev")
        gld = ctx.sem("gld")

        pb_sb = ctx.sbuf("pb", [44, 16], F32)
        epb = ctx.sbuf("epb", [44, 16], BF16)
        MCW = 4096
        mc_sb = [ctx.sbuf(f"mc{j}", [44, MCW], BF16) for j in range(2)]
        bst = ctx.sbuf("bst", [16, 512], F32)
        bps = ctx.psum("bps", [16, 512])
        rec.add("sync", lambda e: e.dma_start(out=pb_sb[:], in_=pb_h), incs=[(ld, 16)])
        lv0 = ld.inc(16)
        rec.add("scalar", lambda e: e.activation(epb[:], pb_sb[:], AF.Exp),
                waits=[(ld, lv0)], incs=[(sd, 1)])
        sd_epb = sd.inc()
        mm_done = {}
        for j in range(NPOS // MCW):
            mcb = mc_sb[j % 2]
            w = [(td, mm_done[j - 2])] if j >= 2 else []
            rec.add("gpsimd", lambda e, j=j, mcb=mcb: e.dma_start(
                out=mcb[:], in_=mc_h[:, j * MCW:(j + 1) * MCW]), waits=w, incs=[(gld, 16)])
            lvj = gld.inc(16)
            for i2 in range(MCW // 512):
                i = j * (MCW // 512) + i2
                rec.add("tensor", lambda e, i2=i2, mcb=mcb: e.matmul(
                    bps[:], epb[:], mcb[:, i2 * 512:(i2 + 1) * 512], start=True, stop=True),
                    waits=[(sd, sd_epb), (gld, lvj)] + ([(ev, ctx.sems["ev"].n)] if i >= 1 else []),
                    incs=[(td, 1)])
                tv = td.inc()
                mm_done[j] = tv
                rec.add("vector", lambda e: e.tensor_copy(bst[:], bps[:]),
                        waits=[(td, tv)] + ([(st, ctx.sems["st"].n)] if i >= 1 else []),
                        incs=[(ev, 1)])
                evv = ev.inc()
                rec.add("sync", lambda e, i=i: e.dma_start(
                    out=ball_h[:, i * 512:(i + 1) * 512], in_=bst[:]),
                    waits=[(ev, evv)], incs=[(st, 16)])
                st.inc(16)
        bnear = ctx.sbuf("bnear", [128, 16, 384], BF16)
        pbf = ctx.sbuf("pbf", [128, 16, len(FAR_KS)], F32)
        wb = [(st, st.n)]
        for h in range(16):
            rec.add("gpsimd", lambda e, h=h: e.dma_start(
                out=bnear[:, h, :],
                in_=ball_h[h, 0:NPOS_NEAR].rearrange("(j c) -> j c", j=128)),
                waits=wb if h == 0 else [], incs=[(gld, 16)])
            gld.inc(16)
        rec.add("sync", lambda e: e.dma_start(
            out=pbf[:],
            in_=ball_h[:, NPOS_NEAR:NPOS_NEAR + len(FAR_KS)].unsqueeze(0).broadcast_to(
                [128, 16, len(FAR_KS)])), waits=wb, incs=[(ld, 16)])
        ld.inc(16)
        mask_sb = ctx.sbuf("masks", [128, TC // 128], F32)
        rec.add("sync", lambda e: e.dma_start(
            out=mask_sb[:], in_=msk_h.rearrange("(i p) -> p i", p=128),
            allow_slow_non_contiguous=True), incs=[(ld, 16)])
        prep_ld = ld.inc(16)
        prep_gld = gld.n

        RING = 16
        q_sb = [ctx.sbuf(f"qsb{i}", [64, 16, 128], BF16) for i in range(2)]
        k_sb = ctx.sbuf("kring", [64, 16, RING * 128], BF16)
        v_sb = ctx.sbuf("vring", [128, RING, SLOT_H * 65], BF16)
        E_sb = [ctx.sbuf(f"esb{i}", [128, 1024], BF16) for i in range(2)]
        G_sb = [ctx.sbuf(f"gsb{i}", [128, 1024], BF16) for i in range(2)]
        o_stage = ctx.sbuf("ostg", [128, SLOT_H, 65], F32)
        zcol = ctx.sbuf("zcol", [128, 2 * SLOT_H], F32)
        tmpc = ctx.sbuf("tmpc", [128, 8], F32)
        flat_sb = [ctx.sbuf(f"flt{i}", [128, D], BF16) for i in range(2)]
        ps_e = [ctx.psum(f"pse{i}", [128, 1024]) for i in range(2)]
        o_ps = [ctx.psum(f"ops{i}", [128, 512]) for i in range(2)]

        tiles = list(range(HALO // 128, TC // 128))
        dbg_barrier = [0]
        tile_mm_done = {}
        tile_pv_done = {}
        exp_done = {}
        g_done = {}
        oev_done = {}
        flat_st = {}
        first_i = tiles[0]
        pre0 = first_i - 12
        assert (pre0 % RING) + 12 <= RING
        rec.add("sync", lambda e: e.dma_start(
            out=k_sb[:, :, (pre0 % RING) * 128:((pre0 % RING) + 12) * 128],
            in_=k_h[:, pre0 * 128:(pre0 + 12) * 128].rearrange(
                "(h p) t -> p h t", p=64)), incs=[(ld, 16)])
        ld.inc(16)
        rec.add("gpsimd", lambda e: e.dma_start(
            out=v_sb[:, pre0 % RING:(pre0 % RING) + 12, :],
            in_=v_h[pre0 * 128:(pre0 + 12) * 128, :].rearrange(
                "(b p) e -> p b e", p=128)), incs=[(gld, 16)])
        gld.inc(16)
        for ti, i in enumerate(tiles):
            p = ti % 2
            qs, ks, vs = q_sb[p], k_sb, v_sb
            blocks = [i - k for k in SLOT_BLOCKS]
            w = []
            if ti >= 2:
                w = [(td, tile_pv_done[ti - 2])]
            rec.add("sync", lambda e, i=i, qs=qs: e.dma_start(
                out=qs[:], in_=q_h[:, i * 128:(i + 1) * 128].rearrange(
                    "(h p) t -> p h t", p=64)), waits=w, incs=[(ld, 16)])
            wr = []
            if ti - 4 >= 0:
                wr = [(td, tile_pv_done[ti - 4])]
            rec.add("sync", lambda e, i=i: e.dma_start(
                out=k_sb[:, :, (i % RING) * 128:((i % RING) + 1) * 128],
                in_=k_h[:, i * 128:(i + 1) * 128].rearrange(
                    "(h p) t -> p h t", p=64)), waits=wr, incs=[(ld, 16)])
            rec.add("gpsimd", lambda e, i=i: e.dma_start(
                out=v_sb[:, i % RING, :], in_=v_h[i * 128:(i + 1) * 128, :]),
                waits=wr, incs=[(gld, 16)])
            lv = ld.inc(32)
            lvg = gld.inc(16)
            for h in range(16):
                hp = h % 2
                pe = ps_e[hp]
            
                wm = [(ld, lv), (gld, lvg)]
                if (ti, h - 2) in exp_done:
                    wm.append((sd, exp_done[(ti, h - 2)]))
                elif ti > 0 and (ti - 1, 14 + h) in exp_done:
                    wm.append((sd, exp_done[(ti - 1, 14 + h)]))
                for kidx in range(8):
                    rs = blocks[kidx] % RING
                    rec.add("tensor", lambda e, pe=pe, kidx=kidx, rs=rs, ks=ks, qs=qs, h=h: e.matmul(
                        pe[:, kidx * 128:(kidx + 1) * 128],
                        ks[:, h, rs * 128:(rs + 1) * 128],
                        qs[:, h, :], start=True, stop=True),
                        waits=wm if kidx == 0 else [], incs=[(td, 1)] if kidx == 7 else [])
                t_mm = td.inc()
                if h == 15:
                    tile_mm_done[ti] = t_mm
                Eb = E_sb[hp]
                we = [(td, t_mm)]
                if dbg_barrier[0]:
                    we.append((st, dbg_barrier[0]))
                if (ti, h - 2) in g_done:
                    we.append((vd, g_done[(ti, h - 2)]))
                elif ti > 0 and (ti - 1, 14 + h) in g_done:
                    we.append((vd, g_done[(ti - 1, 14 + h)]))
                rec.add("scalar", lambda e, pe=pe, Eb=Eb: e.activation(
                    Eb[:], pe[:], AF.Exp), waits=we, incs=[(sd, 1)])
                exp_done[(ti, h)] = sd.inc()
                Gb = G_sb[hp]
                wg = [(sd, exp_done[(ti, h)])]
                if (ti, h - 2) in tile_pv_done:
                    wg.append((td, tile_pv_done[(ti, h - 2)]))
                elif ti > 0 and (ti - 1, 14 + h) in tile_pv_done:
                    wg.append((td, tile_pv_done[(ti - 1, 14 + h)]))
                gops = []
                gops.append((lambda e, Eb=Eb, Gb=Gb, h=h: e.tensor_tensor(
                    Gb[:, 0:384], Eb[:, 0:384], bnear[:, h, :], ALU.mult), wg))
                for kidx in range(3):
                    bu = blocks[kidx]
                    if bu < HALO // 128:
                        gops.append((lambda e, Gb=Gb, kidx=kidx, bu=bu: e.tensor_scalar(
                            Gb[:, kidx * 128:(kidx + 1) * 128],
                            Gb[:, kidx * 128:(kidx + 1) * 128],
                            mask_sb[:, bu:bu + 1], None, ALU.mult), []))
                for f, kidx in enumerate(range(3, 8)):
                    bu = blocks[kidx]
                    if bu < HALO // 128:
                        rec.add("vector", lambda e, f=f, h=h, bu=bu: e.tensor_scalar(
                            tmpc[:, f:f + 1], pbf[:, h, f:f + 1], mask_sb[:, bu:bu + 1],
                            None, ALU.mult), waits=wg, incs=[(vd, 1)])
                        vv = vd.inc()
                        gops.append((lambda e, Gb=Gb, Eb=Eb, kidx=kidx, f=f: e.scalar_tensor_tensor(
                            Gb[:, kidx * 128:(kidx + 1) * 128],
                            Eb[:, kidx * 128:(kidx + 1) * 128],
                            tmpc[:, f:f + 1], idn[:], ALU.mult, ALU.mult),
                            [(vd, vv)]))
                    else:
                        gops.append((lambda e, Gb=Gb, Eb=Eb, kidx=kidx, f=f, h=h: e.scalar_tensor_tensor(
                            Gb[:, kidx * 128:(kidx + 1) * 128],
                            Eb[:, kidx * 128:(kidx + 1) * 128],
                            pbf[:, h, f:f + 1], idn[:], ALU.mult, ALU.mult), []))
                for oi_, (fn_, w_) in enumerate(gops):
                    last = oi_ == len(gops) - 1
                    rec.add("vector", fn_, waits=w_, incs=[(vd, 1)] if last else [])
                g_done[(ti, h)] = vd.inc()
                if dbg_tile is not None and i == dbg_tile and h == dbg_head:
                    scrS = ctx.sbuf("scrS", [128, 1024], F32)
                    rec.add("vector", lambda e, pe=pe: e.tensor_copy(scrS[:], pe[:]),
                            waits=[(td, t_mm)], incs=[(vd, 1)])
                    v_ps = vd.inc()
                    rec.add("sync", lambda e: e.dma_start(out=dbgS_h, in_=scrS[:]),
                            waits=[(vd, v_ps)], incs=[(st, 16)])
                    st.inc(16)
                    rec.add("sync", lambda e, qs=qs: e.dma_start(out=dbgq_h, in_=qs[:]),
                            waits=[(td, t_mm)], incs=[(st, 16)])
                    st.inc(16)
                    rec.add("sync", lambda e, ks=ks: e.dma_start(out=dbgk_h, in_=ks[:]),
                            waits=[(td, t_mm)], incs=[(st, 16)])
                    st.inc(16)
                    rec.add("sync", lambda e, Eb=Eb: e.dma_start(out=dbgE_h, in_=Eb[:]),
                            waits=[(vd, g_done[(ti, h)])], incs=[(st, 16)])
                    st.inc(16)
                    rec.add("sync", lambda e, Gb=Gb: e.dma_start(out=dbgG_h, in_=Gb[:]),
                            waits=[(vd, g_done[(ti, h)])], incs=[(st, 16)])
                    st.inc(16)
                    rec.add("sync", lambda e: e.dma_start(out=dbgB_h, in_=bnear[:]),
                            incs=[(st, 16)])
                    st.inc(16)
                    dbg_barrier[0] = st.n
                def emit_pv(hh, Gbb):
                    po = o_ps[hh % 2]
                    wp = [(vd, g_done[(ti, hh)])]
                    if (ti, hh - 2) in oev_done:
                        wp.append((ev, oev_done[(ti, hh - 2)]))
                    elif ti > 0 and (ti - 1, 14 + hh) in oev_done:
                        wp.append((ev, oev_done[(ti - 1, 14 + hh)]))
                    for kidx in range(8):
                        rs = blocks[kidx] % RING
                        rec.add("tensor", lambda e, po=po, Gbb=Gbb, kidx=kidx, rs=rs, vs=vs, hh=hh: e.matmul(
                            po[:, 0:65], Gbb[:, kidx * 128:(kidx + 1) * 128],
                            vs[:, rs, 65 * hh:65 * hh + 65],
                            start=(kidx == 0), stop=(kidx == 7)),
                            waits=wp if kidx == 0 else [], incs=[(td, 1)] if kidx == 7 else [])
                    tile_pv_done[(ti, hh)] = td.inc()
                    if hh == 15:
                        tile_pv_done[ti] = tile_pv_done[(ti, hh)]
                    rec.add("vector", lambda e, po=po, hh=hh: e.tensor_copy(
                        o_stage[:, hh, :], po[:, 0:65]),
                        waits=[(td, tile_pv_done[(ti, hh)])], incs=[(ev, 1)])
                    oev_done[(ti, hh)] = ev.inc()

                if h >= 1:
                    emit_pv(h - 1, G_sb[(h - 1) % 2])
                if h == 15:
                    emit_pv(15, Gb)
            if dbg_tile is not None and i == dbg_tile:
                rec.add("sync", lambda e: e.dma_start(
                    out=dbgO_h, in_=o_stage[:].rearrange("p h e -> p (h e)")),
                    waits=[(ev, oev_done[(ti, 15)])], incs=[(st, 16)])
                st.inc(16)
                dbg_barrier[0] = st.n
            fb = flat_sb[p]
            wn = [(ev, oev_done[(ti, 15)])]
            if ti >= 2:
                wn.append((st, flat_st[ti - 2]))
            rec.add("vector", lambda e: e.tensor_scalar(
                zcol[:, 0:16], o_stage[:, :, 64], ctx.c("tiny"), None, ALU.max),
                waits=wn, incs=[(vd, 1)])
            v_z = vd.inc()
            rec.add("vector", lambda e: e.reciprocal(
                zcol[:, 16:32], zcol[:, 0:16]), waits=[(vd, v_z)], incs=[(vd, 1)])
            v_rz = vd.inc()
            for h in range(16):
                rec.add("vector", lambda e, fb=fb, h=h: e.tensor_scalar(
                    fb[:, h * 64:(h + 1) * 64], o_stage[:, h, 0:64],
                    zcol[:, 16 + h:17 + h], None, ALU.mult),
                    waits=[(vd, v_rz)] if h == 0 else [],
                    incs=[(vd, 1)] if h == 15 else [])
            v_fl = vd.inc()
            rec.add("sync", lambda e, i=i, fb=fb: e.dma_start(
                out=flat_h[i * 128:(i + 1) * 128, :], in_=fb[:]),
                waits=[(vd, v_fl)], incs=[(st, 16)])
            flat_st[ti] = st.inc(16)
        rec.add("sync", lambda e: e.nop(), waits=[(st, st.n)])

def build_slot_attn_phase(dbg_tile=None, dbg_head=0):
    ctx = Ctx("emit_slot_attn")
    nc = ctx.nc
    nc = ctx.nc
    q_h = nc.dram_tensor("q_fm", [D, TC], BF16, kind="ExternalInput").ap()
    k_h = nc.dram_tensor("k_fm", [D, TC], BF16, kind="ExternalInput").ap()
    v_h = nc.dram_tensor("v65", [TC, SLOT_H * 65], BF16, kind="ExternalInput").ap()
    pb_h = nc.dram_tensor("pos_bias", [44, 16], F32, kind="ExternalInput").ap()
    mc_h = nc.dram_tensor("mconst", [44, NPOS], BF16, kind="ExternalInput").ap()
    msk_h = nc.dram_tensor("kvmask", [TC], F32, kind="ExternalInput").ap()
    flat_h = nc.dram_tensor("flat", [TC, D], BF16, kind="ExternalOutput").ap()
    ball_h = nc.dram_tensor("ball_scratch", [16, NPOS], F32, kind="Internal").ap()
    if dbg_tile is not None:
        dbgS_h = nc.dram_tensor("dbgS", [128, 1024], F32, kind="ExternalOutput").ap()
        dbgq_h = nc.dram_tensor("dbgq", [64, 16, 128], BF16, kind="ExternalOutput").ap()
        dbgk_h = nc.dram_tensor("dbgk", [64, 16, 1024], BF16, kind="ExternalOutput").ap()
        dbgE_h = nc.dram_tensor("dbgE", [128, 1024], BF16, kind="ExternalOutput").ap()
        dbgG_h = nc.dram_tensor("dbgG", [128, 1024], BF16, kind="ExternalOutput").ap()
        dbgO_h = nc.dram_tensor("dbgO", [128, SLOT_H * 65], F32, kind="ExternalOutput").ap()
        dbgB_h = nc.dram_tensor("dbgB", [128, 16, 384], BF16, kind="ExternalOutput").ap()

    io = {
        "q_fm": q_h,
        "k_fm": k_h,
        "v65": v_h,
        "pos_bias": pb_h,
        "mconst": mc_h,
        "kvmask": msk_h,
        "flat": flat_h,
        "ball_scratch": ball_h,
    }
    if dbg_tile is not None:
        io.update({"dbgS": dbgS_h, "dbgq": dbgq_h, "dbgk": dbgk_h,
                   "dbgE": dbgE_h, "dbgG": dbgG_h, "dbgO": dbgO_h, "dbgB": dbgB_h})
    with ctx.es:
        emit_slot_attn(ctx, io, dbg_tile=dbg_tile, dbg_head=dbg_head)
        finish(ctx.nc, ctx.rec, ctx.es, ctx)
    return ctx.nc


def emit_gate(ctx, io):
    nc = ctx.nc
    rec = ctx.rec
    xn_fm_h = io["xn_fm"]
    flat_h = io["flat"]
    wg_h = io["Wgate"]
    bg_h = io["bgate"]
    wo_h = io["Wout"]
    bo_h = io["bout"]
    s_h = io["s_tm"]
    if True:
        ctx.setup_consts()
        idn = build_identity_bf16(ctx, io.get("ident"))
        rec = ctx.rec
        ld = ctx.sem("ld"); st = ctx.sem("st"); td = ctx.sem("td")
        vd = ctx.sem("vd"); sd = ctx.sem("sd"); ev = ctx.sem("ev")

        wg = ctx.sbuf("wg", [128, 8, D], BF16)
        wo = ctx.sbuf("wo", [128, 8, D], BF16)
        bgrow = ctx.sbuf("bgrow", [1, D], F32)
        borow = ctx.sbuf("borow", [1, D], F32)
        ones_row = ctx.sbuf("ones_row", [1, 128], F32)
        rec.add("gpsimd", lambda e: e.dma_start(
            out=wg[:], in_=wg_h.rearrange("(c p) e -> p c e", p=128)), incs=[(ld, 16)])
        rec.add("gpsimd", lambda e: e.dma_start(
            out=wo[:], in_=wo_h.rearrange("(c p) e -> p c e", p=128)), incs=[(ld, 16)])
        rec.add("sync", lambda e: e.dma_start(
            out=bgrow[:], in_=bg_h.rearrange("(o e) -> o e", o=1)), incs=[(ld, 16)])
        rec.add("sync", lambda e: e.dma_start(
            out=borow[:], in_=bo_h.rearrange("(o e) -> o e", o=1)), incs=[(ld, 16)])
        rec.add("vector", lambda e: e.memset(ones_row[:], 1.0), incs=[(vd, 1)])
        prep_ld = ld.inc(64)
        prep_vd = vd.inc()

        xn_sb = [ctx.sbuf(f"xn{i}", [128, 8, 128], BF16) for i in range(2)]
        fl_sb = [ctx.sbuf(f"fl{i}", [128, D], BF16) for i in range(2)]
        gt_sb = [ctx.sbuf(f"gt{i}", [128, D], BF16) for i in range(2)]
        fg_sb = [ctx.sbuf(f"fg{i}", [128, 8, 128], BF16) for i in range(2)]
        s_sb = [ctx.sbuf(f"ssb{i}", [128, D], BF16) for i in range(2)]
        gps = [ctx.psum(f"gps{i}", [128, 512]) for i in range(2)]
        tps = ctx.psum("tps", [128, 1024], BF16)
        sps = [ctx.psum(f"sps{i}", [128, 512]) for i in range(2)]

        tiles = list(range(HALO // 128, TC // 128))
        trk = {}
        for ti, i in enumerate(tiles):
            p = ti % 2
            xs, fl, gt, fg, ss = xn_sb[p], fl_sb[p], gt_sb[p], fg_sb[p], s_sb[p]
            w = [(ld, prep_ld)] if ti == 0 else []
            if ti >= 2:
                w = [(td, trk[("smm", ti - 2)])]
            rec.add("sync", lambda e, i=i, xs=xs: e.dma_start(
                out=xs[:], in_=xn_fm_h[:, i * 128:(i + 1) * 128].rearrange(
                    "(c p) t -> p c t", p=128)), waits=w, incs=[(ld, 16)])
            rec.add("sync", lambda e, i=i, fl=fl: e.dma_start(
                out=fl[:], in_=flat_h[i * 128:(i + 1) * 128, :]), incs=[(ld, 16)])
            lv = ld.inc(32)
            for half in range(2):
                pg = gps[half]
                wm = [(ld, lv)]
                if ti >= 1:
                    wm.append((sd, trk[("sig", ti - 1, half)]))
                for c in range(8):
                    rec.add("tensor", lambda e, pg=pg, xs=xs, c=c, half=half: e.matmul(
                        pg[:], xs[:, c, :], wg[:, c, half * 512:(half + 1) * 512],
                        start=(c == 0), stop=False), waits=wm if c == 0 else [])
                rec.add("tensor", lambda e, pg=pg, half=half: e.matmul(
                    pg[:], ones_row[:], bgrow[:, half * 512:(half + 1) * 512],
                    start=False, stop=True), incs=[(td, 1)])
                tv = td.inc()
                rec.add("scalar", lambda e, pg=pg, gt=gt, half=half: e.activation(
                    gt[:, half * 512:(half + 1) * 512], pg[:], AF.Sigmoid),
                    waits=[(td, tv)], incs=[(sd, 1)])
                trk[("sig", ti, half)] = sd.inc()
            wfg = [(sd, trk[("sig", ti, 1)])]
            if ti >= 1:
                wfg.append((td, trk[("tp", ti - 1)]))
            rec.add("vector", lambda e, fl=fl, gt=gt, fg=fg: e.tensor_tensor(
                fg[:].rearrange("p c t -> p (c t)"), fl[:], gt[:], ALU.mult),
                waits=wfg, incs=[(vd, 1)])
            v_fg = vd.inc()
            wtr = [(vd, v_fg)]
            if ti >= 1:
                wtr.append((ev, trk[("fgev", ti - 1)]))
            for c in range(8):
                rec.add("tensor", lambda e, fg=fg, c=c: e.transpose(
                    tps[:, c * 128:(c + 1) * 128], fg[:, c, :], idn[:]),
                    waits=wtr if c == 0 else [], incs=[(td, 1)] if c == 7 else [])
            trk[("tp", ti)] = td.inc()
            rec.add("vector", lambda e, fg=fg: e.tensor_copy(
                fg[:].rearrange("p c t -> p (c t)"), tps[:]),
                waits=[(td, trk[("tp", ti)])], incs=[(ev, 1)])
            trk[("fgev", ti)] = ev.inc()
            for half in range(2):
                pq = sps[half]
                wm = [(ev, trk[("fgev", ti)])]
                if ti >= 1:
                    wm.append((ev, trk[("sev", ti - 1, half)]))
                for c in range(8):
                    rec.add("tensor", lambda e, pq=pq, fg=fg, c=c, half=half: e.matmul(
                        pq[:], fg[:, c, :], wo[:, c, half * 512:(half + 1) * 512],
                        start=(c == 0), stop=False), waits=wm if c == 0 else [])
                rec.add("tensor", lambda e, pq=pq, half=half: e.matmul(
                    pq[:], ones_row[:], borow[:, half * 512:(half + 1) * 512],
                    start=False, stop=True), incs=[(td, 1)])
                tvs = td.inc()
                if half == 1:
                    trk[("smm", ti)] = tvs
                wv = [(td, tvs)]
                if ti >= 2:
                    wv.append((st, trk[("sst", ti - 2)]))
                rec.add("vector", lambda e, pq=pq, ss=ss, half=half: e.tensor_copy(
                    ss[:, half * 512:(half + 1) * 512], pq[:]),
                    waits=wv, incs=[(ev, 1)])
                trk[("sev", ti, half)] = ev.inc()
            rec.add("sync", lambda e, i=i, ss=ss: e.dma_start(
                out=s_h[i * 128:(i + 1) * 128, :], in_=ss[:]),
                waits=[(ev, trk[("sev", ti, 1)])], incs=[(st, 16)])
            trk[("sst", ti)] = st.inc(16)
        rec.add("sync", lambda e: e.nop(), waits=[(st, st.n)])

def build_gate_phase():
    ctx = Ctx("emit_gate")
    nc = ctx.nc
    nc = ctx.nc
    xn_fm_h = nc.dram_tensor("xn_fm", [D, TC], BF16, kind="ExternalInput").ap()
    flat_h = nc.dram_tensor("flat", [TC, D], BF16, kind="ExternalInput").ap()
    wg_h = nc.dram_tensor("Wgate", [D, D], F32, kind="ExternalInput").ap()
    bg_h = nc.dram_tensor("bgate", [D], F32, kind="ExternalInput").ap()
    wo_h = nc.dram_tensor("Wout", [D, D], F32, kind="ExternalInput").ap()
    bo_h = nc.dram_tensor("bout", [D], F32, kind="ExternalInput").ap()
    s_h = nc.dram_tensor("s_tm", [TC, D], BF16, kind="ExternalOutput").ap()

    io = {
        "xn_fm": xn_fm_h,
        "flat": flat_h,
        "Wgate": wg_h,
        "bgate": bg_h,
        "Wout": wo_h,
        "bout": bo_h,
        "s_tm": s_h,
    }
    with ctx.es:
        emit_gate(ctx, io)
        finish(ctx.nc, ctx.rec, ctx.es, ctx)
    return ctx.nc


def make_rotary_tables(t0):
    t = np.arange(t0 - HALO, t0 + T_OWN, dtype=np.float64)
    half = np.arange(0, 64, 2, dtype=np.float64)
    inv_freq = 1.0 / 10000.0 ** (half / 64.0)
    fr = t[:, None] * inv_freq[None, :]
    fr = np.concatenate([fr, fr], -1)
    base = (half + 0.4 * 64) / (1.4 * 64)
    power = (t - N // 2) / float(WIN // 2)
    scale = base[None, :] ** power[:, None]
    scale = np.concatenate([scale, scale], -1)
    sign = np.concatenate([-np.ones(32), np.ones(32)])[None, :]
    cq = (np.cos(fr) * scale * 8.0).astype(np.float32)
    sq = (np.sin(fr) * scale * 8.0 * sign).astype(np.float32)
    ck = (np.cos(fr) / scale).astype(np.float32)
    sk = (np.sin(fr) / scale * sign).astype(np.float32)
    return cq, sq, ck, sk


FQ_T0 = 4
FA_T0 = 8


def emit_ffn_qkv(ctx, io, t0tile=FQ_T0):
    nc = ctx.nc
    rec = ctx.rec
    tab_h = {n: io[n] for n in ("cq", "sq", "ck", "sk")}
    xin_h = io["xin_fm"]
    w_h = io["wqkv"]
    qs_h = io["q_scale"]
    ks_h = io["k_scale"]
    q_out = io["q_fm"]
    k_out = io["k_fm"]
    v_out = io["v65"]
    if True:
        ctx.setup_consts()
        idn = build_identity_bf16(ctx, io.get("ident"))
        rec = ctx.rec
        ld = ctx.sem("ld"); gld = ctx.sem("gld"); st = ctx.sem("st")
        td = ctx.sem("td"); vd = ctx.sem("vd"); sd = ctx.sem("sd"); ev = ctx.sem("ev")

        wq = ctx.sbuf("wq", [128, 8, 1536], BF16)
        scl = ctx.sbuf("scl", [128, 4, 64], F32)
        rec.add("gpsimd", lambda e: e.dma_start(
            out=wq[:], in_=w_h.rearrange("(c p) e -> p c e", p=128)), incs=[(gld, 16)])
        for j, hh in enumerate((qs_h, ks_h)):
            rec.add("sync", lambda e, j=j, hh=hh: e.dma_start(
                out=scl[:, 2 * j, :], in_=hh.unsqueeze(0).broadcast_to([128, 64])),
                incs=[(ld, 16)])
            rec.add("sync", lambda e, j=j, hh=hh: e.dma_start(
                out=scl[:, 2 * j + 1, 0:32],
                in_=hh[32:64].unsqueeze(0).broadcast_to([128, 32])), incs=[(ld, 16)])
            rec.add("sync", lambda e, j=j, hh=hh: e.dma_start(
                out=scl[:, 2 * j + 1, 32:64],
                in_=hh[0:32].unsqueeze(0).broadcast_to([128, 32])), incs=[(ld, 16)])
        prep_ld = ld.inc(16 * 6)
        w_gld = gld.inc(16)

        xs = [ctx.sbuf(f"xs{i}", [128, 8, 128], BF16) for i in range(2)]
        tabs = [ctx.sbuf(f"tab{i}", [128, 4, 64], F32) for i in range(2)]
        fold = ctx.sbuf("fold", [128, 4, 64], F32)
        qkt = [ctx.sbuf(f"qkt{i}", [128, 2, 512], BF16) for i in range(2)]
        v_sb = [ctx.sbuf(f"vsb{i}", [128, FFN_H, 65], BF16) for i in range(2)]
        rns = ctx.sbuf("rns", [128, 2, 8], F32)
        scr = ctx.sbuf("scr", [128, 512], F32)
        tmp1 = ctx.sbuf("tmp1", [128, 512], F32)
        tmp2 = ctx.sbuf("tmp2", [128, 512], F32)
        fmst = [ctx.sbuf(f"fmst{i}", [128, 512], BF16) for i in range(2)]
        ps = [ctx.psum(f"ps{i}", [128, 512]) for i in range(3)]
        tp = ctx.psum("tp", [128, 512], BF16)

        trk = {}
        tiles = list(range(t0tile, TC // 128))
        for ti, i in enumerate(tiles):
            p = ti % 2
            x_sb, tb, qk, vs = xs[p], tabs[p], qkt[p], v_sb[p]
            w = [(ld, prep_ld)] if ti == 0 else []
            if ti >= 2:
                w = [(td, trk[("mmv", ti - 2)]), (vd, trk[("fold", ti - 2)])]
            rec.add("sync", lambda e, i=i, x_sb=x_sb: e.dma_start(
                out=x_sb[:, 0:4, :],
                in_=xin_h[0:512, i * 128:(i + 1) * 128].rearrange(
                    "(c p) t -> p c t", p=128)), waits=w, incs=[(ld, 16)])
            rec.add("sync", lambda e, i=i, x_sb=x_sb: e.dma_start(
                out=x_sb[:, 4:8, :],
                in_=xin_h[512:1024, i * 128 - 1:(i + 1) * 128 - 1].rearrange(
                    "(c p) t -> p c t", p=128)), incs=[(ld, 16)])
            for j, nmn in enumerate(("cq", "sq", "ck", "sk")):
                rec.add("sync", lambda e, i=i, j=j, hh=tab_h[nmn], tb=tb: e.dma_start(
                    out=tb[:, j, :], in_=hh[i * 128:(i + 1) * 128, :]), incs=[(ld, 16)])
            lv = ld.inc(16 * 6)
            wf = [(ld, lv)]
            if ti >= 1:
                wf.append((vd, trk[("rotdone", ti - 1)]))
            for j in range(4):
                rec.add("vector", lambda e, j=j, tb=tb: e.tensor_tensor(
                    fold[:, j, :], tb[:, j, :], scl[:, j, :], ALU.mult),
                    waits=wf if j == 0 else [], incs=[(vd, 1)] if j == 3 else [])
            trk[("fold", ti)] = vd.inc()
            for part in range(3):
                pp = ps[part]
                wm = [(ld, lv), (gld, w_gld)]
                if ti >= 1:
                    wm.append(trk[("pfree", ti - 1, part)])
                for c in range(8):
                    rec.add("tensor", lambda e, pp=pp, x_sb=x_sb, c=c, part=part: e.matmul(
                        pp[:], x_sb[:, c, :], wq[:, c, part * 512:(part + 1) * 512],
                        start=(c == 0), stop=(c == 7)),
                        waits=wm if c == 0 else [], incs=[(td, 1)] if c == 7 else [])
                tv = td.inc()
                if part == 2:
                    trk[("mmv", ti)] = tv
                    if ti < 2:
                        rec.add("vector", lambda e, vs=vs: e.memset(vs[:, :, 64:65], 1.0))
                    wv = [(td, tv)]
                    if ti >= 2:
                        wv.append((st, trk[("vst", ti - 2)]))
                    rec.add("vector", lambda e, pp=pp, vs=vs: e.tensor_copy(
                        vs[:, :, 0:64], pp[:].rearrange("p (h d) -> p h d", d=64)),
                        waits=wv, incs=[(ev, 1)])
                    evv = ev.inc()
                    trk[("pfree", ti, part)] = (ev, evv)
                    rec.add("sync", lambda e, i=i, vs=vs: e.dma_start(
                        out=v_out[i * 128:(i + 1) * 128, :],
                        in_=vs[:].rearrange("p h e -> p (h e)")),
                        waits=[(ev, evv)], incs=[(st, 16)])
                    trk[("vst", ti)] = st.inc(16)
                    continue
                wv = [(td, tv)]
                rec.add("scalar", lambda e, pp=pp: e.activation(
                    scr[:], pp[:], AF.Square), waits=wv, incs=[(sd, 1)])
                s_sq = sd.inc()
                rec.add("vector", lambda e, part=part: e.reduce_sum(
                    rns[:, part, :], scr[:].rearrange("p (h d) -> p h d", d=64),
                    AX.X), waits=[(sd, s_sq)])
                rec.add("vector", lambda e, part=part: e.tensor_scalar(
                    rns[:, part, :], rns[:, part, :], ctx.c("clip2"), None,
                    ALU.max), incs=[(vd, 1)])
                v_ss = vd.inc()
                rec.add("scalar", lambda e, part=part: e.activation(
                    rns[:, part, :], rns[:, part, :], AF.Sqrt),
                    waits=[(vd, v_ss)], incs=[(sd, 1)])
                s_rt = sd.inc()
                rec.add("vector", lambda e, part=part: e.reciprocal(
                    rns[:, part, :], rns[:, part, :]), waits=[(sd, s_rt)], incs=[(vd, 1)])
                v_rn = vd.inc()
                rec.add("vector", lambda e, pp=pp, part=part: e.tensor_tensor(
                    tmp1[:].rearrange("p (h d) -> p h d", d=64),
                    pp[:].rearrange("p (h d) -> p h d", d=64),
                    rns[:, part, :].unsqueeze(2).broadcast_to([128, 8, 64]),
                    ALU.mult), waits=[(vd, v_rn)], incs=[(vd, 1)])
                trk[("pfree", ti, part)] = (vd, vd.inc())
                so = 2 * part + 1
                for hside in range(2):
                    dst = (slice(0, 32), slice(32, 64))[hside]
                    srcs = (slice(32, 64), slice(0, 32))[hside]
                    rec.add("vector", lambda e, dst=dst, srcs=srcs, so=so: e.tensor_tensor(
                        tmp2[:].rearrange("p (h d) -> p h d", d=64)[:, :, dst],
                        tmp1[:].rearrange("p (h d) -> p h d", d=64)[:, :, srcs],
                        fold[:, so, dst].unsqueeze(1).broadcast_to([128, 8, 32]),
                        ALU.mult))
                rec.add("vector", lambda e, so=so: e.tensor_tensor(
                    scr[:].rearrange("p (h d) -> p h d", d=64),
                    tmp1[:].rearrange("p (h d) -> p h d", d=64),
                    fold[:, so - 1, :].unsqueeze(1).broadcast_to([128, 8, 64]),
                    ALU.mult))
                wv2 = []
                if ti >= 2:
                    wv2.append((td, trk[("tpd", ti - 2, part)]))
                rec.add("vector", lambda e, qk=qk, part=part: e.tensor_tensor(
                    qk[:, part, :], scr[:], tmp2[:], ALU.add),
                    waits=wv2, incs=[(vd, 1)])
                v_rot = vd.inc()
                if part == 1:
                    trk[("rotdone", ti)] = v_rot
                wt = [(vd, v_rot)]
                if (ti, part) != (0, 0):
                    prevkey = (ti, part - 1) if part == 1 else (ti - 1, 1)
                    wt.append((ev, trk[("fmev", prevkey)]))
                for c in range(4):
                    rec.add("tensor", lambda e, qk=qk, part=part, c=c: e.transpose(
                        tp[:, c * 128:(c + 1) * 128], qk[:, part, c * 128:(c + 1) * 128],
                        idn[:]), waits=wt if c == 0 else [],
                        incs=[(td, 1)] if c == 3 else [])
                trk[("tpd", ti, part)] = td.inc()
                fs = fmst[part % 2]
                wv3 = [(td, trk[("tpd", ti, part)])]
                if ti >= 1:
                    wv3.append((st, trk[("fmst", ti - 1, part)]))
                rec.add("vector", lambda e, fs=fs: e.tensor_copy(fs[:], tp[:]),
                        waits=wv3, incs=[(ev, 1)])
                trk[("fmev", (ti, part))] = ev.inc()
                dst_h = (q_out, k_out)[part]
                rec.add("sync", lambda e, dst_h=dst_h, i=i, fs=fs: e.dma_start(
                    out=dst_h[:, i * 128:(i + 1) * 128].rearrange(
                        "(c f) t -> f c t", f=128),
                    in_=fs[:].rearrange("f (c t) -> f c t", t=128)),
                    waits=[(ev, trk[("fmev", (ti, part))])], incs=[(st, 16)])
                trk[("fmst", ti, part)] = st.inc(16)
        rec.add("sync", lambda e: e.nop(), waits=[(st, st.n)])

def build_ffn_qkv_phase():
    ctx = Ctx("emit_ffn_qkv")
    nc = ctx.nc
    nc = ctx.nc
    xin_h = nc.dram_tensor("xin_fm", [D, TC], BF16, kind="ExternalInput").ap()
    w_h = nc.dram_tensor("wqkv", [D, 1536], F32, kind="ExternalInput").ap()
    qs_h = nc.dram_tensor("q_scale", [64], F32, kind="ExternalInput").ap()
    ks_h = nc.dram_tensor("k_scale", [64], F32, kind="ExternalInput").ap()
    tab_h = {n: nc.dram_tensor(n, [TC, 64], F32, kind="ExternalInput").ap()
             for n in ("cq", "sq", "ck", "sk")}
    q_out = nc.dram_tensor("q_fm", [512, TC], BF16, kind="ExternalOutput").ap()
    k_out = nc.dram_tensor("k_fm", [512, TC], BF16, kind="ExternalOutput").ap()
    v_out = nc.dram_tensor("v65", [TC, FFN_H * 65], BF16, kind="ExternalOutput").ap()

    io = {
        "xin_fm": xin_h,
        "wqkv": w_h,
        "q_scale": qs_h,
        "k_scale": ks_h,
        "q_fm": q_out,
        "k_fm": k_out,
        "v65": v_out,
        **tab_h,
    }
    with ctx.es:
        emit_ffn_qkv(ctx, io)
        finish(ctx.nc, ctx.rec, ctx.es, ctx)
    return ctx.nc


def make_w01():
    j = np.arange(128)[:, None]
    c = np.arange(128)[None, :]
    a = (c <= j).astype(np.float32)
    b = (c >= j).astype(np.float32)
    return a.astype(BF), b.astype(BF)


def emit_ffn_attn(ctx, io, t0tile=FA_T0, dbg_tile=None, dbg_head=0):
    nc = ctx.nc
    rec = ctx.rec
    q_h = io["q_fm"]
    k_h = io["k_fm"]
    v_h = io["v65"]
    w01a_h = io["w01a"]
    w01b_h = io["w01b"]
    msk_h = io["kvmask"]
    o_h = io["o_tm"]
    dbgE_h = io.get("dbgE")
    dbgG_h = io.get("dbgG")
    dbgO_h = io.get("dbgO")
    if True:
        ctx.setup_consts()
        rec = ctx.rec
        ld = ctx.sem("ld"); gld = ctx.sem("gld"); st = ctx.sem("st")
        td = ctx.sem("td"); vd = ctx.sem("vd"); sd = ctx.sem("sd"); ev = ctx.sem("ev")

        w01a = ctx.sbuf("w01a_sb", [128, 128], BF16)
        w01b = ctx.sbuf("w01b_sb", [128, 128], BF16)
        mask_sb = ctx.sbuf("masks", [128, TC // 128], F32)
        rec.add("sync", lambda e: e.dma_start(out=w01a[:], in_=w01a_h), incs=[(ld, 16)])
        rec.add("sync", lambda e: e.dma_start(out=w01b[:], in_=w01b_h), incs=[(ld, 16)])
        rec.add("sync", lambda e: e.dma_start(
            out=mask_sb[:], in_=msk_h.rearrange("(i p) -> p i", p=128),
            allow_slow_non_contiguous=True), incs=[(ld, 16)])
        prep_ld = ld.inc(48)

        RINGF = 8
        q_sb = [ctx.sbuf(f"qsb{i}", [64, 8, 128], BF16) for i in range(2)]
        k_sb = ctx.sbuf("kringf", [64, 8, RINGF * 128], BF16)
        v_sb = ctx.sbuf("vringf", [128, RINGF, FFN_H * 65], BF16)
        E_sb = [ctx.sbuf(f"esb{i}", [128, 640], BF16) for i in range(2)]
        G_sb = [ctx.sbuf(f"gsb{i}", [128, 640], BF16) for i in range(2)]
        o_stage = ctx.sbuf("ostg", [128, FFN_H, 65], F32)
        zcol = ctx.sbuf("zcol", [128, 2 * FFN_H], F32)
        o_tm = [ctx.sbuf(f"otm{i}", [128, 512], BF16) for i in range(2)]
        ps_e = [ctx.psum(f"pse{i}", [128, 1024]) for i in range(2)]
        o_ps = [ctx.psum(f"ops{i}", [128, 512]) for i in range(2)]

        tiles = list(range(t0tile, TC // 128))
        dbgbar = [0]
        exp_done = {}; g_done = {}; pv_done = {}; oev_done = {}; o_st = {}
        first_i = tiles[0]
        pre0 = first_i - 4
        assert (pre0 % RINGF) + 4 <= RINGF
        rec.add("sync", lambda e: e.dma_start(
            out=k_sb[:, :, (pre0 % RINGF) * 128:((pre0 % RINGF) + 4) * 128],
            in_=k_h[:, pre0 * 128:(pre0 + 4) * 128].rearrange(
                "(h p) t -> p h t", p=64)), incs=[(ld, 16)])
        ld.inc(16)
        rec.add("gpsimd", lambda e: e.dma_start(
            out=v_sb[:, pre0 % RINGF:(pre0 % RINGF) + 4, :],
            in_=v_h[pre0 * 128:(pre0 + 4) * 128, :].rearrange(
                "(b p) e -> p b e", p=128)), incs=[(gld, 16)])
        gld.inc(16)
        use_g_all = [None, None]
        for ti, i in enumerate(tiles):
            p = ti % 2
            qs, ks, vs = q_sb[p], k_sb, v_sb
            blocks = [i - 4 + j for j in range(5)]
            w = []
            if ti >= 2:
                w = [(td, pv_done[ti - 2])]
            rec.add("sync", lambda e, i=i, qs=qs: e.dma_start(
                out=qs[:], in_=q_h[:, i * 128:(i + 1) * 128].rearrange(
                    "(h p) t -> p h t", p=64)), waits=w, incs=[(ld, 16)])
            wr = []
            if ti - 3 >= 0:
                wr = [(td, pv_done[ti - 3])]
            rec.add("sync", lambda e, i=i: e.dma_start(
                out=k_sb[:, :, (i % RINGF) * 128:((i % RINGF) + 1) * 128],
                in_=k_h[:, i * 128:(i + 1) * 128].rearrange(
                    "(h p) t -> p h t", p=64)), waits=wr, incs=[(ld, 16)])
            rec.add("gpsimd", lambda e, i=i: e.dma_start(
                out=v_sb[:, i % RINGF, :], in_=v_h[i * 128:(i + 1) * 128, :]),
                waits=wr, incs=[(gld, 16)])
            lv = ld.inc(32)
            lvg = gld.inc(16)
            for h in range(FFN_H):
                hp = h % 2
                pe = ps_e[hp]
                wm = [(ld, lv), (gld, lvg)]
                key = (ti, h - 2) if h >= 2 else (ti - 1, FFN_H - 2 + h)
                if key in exp_done:
                    wm.append((sd, exp_done[key]))
                for j in range(5):
                    rs = blocks[j] % RINGF
                    rec.add("tensor", lambda e, pe=pe, j=j, rs=rs, ks=ks, qs=qs, h=h: e.matmul(
                        pe[:, j * 128:(j + 1) * 128], ks[:, h, rs * 128:(rs + 1) * 128],
                        qs[:, h, :], start=True, stop=True),
                        waits=wm if j == 0 else [], incs=[(td, 1)] if j == 4 else [])
                t_mm = td.inc()
                Eb = E_sb[hp]
                we = [(td, t_mm)]
                if key in g_done:
                    we.append((vd, g_done[key]))
                if key in pv_done:
                    we.append((td, pv_done[key]))
                if dbgbar[0]:
                    we.append((st, dbgbar[0]))
                rec.add("scalar", lambda e, pe=pe, Eb=Eb: e.activation(
                    Eb[:], pe[:, 0:640], AF.Exp), waits=we, incs=[(sd, 1)])
                exp_done[(ti, h)] = sd.inc()
                Gb = G_sb[hp]
                wg = [(sd, exp_done[(ti, h)])]
                if key in pv_done:
                    wg.append((td, pv_done[key]))
                use_g = [False] * 5
                gops = []
                gops.append((lambda e, Eb=Eb, Gb=Gb: e.tensor_tensor(
                    Gb[:, 0:128], Eb[:, 0:128], w01a[:], ALU.mult), wg))
                use_g[0] = True
                gops.append((lambda e, Eb=Eb, Gb=Gb: e.tensor_tensor(
                    Gb[:, 512:640], Eb[:, 512:640], w01b[:], ALU.mult), []))
                use_g[4] = True
                for j, bu in enumerate(blocks):
                    if bu < HALO // 128:
                        if j in (0, 4):
                            gops.append((lambda e, Gb=Gb, j=j, bu=bu: e.tensor_scalar(
                                Gb[:, j * 128:(j + 1) * 128],
                                Gb[:, j * 128:(j + 1) * 128],
                                mask_sb[:, bu:bu + 1], None, ALU.mult), []))
                        else:
                            gops.append((lambda e, Gb=Gb, Eb=Eb, j=j, bu=bu: e.tensor_scalar(
                                Gb[:, j * 128:(j + 1) * 128],
                                Eb[:, j * 128:(j + 1) * 128],
                                mask_sb[:, bu:bu + 1], None, ALU.mult), []))
                            use_g[j] = True
                for oi_, (fn_, w_) in enumerate(gops):
                    rec.add("vector", fn_, waits=w_,
                            incs=[(vd, 1)] if oi_ == len(gops) - 1 else [])
                g_done[(ti, h)] = vd.inc()
                use_g_all[h % 2] = list(use_g)

                def emit_pv(hh):
                    po = o_ps[hh % 2]
                    wp = [(vd, g_done[(ti, hh)])]
                    kk = (ti, hh - 2) if hh >= 2 else (ti - 1, FFN_H - 2 + hh)
                    if kk in oev_done:
                        wp.append((ev, oev_done[kk]))
                    for j in range(5):
                        opnd = G_sb[hh % 2] if use_g_all[hh % 2][j] else E_sb[hh % 2]
                        rs = blocks[j] % RINGF
                        rec.add("tensor", lambda e, po=po, opnd=opnd, j=j, rs=rs, hh=hh: e.matmul(
                            po[:, 0:65], opnd[:, j * 128:(j + 1) * 128],
                            v_sb[:, rs, 65 * hh:65 * hh + 65],
                            start=(j == 0), stop=(j == 4)),
                            waits=wp if j == 0 else [], incs=[(td, 1)] if j == 4 else [])
                    pv_done[(ti, hh)] = td.inc()
                    if hh == FFN_H - 1:
                        pv_done[ti] = pv_done[(ti, hh)]
                    rec.add("vector", lambda e, po=po, hh=hh: e.tensor_copy(
                        o_stage[:, hh, :], po[:, 0:65]),
                        waits=[(td, pv_done[(ti, hh)])], incs=[(ev, 1)])
                    oev_done[(ti, hh)] = ev.inc()

                if h >= 1:
                    emit_pv(h - 1)
                if h == FFN_H - 1:
                    emit_pv(h)
                if dbg_tile is not None and i == dbg_tile and h == dbg_head:
                    rec.add("sync", lambda e, Eb=Eb: e.dma_start(out=dbgE_h, in_=Eb[:]),
                            waits=[(vd, g_done[(ti, h)])], incs=[(st, 16)])
                    st.inc(16)
                    rec.add("sync", lambda e, Gb=Gb: e.dma_start(out=dbgG_h, in_=Gb[:]),
                            waits=[(vd, g_done[(ti, h)])], incs=[(st, 16)])
                    dbgbar[0] = st.inc(16)
            if dbg_tile is not None and i == dbg_tile:
                rec.add("sync", lambda e: e.dma_start(
                    out=dbgO_h, in_=o_stage[:].rearrange("p h e -> p (h e)")),
                    waits=[(ev, oev_done[(ti, FFN_H - 1)])], incs=[(st, 16)])
                dbgbar[0] = st.inc(16)
            ob = o_tm[p]
            wn = [(ev, oev_done[(ti, FFN_H - 1)])]
            if ti >= 2:
                wn.append((st, o_st[ti - 2]))
            rec.add("vector", lambda e: e.tensor_scalar(
                zcol[:, 0:8], o_stage[:, :, 64], ctx.c("tiny"), None, ALU.max),
                waits=wn, incs=[(vd, 1)])
            v_z = vd.inc()
            rec.add("vector", lambda e: e.reciprocal(
                zcol[:, 8:16], zcol[:, 0:8]), waits=[(vd, v_z)], incs=[(vd, 1)])
            v_rz = vd.inc()
            for h in range(FFN_H):
                rec.add("vector", lambda e, ob=ob, h=h: e.tensor_scalar(
                    ob[:, h * 64:(h + 1) * 64], o_stage[:, h, 0:64],
                    zcol[:, 8 + h:9 + h], None, ALU.mult),
                    waits=[(vd, v_rz)] if h == 0 else [],
                    incs=[(vd, 1)] if h == FFN_H - 1 else [])
            v_fl = vd.inc()
            rec.add("sync", lambda e, i=i, ob=ob: e.dma_start(
                out=o_h[i * 128:(i + 1) * 128, :], in_=ob[:]),
                waits=[(vd, v_fl)], incs=[(st, 16)])
            o_st[ti] = st.inc(16)
        rec.add("sync", lambda e: e.nop(), waits=[(st, st.n)])

def build_ffn_attn_phase(dbg_tile=None, dbg_head=0):
    ctx = Ctx("emit_ffn_attn")
    nc = ctx.nc
    nc = ctx.nc
    q_h = nc.dram_tensor("q_fm", [512, TC], BF16, kind="ExternalInput").ap()
    k_h = nc.dram_tensor("k_fm", [512, TC], BF16, kind="ExternalInput").ap()
    v_h = nc.dram_tensor("v65", [TC, FFN_H * 65], BF16, kind="ExternalInput").ap()
    w01a_h = nc.dram_tensor("w01a", [128, 128], BF16, kind="ExternalInput").ap()
    w01b_h = nc.dram_tensor("w01b", [128, 128], BF16, kind="ExternalInput").ap()
    msk_h = nc.dram_tensor("kvmask", [TC], F32, kind="ExternalInput").ap()
    o_h = nc.dram_tensor("o_tm", [TC, 512], BF16, kind="ExternalOutput").ap()
    if dbg_tile is not None:
        dbgE_h = nc.dram_tensor("dbgE", [128, 640], BF16, kind="ExternalOutput").ap()
        dbgG_h = nc.dram_tensor("dbgG", [128, 640], BF16, kind="ExternalOutput").ap()
        dbgO_h = nc.dram_tensor("dbgO", [128, FFN_H * 65], F32, kind="ExternalOutput").ap()

    io = {
        "q_fm": q_h,
        "k_fm": k_h,
        "v65": v_h,
        "w01a": w01a_h,
        "w01b": w01b_h,
        "kvmask": msk_h,
        "o_tm": o_h,
    }
    if dbg_tile is not None:
        io.update({"dbgE": dbgE_h, "dbgG": dbgG_h, "dbgO": dbgO_h})
    with ctx.es:
        emit_ffn_attn(ctx, io, dbg_tile=dbg_tile, dbg_head=dbg_head)
        finish(ctx.nc, ctx.rec, ctx.es, ctx)
    return ctx.nc


def emit_ffn_wo(ctx, io, t0tile=FA_T0):
    nc = ctx.nc
    rec = ctx.rec
    o_h = io["o_tm"]
    wo_h = io["wo"]
    xr_h = io["xres_tm"]
    msk_h = io["kvmask"]
    x1_h = io["x1_tm"]
    if True:
        ctx.setup_consts()
        idn = build_identity_bf16(ctx, io.get("ident"))
        rec = ctx.rec
        ld = ctx.sem("ld"); gld = ctx.sem("gld"); st = ctx.sem("st")
        td = ctx.sem("td"); vd = ctx.sem("vd"); ev = ctx.sem("ev")

        wo = ctx.sbuf("wo_sb", [128, 4, D], BF16)
        mask_sb = ctx.sbuf("masks", [128, TC // 128], F32)
        rec.add("gpsimd", lambda e: e.dma_start(
            out=wo[:], in_=wo_h.rearrange("(c p) e -> p c e", p=128)), incs=[(gld, 16)])
        w_gld = gld.inc(16)
        rec.add("sync", lambda e: e.dma_start(
            out=mask_sb[:], in_=msk_h.rearrange("(i p) -> p i", p=128),
            allow_slow_non_contiguous=True), incs=[(ld, 16)])
        prep_ld = ld.inc(16)

        o_sb = [ctx.sbuf(f"osb{i}", [128, 512], BF16) for i in range(2)]
        xr_sb = [ctx.sbuf(f"xrsb{i}", [128, D], BF16) for i in range(2)]
        ofm = [ctx.sbuf(f"ofm{i}", [128, 4, 128], BF16) for i in range(2)]
        x1s = [ctx.sbuf(f"x1s{i}", [128, D], BF16) for i in range(2)]
        tp = ctx.psum("tp", [128, 512], BF16)
        ps = [ctx.psum(f"ps{i}", [128, 512]) for i in range(2)]

        trk = {}
        tiles = list(range(t0tile, TC // 128))
        zt = ctx.sbuf("ztile", [128, D], BF16)
        rec.add("vector", lambda e: e.memset(zt[:], 0.0), incs=[(vd, 1)])
        z_v = vd.inc()
        rec.add("sync", lambda e: e.dma_start(
            out=x1_h[(t0tile - 1) * 128:t0tile * 128, :], in_=zt[:]),
            waits=[(vd, z_v)], incs=[(st, 16)])
        st.inc(16)
        for ti, i in enumerate(tiles):
            p = ti % 2
            ob, xr, of, x1 = o_sb[p], xr_sb[p], ofm[p], x1s[p]
            w = [(ld, prep_ld)] if ti == 0 else []
            if ti >= 2:
                w = [(td, trk[("tp", ti - 2)])]
            rec.add("sync", lambda e, i=i, ob=ob: e.dma_start(
                out=ob[:], in_=o_h[i * 128:(i + 1) * 128, :]), waits=w, incs=[(ld, 16)])
            rec.add("sync", lambda e, i=i, xr=xr: e.dma_start(
                out=xr[:], in_=xr_h[i * 128:(i + 1) * 128, :]), incs=[(ld, 16)])
            lv = ld.inc(32)
            wt = [(ld, lv)]
            if ti >= 1:
                wt.append((ev, trk[("ofev", ti - 1)]))
            for c in range(4):
                rec.add("tensor", lambda e, ob=ob, c=c: e.transpose(
                    tp[:, c * 128:(c + 1) * 128], ob[:, c * 128:(c + 1) * 128], idn[:]),
                    waits=wt if c == 0 else [], incs=[(td, 1)] if c == 3 else [])
            trk[("tp", ti)] = td.inc()
            rec.add("vector", lambda e, of=of: e.tensor_copy(
                of[:].rearrange("p c t -> p (c t)"), tp[:]),
                waits=[(td, trk[("tp", ti)])], incs=[(ev, 1)])
            trk[("ofev", ti)] = ev.inc()
            for half in range(2):
                pp = ps[half]
                wm = [(ev, trk[("ofev", ti)]), (gld, w_gld)]
                if ti >= 1:
                    wm.append((ev, trk[("x1ev", ti - 1, half)]))
                for c in range(4):
                    rec.add("tensor", lambda e, pp=pp, of=of, c=c, half=half: e.matmul(
                        pp[:], of[:, c, :], wo[:, c, half * 512:(half + 1) * 512],
                        start=(c == 0), stop=(c == 3)),
                        waits=wm if c == 0 else [], incs=[(td, 1)] if c == 3 else [])
                tv = td.inc()
                wv = [(td, tv)]
                if ti >= 2:
                    wv.append((st, trk[("x1st", ti - 2)]))
                rec.add("vector", lambda e, pp=pp, x1=x1, xr=xr, half=half: e.scalar_tensor_tensor(
                    x1[:, half * 512:(half + 1) * 512], pp[:], ctx.c("one"),
                    xr[:, half * 512:(half + 1) * 512], ALU.mult, ALU.add),
                    waits=wv, incs=[(ev, 1)] if i >= HALO // 128 else [])
                if i < HALO // 128:
                    rec.add("vector", lambda e, x1=x1, half=half, i=i: e.tensor_scalar(
                        x1[:, half * 512:(half + 1) * 512],
                        x1[:, half * 512:(half + 1) * 512],
                        mask_sb[:, i:i + 1], None, ALU.mult), incs=[(ev, 1)])
                trk[("x1ev", ti, half)] = ev.inc()
            rec.add("sync", lambda e, i=i, x1=x1: e.dma_start(
                out=x1_h[i * 128:(i + 1) * 128, :], in_=x1[:]),
                waits=[(ev, trk[("x1ev", ti, 1)])], incs=[(st, 16)])
            trk[("x1st", ti)] = st.inc(16)
        rec.add("sync", lambda e: e.nop(), waits=[(st, st.n)])

def build_ffn_wo_phase():
    ctx = Ctx("emit_ffn_wo")
    nc = ctx.nc
    nc = ctx.nc
    o_h = nc.dram_tensor("o_tm", [TC, 512], BF16, kind="ExternalInput").ap()
    wo_h = nc.dram_tensor("wo", [512, D], F32, kind="ExternalInput").ap()
    xr_h = nc.dram_tensor("xres_tm", [TC, D], BF16, kind="ExternalInput").ap()
    msk_h = nc.dram_tensor("kvmask", [TC], F32, kind="ExternalInput").ap()
    x1_h = nc.dram_tensor("x1_tm", [TC, D], BF16, kind="ExternalOutput").ap()

    io = {
        "o_tm": o_h,
        "wo": wo_h,
        "xres_tm": xr_h,
        "kvmask": msk_h,
        "x1_tm": x1_h,
    }
    with ctx.es:
        emit_ffn_wo(ctx, io)
        finish(ctx.nc, ctx.rec, ctx.es, ctx)
    return ctx.nc


def emit_ffn_mid(ctx, io, fm_out, t0tile=FA_T0, dbg_group=None):
    nc = ctx.nc
    rec = ctx.rec
    NCH = 2 * HIDP // 128
    NMID = HIDP // 128
    x1_h = io["x1_tm"]
    lg_h = io["ln_g"]
    lb_h = io["ln_b"]
    w1_h = io["w1p"]
    b1_h = io["b1p"]
    w2_h = io["w2p"]
    b2_h = io["b2"]
    msk_h = io["kvmask"]
    x2_h = io["x2_tm"]
    x2f_h = io.get("x2_fm")
    dbgh_h = io.get("dbgH")
    dbgm_h = io.get("dbgM")
    if True:
        ctx.setup_consts()
        idn = build_identity_bf16(ctx, io.get("ident"))
        rec = ctx.rec
        ld = ctx.sem("ld"); gld = ctx.sem("gld"); st = ctx.sem("st")
        td = ctx.sem("td"); vd = ctx.sem("vd"); sd = ctx.sem("sd"); ev = ctx.sem("ev")

        w1 = ctx.sbuf("w1_sb", [128, 8, 2 * HIDP], BF16)
        w2 = ctx.sbuf("w2_sb", [128, NMID, D], BF16)
        cols = ctx.sbuf("cols_sb", [128, 8 + 8 + NCH], F32)
        b2row = ctx.sbuf("b2row", [1, D], F32)
        ones_row = ctx.sbuf("ones_row2", [1, 128], F32)
        mask_sb = ctx.sbuf("masks", [128, TC // 128], F32)
        rec.add("gpsimd", lambda e: e.dma_start(
            out=w1[:], in_=w1_h.rearrange("(c p) e -> p c e", p=128)), incs=[(gld, 16)])
        rec.add("gpsimd", lambda e: e.dma_start(
            out=w2[:], in_=w2_h.rearrange("(c p) e -> p c e", p=128)), incs=[(gld, 16)])
        w_gld = gld.inc(32)
        rec.add("sync", lambda e: e.dma_start(
            out=cols[:, 0:8], in_=lg_h.rearrange("(c p) -> p c", p=128),
            allow_slow_non_contiguous=True), incs=[(ld, 16)])
        rec.add("sync", lambda e: e.dma_start(
            out=cols[:, 8:16], in_=lb_h.rearrange("(c p) -> p c", p=128),
            allow_slow_non_contiguous=True), incs=[(ld, 16)])
        rec.add("sync", lambda e: e.dma_start(
            out=cols[:, 16:16 + NCH], in_=b1_h.rearrange("(c p) -> p c", p=128),
            allow_slow_non_contiguous=True), incs=[(ld, 16)])
        rec.add("sync", lambda e: e.dma_start(
            out=b2row[:], in_=b2_h.rearrange("(o e) -> o e", o=1)), incs=[(ld, 16)])
        rec.add("sync", lambda e: e.dma_start(
            out=mask_sb[:], in_=msk_h.rearrange("(i p) -> p i", p=128),
            allow_slow_non_contiguous=True), incs=[(ld, 16)])
        rec.add("vector", lambda e: e.memset(ones_row[:], 1.0), incs=[(vd, 1)])
        prep_ld = ld.inc(16 * 5)
        prep_vd = vd.inc()

        x1r = [ctx.sbuf(f"x1r{i}", [128, D], BF16) for i in range(4)]
        h2b = [ctx.sbuf(f"h2b{i}", [128, 512], BF16) for i in range(2)]
        h2n = ctx.sbuf("h2n", [128, D], BF16)
        h2nf = ctx.sbuf("h2nf", [128, 8, 512], BF16)
        mid = ctx.sbuf("mid_sb", [128, NMID, 512], BF16)
        Gg = ctx.sbuf("Gg", [128, 512], BF16)
        stats = ctx.sbuf("stats2", [128, 12], F32)
        scr = ctx.sbuf("scr2", [128, D], F32)
        x2s = [ctx.sbuf(f"x2s{i}", [128, D], BF16) for i in range(2)]
        fmst = ctx.sbuf("fmst2", [128, D], BF16)
        tp = ctx.psum("tp2", [128, 512], BF16)
        hp1 = [ctx.psum(f"hp{i}", [128, 512]) for i in range(2)]
        xp = [ctx.psum(f"xp{i}", [128, 512]) for i in range(2)]

        trk = {}
        ntile = TC // 128
        groups = []
        if fm_out:
            zt2 = ctx.sbuf("ztile2", [128, D], BF16)
            rec.add("vector", lambda e: e.memset(zt2[:], 0.0), incs=[(vd, 1)])
            z_v2 = vd.inc()
            rec.add("sync", lambda e: e.dma_start(
                out=x2f_h[:, (t0tile - 1) * 128:t0tile * 128].rearrange(
                    "(c f) t -> f c t", f=128),
                in_=zt2[:].rearrange("f (c t) -> f c t", t=128)),
                waits=[(vd, z_v2)], incs=[(st, 16)])
            st.inc(16)
        t = t0tile * 128
        while t < TC:
            spn = min(512, TC - t)
            groups.append((t, spn))
            t += spn
        for gi, (g0, spn) in enumerate(groups):
            nt = spn // 128
            for tt in range(nt):
                i = g0 // 128 + tt
                p = i % 2
                xr, hb = x1r[i % 4], h2b[p]
                w = [(ld, prep_ld), (vd, prep_vd)] if gi == 0 and tt == 0 else []
                if ("x2ev", i - 4, 1) in trk:
                    w = [(ev, trk[("x2ev", i - 4, 1)])]
                elif (i - 2) * 128 >= t0tile * 128:
                    w = [(vd, trk[("h2n", i - 2)])]
                if ("sh2n", i - 2) in trk:
                    w.append((sd, trk[("sh2n", i - 2)]))
                rec.add("sync", lambda e, i=i, xr=xr: e.dma_start(
                    out=xr[:], in_=x1_h[i * 128:(i + 1) * 128, :]),
                    waits=w, incs=[(ld, 16)])
                rec.add("sync", lambda e, i=i, hb=hb: e.dma_start(
                    out=hb[:], in_=x1_h[i * 128 - 1:(i + 1) * 128 - 1, 512:1024]),
                    incs=[(ld, 16)])
                lv = ld.inc(32)
                rec.add("vector", lambda e, xr=xr: e.reduce_sum(
                    stats[:, 0:1], xr[:, 0:512], AX.X), waits=[(ld, lv)])
                rec.add("vector", lambda e, hb=hb: e.reduce_sum(
                    stats[:, 1:2], hb[:], AX.X))
                rec.add("scalar", lambda e, xr=xr: e.activation(
                    scr[:, 0:512], xr[:, 0:512], AF.Square), waits=[(ld, lv)])
                rec.add("scalar", lambda e, hb=hb: e.activation(
                    scr[:, 512:1024], hb[:], AF.Square), incs=[(sd, 1)])
                s_sq = sd.inc()
                rec.add("vector", lambda e: e.reduce_sum(
                    stats[:, 2:3], scr[:], AX.X), waits=[(sd, s_sq)])
                rec.add("vector", lambda e: e.tensor_tensor(
                    stats[:, 3:4], stats[:, 0:1], stats[:, 1:2], ALU.add), incs=[(vd, 1)])
                v_s = vd.inc()
                rec.add("vector", lambda e: e.tensor_scalar(
                    stats[:, 4:5], stats[:, 3:4], ctx.c("inv1024"), None, ALU.mult))
                rec.add("vector", lambda e: e.tensor_scalar(
                    stats[:, 5:6], stats[:, 2:3], ctx.c("inv1024"), None, ALU.mult),
                    incs=[(vd, 1)])
                v_m = vd.inc()
                rec.add("vector", lambda e: e.scalar_tensor_tensor(
                    stats[:, 6:7], stats[:, 4:5], stats[:, 4:5], stats[:, 5:6],
                    ALU.mult, ALU.subtract), waits=[(vd, v_m)], incs=[(vd, 1)])
                v_t = vd.inc()
                rec.add("scalar", lambda e: e.activation(
                    stats[:, 7:8], stats[:, 6:7], AF.Sqrt, bias=ctx.c("eps"),
                    scale=ctx.c("neg1")), waits=[(vd, v_t)], incs=[(sd, 1)])
                s_std = sd.inc()
                rec.add("vector", lambda e: e.reciprocal(
                    stats[:, 8:9], stats[:, 7:8]), waits=[(sd, s_std)], incs=[(vd, 1)])
                v_rs = vd.inc()
                rec.add("vector", lambda e: e.tensor_scalar(
                    stats[:, 9:10], stats[:, 4:5], stats[:, 8:9], ctx.c("neg1"),
                    ALU.mult, ALU.mult), waits=[(vd, v_rs)], incs=[(vd, 1)])
                v_mb = vd.inc()
                wa = [(vd, v_mb)]
                if (i - 1) * 128 >= t0tile * 128:
                    wa.append((td, trk[("h2tp", i - 1)]))
                rec.add("scalar", lambda e, xr=xr: e.activation(
                    h2n[:, 0:512], xr[:, 0:512], AF.Identity, bias=stats[:, 9:10],
                    scale=stats[:, 8:9]), waits=wa)
                rec.add("scalar", lambda e, hb=hb: e.activation(
                    h2n[:, 512:1024], hb[:], AF.Identity, bias=stats[:, 9:10],
                    scale=stats[:, 8:9]), incs=[(sd, 1)])
                s_h2n = sd.inc()
                trk[("sh2n", i)] = s_h2n
                for c in range(8):
                    wt = [(sd, s_h2n)] if c % 4 == 0 else []
                    if c % 4 == 0 and (gi > 0 or tt > 0 or c > 0):
                        wt.append((ev, ctx.sems["ev"].n))
                    rec.add("tensor", lambda e, c=c: e.transpose(
                        tp[:, (c % 4) * 128:((c % 4) + 1) * 128],
                        h2n[:, c * 128:(c + 1) * 128], idn[:]),
                        waits=wt, incs=[(td, 1)] if c % 4 == 3 else [])
                    if c % 4 == 3:
                        tv = td.inc()
                        for cc in range(c - 3, c + 1):
                            lastcc = cc == c
                            rec.add("vector", lambda e, cc=cc, tt=tt: e.scalar_tensor_tensor(
                                h2nf[:, cc, tt * 128:(tt + 1) * 128],
                                tp[:, (cc % 4) * 128:((cc % 4) + 1) * 128],
                                cols[:, cc:cc + 1],
                                cols[:, 8 + cc:9 + cc].broadcast_to([128, 128]),
                                ALU.mult, ALU.add),
                                waits=[(td, tv)], incs=[(ev, 1)])
                            ev.inc()
                trk[("h2tp", i)] = td.n
                rec.add("vector", lambda e: e.tensor_copy(
                    stats[:, 10:11], stats[:, 8:9]), incs=[(vd, 1)])
                trk[("h2n", i)] = vd.inc()
            h2nf_ev = ctx.sems["ev"].n
            for m in range(NMID):
                pg = hp1[0]
                pa = hp1[1]
                wmg = [(ev, h2nf_ev), (gld, w_gld)]
                if (gi, m - 1) in trk:
                    wmg.append((sd, trk[(gi, m - 1)]))
                    wmg.append((vd, trk[(gi, m - 1, "a")]))
                elif gi > 0:
                    wmg.append((sd, trk[(gi - 1, NMID - 1)]))
                    wmg.append((vd, trk[(gi - 1, NMID - 1, "a")]))
                for c in range(8):
                    rec.add("tensor", lambda e, pg=pg, c=c, m=m, spn=spn: e.matmul(
                        pg[:, 0:spn], w1[:, c, (NMID + m) * 128:(NMID + m + 1) * 128],
                        h2nf[:, c, 0:spn], start=(c == 0), stop=(c == 7)),
                        waits=wmg if c == 0 else [], incs=[(td, 1)] if c == 7 else [])
                tvg = td.inc()
                rec.add("scalar", lambda e, pg=pg, m=m, spn=spn: e.activation(
                    Gg[:, 0:spn], pg[:, 0:spn], AF.Gelu,
                    bias=cols[:, 16 + NMID + m:17 + NMID + m]),
                    waits=[(td, tvg)], incs=[(sd, 1)])
                trk[(gi, m)] = sd.inc()
                wma = [(td, tvg)]
                if (gi, m - 1, "a") in trk:
                    wma.append((vd, trk[(gi, m - 1, "a")]))
                elif gi > 0:
                    wma.append((vd, trk[(gi - 1, NMID - 1, "a")]))
                for c in range(8):
                    rec.add("tensor", lambda e, pa=pa, c=c, m=m, spn=spn: e.matmul(
                        pa[:, 0:spn], w1[:, c, m * 128:(m + 1) * 128],
                        h2nf[:, c, 0:spn], start=(c == 0), stop=(c == 7)),
                        waits=wma if c == 0 else [], incs=[(td, 1)] if c == 7 else [])
                tva = td.inc()
                rec.add("vector", lambda e, pa=pa, m=m, spn=spn: e.scalar_tensor_tensor(
                    mid[:, m, 0:spn], pa[:, 0:spn], cols[:, 16 + m:17 + m],
                    Gg[:, 0:spn], ALU.add, ALU.mult),
                    waits=[(td, tva), (sd, trk[(gi, m)])], incs=[(vd, 1)])
                trk[(gi, m, "a")] = vd.inc()
            mid_vd = trk[(gi, NMID - 1, "a")]
            if dbg_group is not None and gi == dbg_group:
                rec.add("sync", lambda e: e.dma_start(
                    out=dbgh_h, in_=h2nf[:].rearrange("p c t -> p (c t)")),
                    waits=[(vd, mid_vd)], incs=[(st, 16)])
                st.inc(16)
                rec.add("sync", lambda e: e.dma_start(
                    out=dbgm_h, in_=mid[:].rearrange("p c t -> p (c t)")),
                    waits=[(vd, mid_vd)], incs=[(st, 16)])
                st.inc(16)
            for tt in range(nt):
                i = g0 // 128 + tt
                p = i % 2
                x2b = x2s[p]
                for half in range(2):
                    pp = xp[half]
                    wm = [(vd, mid_vd)]
                    if ("x2ev", i - 2, half) in trk:
                        wm.append((ev, trk[("x2ev", i - 2, half)]))
                    if fm_out and ("xf", i - 2) in trk:
                        wm.append((st, trk[("xf", i - 2)]))
                    for m in range(NMID):
                        rec.add("tensor", lambda e, pp=pp, m=m, tt=tt, half=half: e.matmul(
                            pp[:], mid[:, m, tt * 128:(tt + 1) * 128],
                            w2[:, m, half * 512:(half + 1) * 512],
                            start=(m == 0), stop=False),
                            waits=wm if m == 0 else [])
                    rec.add("tensor", lambda e, pp=pp, half=half: e.matmul(
                        pp[:], ones_row[:], b2row[:, half * 512:(half + 1) * 512],
                        start=False, stop=True), incs=[(td, 1)])
                    tvx = td.inc()
                    wv = [(td, tvx)]
                    if ("x2st", i - 2) in trk:
                        wv.append((st, trk[("x2st", i - 2)]))
                    xr = x1r[i % 4]
                    rec.add("vector", lambda e, pp=pp, x2b=x2b, xr=xr, half=half: e.scalar_tensor_tensor(
                        x2b[:, half * 512:(half + 1) * 512], pp[:], ctx.c("one"),
                        xr[:, half * 512:(half + 1) * 512], ALU.mult, ALU.add),
                        waits=wv, incs=[(ev, 1)] if i >= HALO // 128 else [])
                    if i < HALO // 128:
                        rec.add("vector", lambda e, x2b=x2b, half=half, i=i: e.tensor_scalar(
                            x2b[:, half * 512:(half + 1) * 512],
                            x2b[:, half * 512:(half + 1) * 512],
                            mask_sb[:, i:i + 1], None, ALU.mult), incs=[(ev, 1)])
                    trk[("x2ev", i, half)] = ev.inc()
                rec.add("sync", lambda e, i=i, x2b=x2b: e.dma_start(
                    out=x2_h[i * 128:(i + 1) * 128, :], in_=x2b[:]),
                    waits=[(ev, trk[("x2ev", i, 1)])], incs=[(st, 16)])
                trk[("x2st", i)] = st.inc(16)
                if fm_out:
                    for c in range(8):
                        wt = []
                        if c % 4 == 0:
                            wt = [(ev, trk[("x2ev", i, 1)]), (ev, ctx.sems["ev"].n)]
                        rec.add("tensor", lambda e, x2b=x2b, c=c: e.transpose(
                            tp[:, (c % 4) * 128:((c % 4) + 1) * 128],
                            x2b[:, c * 128:(c + 1) * 128], idn[:]),
                            waits=wt, incs=[(td, 1)] if c % 4 == 3 else [])
                        if c % 4 == 3:
                            tvf = td.inc()
                            wvf = [(td, tvf)]
                            if ("xf", i - 1) in trk:
                                wvf.append((st, trk[("xf", i - 1)]))
                            rec.add("vector", lambda e, c=c: e.tensor_copy(
                                fmst[:, (c - 3) * 128:(c + 1) * 128],
                                tp[:]), waits=wvf, incs=[(ev, 1)])
                            ev.inc()
                    fm_ev = ctx.sems["ev"].n
                    rec.add("sync", lambda e, i=i: e.dma_start(
                        out=x2f_h[:, i * 128:(i + 1) * 128].rearrange(
                            "(c f) t -> f c t", f=128),
                        in_=fmst[:].rearrange("f (c t) -> f c t", t=128)),
                        waits=[(ev, fm_ev)], incs=[(st, 16)])
                    trk[("xf", i)] = st.inc(16)
            trk[("gx2", gi)] = trk[("x2ev", g0 // 128 + nt - 1, 1)]
        rec.add("sync", lambda e: e.nop(), waits=[(st, st.n)])

def build_ffn_mid_phase(fm_out, dbg_group=None):
    ctx = Ctx("emit_ffn_mid")
    nc = ctx.nc
    nc = ctx.nc
    x1_h = nc.dram_tensor("x1_tm", [TC, D], BF16, kind="ExternalInput").ap()
    lg_h = nc.dram_tensor("ln_g", [D], F32, kind="ExternalInput").ap()
    lb_h = nc.dram_tensor("ln_b", [D], F32, kind="ExternalInput").ap()
    w1_h = nc.dram_tensor("w1p", [D, 2 * HIDP], F32, kind="ExternalInput").ap()
    b1_h = nc.dram_tensor("b1p", [2 * HIDP], F32, kind="ExternalInput").ap()
    w2_h = nc.dram_tensor("w2p", [HIDP, D], F32, kind="ExternalInput").ap()
    b2_h = nc.dram_tensor("b2", [D], F32, kind="ExternalInput").ap()
    msk_h = nc.dram_tensor("kvmask", [TC], F32, kind="ExternalInput").ap()
    x2_h = nc.dram_tensor("x2_tm", [TC, D], BF16, kind="ExternalOutput").ap()
    if fm_out:
        x2f_h = nc.dram_tensor("x2_fm", [D, TC], BF16, kind="ExternalOutput").ap()
    if dbg_group is not None:
        dbgh_h = nc.dram_tensor("dbgH", [128, 8 * 512], BF16, kind="ExternalOutput").ap()
        dbgm_h = nc.dram_tensor("dbgM", [128, 22 * 512], BF16, kind="ExternalOutput").ap()

    NCH = 2 * HIDP // 128
    NMID = HIDP // 128
    io = {
        "x1_tm": x1_h,
        "ln_g": lg_h,
        "ln_b": lb_h,
        "w1p": w1_h,
        "b1p": b1_h,
        "w2p": w2_h,
        "b2": b2_h,
        "kvmask": msk_h,
        "x2_tm": x2_h,
    }
    if fm_out:
        io["x2_fm"] = x2f_h
    if dbg_group is not None:
        io.update({"dbgH": dbgh_h, "dbgM": dbgm_h})
    with ctx.es:
        emit_ffn_mid(ctx, io, fm_out=fm_out, dbg_group=dbg_group)
        finish(ctx.nc, ctx.rec, ctx.es, ctx)
    return ctx.nc


def emit_combine(ctx, io):
    nc = ctx.nc
    rec = ctx.rec
    x_h = io["x"]
    s_h = io["s_tm"]
    f_h = io["f_tm"]
    o_h = io["out"]
    if True:
        ctx.setup_consts()
        rec = ctx.rec
        ld = ctx.sem("ld"); st = ctx.sem("st")
        vd = ctx.sem("vd"); sd = ctx.sem("sd")
        xb = [ctx.sbuf(f"xb{i}", [128, D], F32) for i in range(2)]
        sb = [ctx.sbuf(f"sb{i}", [128, D], BF16) for i in range(2)]
        fb = [ctx.sbuf(f"fb{i}", [128, D], BF16) for i in range(2)]
        gb = ctx.sbuf("gb", [128, D], BF16)
        tb = ctx.sbuf("tb", [128, D], F32)
        ob = [ctx.sbuf(f"ob{i}", [128, D], F32) for i in range(2)]
        trk = {}
        ntile = T_OWN // 128
        for ti in range(ntile):
            i = HALO // 128 + ti
            p = ti % 2
            w = []
            if ti >= 2:
                w = [(vd, trk[("mul", ti - 2)])]
            rec.add("sync", lambda e, i=i, p=p: e.dma_start(
                out=xb[p][:], in_=x_h[i * 128:(i + 1) * 128, :]), waits=w, incs=[(ld, 16)])
            rec.add("sync", lambda e, i=i, p=p: e.dma_start(
                out=sb[p][:], in_=s_h[i * 128:(i + 1) * 128, :]), incs=[(ld, 16)])
            rec.add("sync", lambda e, i=i, p=p: e.dma_start(
                out=fb[p][:], in_=f_h[i * 128:(i + 1) * 128, :]), incs=[(ld, 16)])
            lv = ld.inc(48)
            ws = [(ld, lv)]
            if ti >= 1:
                ws.append((vd, trk[("mul", ti - 1)]))
            rec.add("scalar", lambda e, p=p: e.activation(
                gb[:], fb[p][:], AF.Silu), waits=ws, incs=[(sd, 1)])
            s_g = sd.inc()
            wv = [(sd, s_g)]
            if ti >= 1:
                wv.append((vd, trk[("add", ti - 1)]))
            rec.add("vector", lambda e, p=p: e.tensor_tensor(
                tb[:], sb[p][:], gb[:], ALU.mult), waits=wv, incs=[(vd, 1)])
            trk[("mul", ti)] = vd.inc()
            wv2 = []
            if ti >= 2:
                wv2 = [(st, trk[("ost", ti - 2)])]
            rec.add("vector", lambda e, p=p: e.tensor_tensor(
                ob[p][:], xb[p][:], tb[:], ALU.add), waits=wv2, incs=[(vd, 1)])
            trk[("add", ti)] = vd.inc()
            rec.add("sync", lambda e, ti=ti, p=p: e.dma_start(
                out=o_h[ti * 128:(ti + 1) * 128, :], in_=ob[p][:]),
                waits=[(vd, trk[("add", ti)])], incs=[(st, 16)])
            trk[("ost", ti)] = st.inc(16)
        rec.add("sync", lambda e: e.nop(), waits=[(st, st.n)])

def build_combine_phase():
    ctx = Ctx("emit_combine")
    nc = ctx.nc
    nc = ctx.nc
    x_h = nc.dram_tensor("x", [TC, D], F32, kind="ExternalInput").ap()
    s_h = nc.dram_tensor("s_tm", [TC, D], BF16, kind="ExternalInput").ap()
    f_h = nc.dram_tensor("f_tm", [TC, D], BF16, kind="ExternalInput").ap()
    o_h = nc.dram_tensor("out", [T_OWN, D], F32, kind="ExternalOutput").ap()

    io = {
        "x": x_h,
        "s_tm": s_h,
        "f_tm": f_h,
        "out": o_h,
    }
    with ctx.es:
        emit_combine(ctx, io)
        finish(ctx.nc, ctx.rec, ctx.es, ctx)
    return ctx.nc


def _pad_slice(full, lo, hi, axis=0, dtype=None):
    n = full.shape[axis]
    lo2, hi2 = max(lo, 0), min(hi, n)
    sl = [slice(None)] * full.ndim
    sl[axis] = slice(lo2, hi2)
    part = full[tuple(sl)]
    pads = [(0, 0)] * full.ndim
    pads[axis] = (lo2 - lo, hi - hi2)
    out = np.pad(part, pads) if (lo2 - lo or hi - hi2) else part
    return out.astype(dtype) if dtype is not None else out


def _core_ranges():
    return [(c // 2, (c % 2) * T_OWN) for c in range(NCORES)]


_IDENT = np.eye(128, dtype=np.float32)


def run_ln(x_full):
    nc = build_ln_phase()
    in_maps = []
    for b, t0 in _core_ranges():
        xs = _pad_slice(x_full[b], t0 - HALO, t0 + T_OWN, axis=0).astype(np.float32)
        in_maps.append({"x": np.ascontiguousarray(xs), "ident": _IDENT})
    res = run_bass_kernel_spmd(nc, in_maps, core_ids=list(range(NCORES)))
    return [r["xn_tm"] for r in res.results], [r["xn_fm"] for r in res.results]


_MCONST = None


def run_slot(xn_fm_list, inputs):
    global _MCONST
    if _MCONST is None:
        _MCONST = make_slot_mconst()
    nc = build_slot_qkv_phase()
    im = [{"xn_fm": xn_fm_list[c], "Wqkv": inputs["L0_attn_Wqkv"],
           "bqkv": inputs["L0_attn_bqkv"]} for c in range(NCORES)]
    res = run_bass_kernel_spmd(nc, im, core_ids=list(range(NCORES)))
    q = [r["q_fm"] for r in res.results]
    k = [r["k_fm"] for r in res.results]
    v = [r["v65"] for r in res.results]
    nc = build_slot_attn_phase()
    masks = []
    for b, t0 in _core_ranges():
        m = np.zeros(TC, dtype=np.float32)
        m[max(0, HALO - t0):] = 1.0
        masks.append(m)
    im = [{"q_fm": q[c], "k_fm": k[c], "v65": v[c],
           "pos_bias": inputs["L0_attn_pos_bias"], "mconst": _MCONST,
           "kvmask": masks[c], "ident": _IDENT} for c in range(NCORES)]
    res = run_bass_kernel_spmd(nc, im, core_ids=list(range(NCORES)))
    flat = [r["flat"] for r in res.results]
    nc = build_gate_phase()
    im = [{"xn_fm": xn_fm_list[c], "flat": flat[c],
           "Wgate": inputs["L0_attn_Wgate"], "bgate": inputs["L0_attn_bgate"],
           "Wout": inputs["L0_attn_Wout"], "bout": inputs["L0_attn_bout"],
           "ident": _IDENT} for c in range(NCORES)]
    res = run_bass_kernel_spmd(nc, im, core_ids=list(range(NCORES)))
    s = [r["s_tm"] for r in res.results]
    return q, k, v, flat, s


def _prep_ffn_weights(inputs, l):
    w1 = np.asarray(inputs["L0_ffn_w1"][l])
    b1 = np.asarray(inputs["L0_ffn_b1"][l])
    w2 = np.asarray(inputs["L0_ffn_w2"][l])
    w1p = np.zeros((D, 2 * HIDP), np.float32)
    w1p[:, 0:HID] = w1[:, 0:HID]
    w1p[:, HIDP:HIDP + HID] = w1[:, HID:2 * HID]
    b1p = np.zeros(2 * HIDP, np.float32)
    b1p[0:HID] = b1[0:HID]
    b1p[HIDP:HIDP + HID] = b1[HID:2 * HID]
    w2p = np.zeros((HIDP, D), np.float32)
    w2p[0:HID] = w2
    return w1p, b1p, w2p


def _masks():
    ms = []
    for b, t0 in _core_ranges():
        m = np.zeros(TC, dtype=np.float32)
        m[max(0, HALO - t0):] = 1.0
        ms.append(m)
    return ms


def _spmd(nc, im):
    return run_bass_kernel_spmd(nc, im, core_ids=list(range(NCORES))).results


def run_ffn_layer(inputs, l, xin_fm, xres_tm, fm_out):
    masks = _masks()
    tabs = [make_rotary_tables(t0) for _, t0 in _core_ranges()]
    nc = build_ffn_qkv_phase()
    im = [{"xin_fm": xin_fm[c], "wqkv": np.asarray(inputs["L0_ffn_wqkv"][l]),
           "q_scale": np.asarray(inputs["L0_ffn_q_scale"][l]),
           "k_scale": np.asarray(inputs["L0_ffn_k_scale"][l]),
           "cq": tabs[c][0], "sq": tabs[c][1], "ck": tabs[c][2], "sk": tabs[c][3],
           "ident": _IDENT} for c in range(NCORES)]
    r = _spmd(nc, im)
    q = [x["q_fm"] for x in r]; k = [x["k_fm"] for x in r]; v = [x["v65"] for x in r]

    w01a, w01b = make_w01()
    nc = build_ffn_attn_phase()
    im = [{"q_fm": q[c], "k_fm": k[c], "v65": v[c], "w01a": w01a, "w01b": w01b,
           "kvmask": masks[c]} for c in range(NCORES)]
    r = _spmd(nc, im)
    o = [x["o_tm"] for x in r]

    nc = build_ffn_wo_phase()
    im = [{"o_tm": o[c], "wo": np.asarray(inputs["L0_ffn_wo"][l]),
           "xres_tm": xres_tm[c], "kvmask": masks[c], "ident": _IDENT}
          for c in range(NCORES)]
    r = _spmd(nc, im)
    x1 = [x["x1_tm"] for x in r]

    w1p, b1p, w2p = _prep_ffn_weights(inputs, l)
    nc = build_ffn_mid_phase(fm_out)
    im = [{"x1_tm": x1[c], "ln_g": np.asarray(inputs["L0_ffn_ln_g"][l]),
           "ln_b": np.asarray(inputs["L0_ffn_ln_b"][l]), "w1p": w1p, "b1p": b1p,
           "w2p": w2p, "b2": np.asarray(inputs["L0_ffn_b2"][l]),
           "kvmask": masks[c], "ident": _IDENT} for c in range(NCORES)]
    r = _spmd(nc, im)
    x2 = [x["x2_tm"] for x in r]
    x2f = [x["x2_fm"] for x in r] if fm_out else None
    return x1, x2, x2f


def build_mega(dbg=False):
    nc = bass.Bass("TRN2", target_bir_lowering=False)
    EI, IN, EO = "ExternalInput", "Internal", "ExternalOutput"
    if dbg:
        IN = EO

    def d(nm, shp, dt, kind):
        return nc.dram_tensor(nm, list(shp), dt, kind=kind).ap()

    g = {}
    g["x"] = d("x", [TC, D], F32, EI)
    g["ident"] = d("ident", [128, 128], F32, EI)
    g["kvmask"] = d("kvmask", [TC], F32, EI)
    g["Wqkv"] = d("Wqkv", [D, 3 * D], F32, EI)
    g["bqkv"] = d("bqkv", [3 * D], F32, EI)
    g["Wgate"] = d("Wgate", [D, D], F32, EI)
    g["bgate"] = d("bgate", [D], F32, EI)
    g["Wout"] = d("Wout", [D, D], F32, EI)
    g["bout"] = d("bout", [D], F32, EI)
    g["pos_bias"] = d("pos_bias", [44, 16], F32, EI)
    g["mconst"] = d("mconst", [44, NPOS], BF16, EI)
    g["w01a"] = d("w01a", [128, 128], BF16, EI)
    g["w01b"] = d("w01b", [128, 128], BF16, EI)
    for n in ("cq", "sq", "ck", "sk"):
        g[n] = d(n, [TC, 64], F32, EI)
    for l in range(2):
        g[f"wqkv{l}"] = d(f"wqkv{l}", [D, 1536], F32, EI)
        g[f"q_scale{l}"] = d(f"q_scale{l}", [64], F32, EI)
        g[f"k_scale{l}"] = d(f"k_scale{l}", [64], F32, EI)
        g[f"wo{l}"] = d(f"wo{l}", [512, D], F32, EI)
        g[f"ln_g{l}"] = d(f"ln_g{l}", [D], F32, EI)
        g[f"ln_b{l}"] = d(f"ln_b{l}", [D], F32, EI)
        g[f"w1p{l}"] = d(f"w1p{l}", [D, 2 * HIDP], F32, EI)
        g[f"b1p{l}"] = d(f"b1p{l}", [2 * HIDP], F32, EI)
        g[f"w2p{l}"] = d(f"w2p{l}", [HIDP, D], F32, EI)
        g[f"b2{l}"] = d(f"b2{l}", [D], F32, EI)
    g["xn_tm"] = d("xn_tm", [TC, D], BF16, IN)
    g["xn_fm"] = d("xn_fm", [D, TC], BF16, IN)
    g["sq_fm"] = d("sq_fm", [D, TC], BF16, IN)
    g["sk_fm"] = d("sk_fm", [D, TC], BF16, IN)
    g["sv65"] = d("sv65", [TC, SLOT_H * 65], BF16, IN)
    g["ball"] = d("ball", [16, NPOS], F32, IN)
    g["flat"] = d("flat", [TC, D], BF16, IN)
    g["s_tm"] = d("s_tm", [TC, D], BF16, IN)
    for l in range(2):
        g[f"fq{l}"] = d(f"fq{l}", [512, TC], BF16, IN)
        g[f"fk{l}"] = d(f"fk{l}", [512, TC], BF16, IN)
        g[f"fv{l}"] = d(f"fv{l}", [TC, FFN_H * 65], BF16, IN)
        g[f"fo{l}"] = d(f"fo{l}", [TC, 512], BF16, IN)
        g[f"fx1{l}"] = d(f"fx1{l}", [TC, D], BF16, IN)
        g[f"fx2{l}"] = d(f"fx2{l}", [TC, D], BF16, IN)
    g["x2_fm"] = d("x2_fm", [D, TC], BF16, IN)
    g["out"] = d("out", [T_OWN, D], F32, EO)

    mega_es = contextlib.ExitStack()
    barrier = [()]

    shared_sems = {}

    def phase(name, emitfn, io, *args, **kw):
        ctx = Ctx(name, nc, sem_es=mega_es, sems=shared_sems)
        with ctx.es:
            emitfn(ctx, io, *args, **kw)
            finish(nc, ctx.rec, ctx.es, ctx, barrier[0])
        barrier[0] = ctx.final_sems()

    with mega_es:
        phase("ln", emit_ln, {"x": g["x"], "xn_tm": g["xn_tm"], "xn_fm": g["xn_fm"],
                              "ident": g["ident"]})
        phase("sqkv", emit_slot_qkv, {"xn_fm": g["xn_fm"], "Wqkv": g["Wqkv"],
                                      "bqkv": g["bqkv"], "q_fm": g["sq_fm"],
                                      "k_fm": g["sk_fm"], "v65": g["sv65"]})
        phase("sattn", emit_slot_attn, {"q_fm": g["sq_fm"], "k_fm": g["sk_fm"],
                                        "v65": g["sv65"], "pos_bias": g["pos_bias"],
                                        "mconst": g["mconst"], "kvmask": g["kvmask"],
                                        "flat": g["flat"], "ball_scratch": g["ball"],
                                        "ident": g["ident"]})
        phase("gate", emit_gate, {"xn_fm": g["xn_fm"], "flat": g["flat"],
                                  "Wgate": g["Wgate"], "bgate": g["bgate"],
                                  "Wout": g["Wout"], "bout": g["bout"],
                                  "s_tm": g["s_tm"], "ident": g["ident"]})
        for l in range(2):
            xin_fm = g["xn_fm"] if l == 0 else g["x2_fm"]
            xres = g["xn_tm"] if l == 0 else g["fx2" + "0"]
            tq = FQ_T0 if l == 0 else FA_T0
            ta = FA_T0 if l == 0 else 12
            phase(f"fqkv{l}", emit_ffn_qkv,
                  {"xin_fm": xin_fm, "wqkv": g[f"wqkv{l}"],
                   "q_scale": g[f"q_scale{l}"], "k_scale": g[f"k_scale{l}"],
                   "cq": g["cq"], "sq": g["sq"], "ck": g["ck"], "sk": g["sk"],
                   "q_fm": g[f"fq{l}"], "k_fm": g[f"fk{l}"], "v65": g[f"fv{l}"],
                   "ident": g["ident"]}, tq)
            phase(f"fattn{l}", emit_ffn_attn,
                  {"q_fm": g[f"fq{l}"], "k_fm": g[f"fk{l}"], "v65": g[f"fv{l}"],
                   "w01a": g["w01a"], "w01b": g["w01b"], "kvmask": g["kvmask"],
                   "o_tm": g[f"fo{l}"]}, ta)
            phase(f"fwo{l}", emit_ffn_wo,
                  {"o_tm": g[f"fo{l}"], "wo": g[f"wo{l}"], "xres_tm": xres,
                   "kvmask": g["kvmask"], "x1_tm": g[f"fx1{l}"], "ident": g["ident"]}, ta)
            io_mid = {"x1_tm": g[f"fx1{l}"], "ln_g": g[f"ln_g{l}"],
                      "ln_b": g[f"ln_b{l}"], "w1p": g[f"w1p{l}"], "b1p": g[f"b1p{l}"],
                      "w2p": g[f"w2p{l}"], "b2": g[f"b2{l}"], "kvmask": g["kvmask"],
                      "x2_tm": g[f"fx2{l}"], "ident": g["ident"]}
            if l == 0:
                io_mid["x2_fm"] = g["x2_fm"]
            phase(f"fmid{l}", emit_ffn_mid, io_mid, l == 0, ta)
        phase("comb", emit_combine, {"x": g["x"], "s_tm": g["s_tm"],
                                     "f_tm": g["fx21"], "out": g["out"]})
    return nc


_MEGA_NC = None


def kernel(**inputs):
    global _MEGA_NC, _MCONST
    if _MCONST is None:
        _MCONST = make_slot_mconst()
    if _MEGA_NC is None:
        _MEGA_NC = build_mega()
    x_full = np.asarray(inputs["L0_attn_x"])
    masks = _masks()
    w01a, w01b = make_w01()
    preps = [_prep_ffn_weights(inputs, l) for l in range(2)]
    in_maps = []
    for c, (b, t0) in enumerate(_core_ranges()):
        xs = _pad_slice(x_full[b], t0 - HALO, t0 + T_OWN, axis=0).astype(np.float32)
        cq, sq, ck, sk = make_rotary_tables(t0)
        im = {"x": np.ascontiguousarray(xs), "ident": _IDENT, "kvmask": masks[c],
              "Wqkv": np.asarray(inputs["L0_attn_Wqkv"]),
              "bqkv": np.asarray(inputs["L0_attn_bqkv"]),
              "Wgate": np.asarray(inputs["L0_attn_Wgate"]),
              "bgate": np.asarray(inputs["L0_attn_bgate"]),
              "Wout": np.asarray(inputs["L0_attn_Wout"]),
              "bout": np.asarray(inputs["L0_attn_bout"]),
              "pos_bias": np.asarray(inputs["L0_attn_pos_bias"]),
              "mconst": _MCONST, "w01a": w01a, "w01b": w01b,
              "cq": cq, "sq": sq, "ck": ck, "sk": sk}
        for l in range(2):
            im[f"wqkv{l}"] = np.asarray(inputs["L0_ffn_wqkv"][l])
            im[f"q_scale{l}"] = np.asarray(inputs["L0_ffn_q_scale"][l])
            im[f"k_scale{l}"] = np.asarray(inputs["L0_ffn_k_scale"][l])
            im[f"wo{l}"] = np.asarray(inputs["L0_ffn_wo"][l])
            im[f"ln_g{l}"] = np.asarray(inputs["L0_ffn_ln_g"][l])
            im[f"ln_b{l}"] = np.asarray(inputs["L0_ffn_ln_b"][l])
            im[f"w1p{l}"], im[f"b1p{l}"], im[f"w2p{l}"] = preps[l]
            im[f"b2{l}"] = np.asarray(inputs["L0_ffn_b2"][l])
        in_maps.append(im)
    res = run_bass_kernel_spmd(_MEGA_NC, in_maps, core_ids=list(range(NCORES)))
    out = np.empty((B, N, D), np.float32)
    for c, (b, t0) in enumerate(_core_ranges()):
        out[b, t0:t0 + T_OWN] = res.results[c]["out"]
    return out


def kernel_unfused(**inputs):
    x_full = np.asarray(inputs["L0_attn_x"])
    xn_tm, xn_fm = run_ln(x_full)
    _, _, _, _, s = run_slot(xn_fm, inputs)
    _, x2, x2f = run_ffn_layer(inputs, 0, xn_fm, xn_tm, True)
    x2p, f, _ = run_ffn_layer(inputs, 1, x2f, x2, False)
    nc = build_combine_phase()
    im = []
    for c, (b, t0) in enumerate(_core_ranges()):
        xs = _pad_slice(x_full[b], t0 - HALO, t0 + T_OWN, axis=0).astype(np.float32)
        im.append({"x": np.ascontiguousarray(xs), "s_tm": s[c], "f_tm": f[c]})
    r = _spmd(nc, im)
    out = np.empty((B, N, D), np.float32)
    for c, (b, t0) in enumerate(_core_ranges()):
        out[b, t0:t0 + T_OWN] = r[c]["out"]
    return out


# version 06
# speedup vs baseline: 1.2651x; 1.0239x over previous
import contextlib
import math
import numpy as np
import ml_dtypes

import concourse.bass as bass
import concourse.mybir as mybir
from concourse.bass_utils import run_bass_kernel_spmd

F32 = mybir.dt.float32
BF16 = mybir.dt.bfloat16
AF = mybir.ActivationFunctionType
ALU = mybir.AluOpType
AX = mybir.AxisListType
BF = ml_dtypes.bfloat16

B, N, D = 4, 8192, 1024
NCORES = 8
T_OWN = 4096
HALO = 1664
TC = T_OWN + HALO
OFFSETS = sorted(set(range(0, 33)) | {48, 64, 96, 128, 192, 256, 384, 512, 768, 1024, 1536})
SLOT_H, SLOT_HD = 16, 64
SLOT_BLOCKS = [0, 1, 2, 3, 4, 6, 8, 12]
FFN_H, FFN_HD = 8, 64
WIN = 512
HID = 2730
HIDP = 2816
NG = TC // 512


def cdiv(a, b):
    return (a + b - 1) // b


class Sem:

    def __init__(self, es, nc, name):
        self.h = es.enter_context(nc.semaphore(name))
        self.n = 0

    def inc(self, k=1):
        self.n += k
        return self.n


class Rec:

    def __init__(self):
        self.ops = {"sync": [], "tensor": [], "vector": [], "scalar": [], "gpsimd": []}

    def add(self, engine, fn, waits=(), incs=()):
        for sem, v in waits:
            rec_n = getattr(sem, "recorded", 0)
            assert v <= rec_n, (
                f"wait on {sem.h} for {v} but only {rec_n} incs recorded so far "
                f"(engine {engine}, op #{len(self.ops[engine])})")
        self.ops[engine].append((fn, list(waits), list(incs)))
        for sem, amount in incs:
            sem.recorded = getattr(sem, "recorded", 0) + amount

    @staticmethod
    def check(ctx):
        for name, sem in ctx.sems.items():
            rec_n = getattr(sem, "recorded", 0)
            assert rec_n == sem.n, (
                f"sem {name}: recorded incs {rec_n} != counter {sem.n}")

    def replay(self, engine_name, barrier=()):
        def runner(eng):
            first = True
            for fn, waits, incs in self.ops[engine_name]:
                if first:
                    for sem, val in barrier:
                        eng.wait_ge(sem.h, val)
                    first = False
                for sem, val in waits:
                    eng.wait_ge(sem.h, val)
                inst = fn(eng)
                for sem, val in incs:
                    inst.then_inc(sem.h, val)
        return runner


def finish(nc, rec, es, ctx=None, barrier=()):
    if ctx is not None:
        Rec.check(ctx)
    block = es.enter_context(nc.Block())
    for name in ("sync", "tensor", "vector", "scalar", "gpsimd"):
        if rec.ops[name]:
            getattr(block, name)(rec.replay(name, barrier))


CONSTS = {
    "one": 1.0, "neg1": -1.0, "inv1024": 1.0 / 1024.0, "eps": 1e-5,
    "tiny": 1e-30, "eighth": 0.125, "half": 0.5, "clip": 1e-12, "clip2": 1e-24,
}


class Ctx:

    def __init__(self, name, nc=None, sem_es=None, sems=None):
        self.nc = nc if nc is not None else bass.Bass("TRN2", target_bir_lowering=False)
        self.prefix = name + "_" if nc is not None else ""
        self.es = contextlib.ExitStack()
        self.sem_es = sem_es if sem_es is not None else self.es
        self.rec = Rec()
        self.sems = sems if sems is not None else {}
        self._sb = {}

    def sem(self, name):
        if name not in self.sems:
            self.sems[name] = Sem(self.sem_es, self.nc, name)
        return self.sems[name]

    def sbuf(self, name, shape, dtype):
        t = self.es.enter_context(self.nc.sbuf_tensor(self.prefix + name, list(shape), dtype))
        self._sb[name] = t
        return t

    def psum(self, name, shape, dtype=F32):
        return self.es.enter_context(self.nc.psum_tensor(self.prefix + name, list(shape), dtype))

    def final_sems(self):
        return [(s, s.n) for s in self.sems.values() if s.n > 0]

    def setup_consts(self):
        names = list(CONSTS)
        self.cst = self.sbuf("csts", [128, len(names)], F32)
        self.cidx = {n: i for i, n in enumerate(names)}
        s = self.sem("cst_ready")
        for n in names:
            i = self.cidx[n]
            v = CONSTS[n]
            self.rec.add("vector", lambda e, i=i, v=v: e.memset(self.cst[:, i:i + 1], v),
                         incs=[(s, 1)])
        self.cst_target = s.inc(len(names))

    def c(self, name):
        i = self.cidx[name]
        return self.cst[:, i:i + 1]


def build_identity_bf16(ctx, ident_h=None):
    if ident_h is None:
        ident_h = ctx.nc.dram_tensor("ident", [128, 128], F32, kind="ExternalInput").ap()
    idn = ctx.sbuf("idn", [128, 128], BF16)
    s = ctx.sem("ident_rdy")
    ctx.rec.add("gpsimd", lambda e: e.dma_start(out=idn[:], in_=ident_h),
                incs=[(s, 16)])
    s.inc(16)
    return idn


def emit_ln(ctx, io):
    nc = ctx.nc
    rec = ctx.rec
    x_h = io["x"]
    xn_tm_h = io["xn_tm"]
    xn_fm_h = io["xn_fm"]
    if True:
        ctx.setup_consts()
        idn = build_identity_bf16(ctx, io.get("ident"))
        rec = ctx.rec
        xt = [ctx.sbuf(f"xt{i}", [128, D], F32) for i in range(2)]
        xnt = [ctx.sbuf(f"xnt{i}", [128, D], BF16) for i in range(2)]
        fmt = [ctx.sbuf(f"fmt{i}", [128, D], BF16) for i in range(2)]
        stats = ctx.sbuf("stats", [128, 8], F32)
        scratch = ctx.sbuf("scr", [128, D], F32)
        tp = [ctx.psum(f"tp{i}", [128, 512], BF16) for i in range(2)]

        ld = ctx.sem("ld"); st = ctx.sem("st"); vdone = ctx.sem("vd")
        sdone = ctx.sem("sd"); tdone = ctx.sem("td"); ev = ctx.sem("ev")

        ntiles = TC // 128
        ttr_done = [0] * ntiles
        aff_done = [0] * ntiles
        xn_cons = [0] * ntiles
        xn_tr = [0] * ntiles
        fm_st = [0] * ntiles
        ev_done = [[0] * ntiles, [0] * ntiles]

        for i in range(ntiles):
            p = i % 2
            x_sb, xn_sb, fm_sb = xt[p], xnt[p], fmt[p]
            w = []
            if i >= 2:
                w = [(vdone, ttr_done[i - 2]), (sdone, aff_done[i - 2])]
            rec.add("sync", lambda e, i=i, x_sb=x_sb: e.dma_start(
                out=x_sb[:], in_=x_h[i * 128:(i + 1) * 128, :]), waits=w, incs=[(ld, 16)])
            ldv = ld.inc(16)

            rec.add("vector", lambda e, x_sb=x_sb: e.reduce_sum(
                stats[:, 0:1], x_sb[:], AX.X), waits=[(ld, ldv), (ctx.sem("cst_ready"), ctx.cst_target)])
            rec.add("scalar", lambda e, x_sb=x_sb: e.activation(
                scratch[:], x_sb[:], AF.Square), waits=[(ld, ldv)], incs=[(sdone, 1)])
            s_sq = sdone.inc()
            rec.add("vector", lambda e: e.reduce_sum(
                stats[:, 1:2], scratch[:], AX.X), waits=[(sdone, s_sq)], incs=[(vdone, 1)])
            ttr_done[i] = vdone.inc()
            rec.add("vector", lambda e: e.tensor_scalar(
                stats[:, 2:3], stats[:, 0:1], ctx.c("inv1024"), None, ALU.mult), incs=[(vdone, 1)])
            v_mu = vdone.inc()
            rec.add("vector", lambda e: e.tensor_scalar(
                stats[:, 3:4], stats[:, 1:2], ctx.c("inv1024"), None, ALU.mult))
            rec.add("vector", lambda e: e.scalar_tensor_tensor(
                stats[:, 4:5], stats[:, 2:3], stats[:, 2:3], stats[:, 3:4],
                ALU.mult, ALU.subtract), waits=[(vdone, v_mu)], incs=[(vdone, 1)])
            v_t4 = vdone.inc()
            rec.add("scalar", lambda e: e.activation(
                stats[:, 5:6], stats[:, 4:5], AF.Sqrt, bias=ctx.c("eps"),
                scale=ctx.c("neg1")), waits=[(vdone, v_t4)], incs=[(sdone, 1)])
            s_std = sdone.inc()
            rec.add("vector", lambda e: e.reciprocal(
                stats[:, 6:7], stats[:, 5:6]), waits=[(sdone, s_std)], incs=[(vdone, 1)])
            v_rec = vdone.inc()
            rec.add("vector", lambda e: e.tensor_scalar(
                stats[:, 7:8], stats[:, 2:3], stats[:, 6:7], ctx.c("neg1"),
                ALU.mult, ALU.mult), waits=[(vdone, v_rec)], incs=[(vdone, 1)])
            v_mb = vdone.inc()
            w = [(vdone, v_mb)]
            if i >= 2:
                w += [(st, xn_cons[i - 2]), (tdone, xn_tr[i - 2])]
            rec.add("scalar", lambda e, x_sb=x_sb, xn_sb=xn_sb: e.activation(
                xn_sb[:], x_sb[:], AF.Identity, bias=stats[:, 7:8], scale=stats[:, 6:7]),
                waits=w, incs=[(sdone, 1)])
            s_aff = sdone.inc()
            aff_done[i] = s_aff
            rec.add("sync", lambda e, i=i, xn_sb=xn_sb: e.dma_start(
                out=xn_tm_h[i * 128:(i + 1) * 128, :], in_=xn_sb[:]),
                waits=[(sdone, s_aff)], incs=[(st, 16)])
            xn_cons[i] = st.inc(16)
            for half in range(2):
                pb = tp[half]
                for j in range(4):
                    c = half * 4 + j
                    wv = []
                    if j == 0:
                        wv.append((sdone, s_aff))
                        if i >= 1:
                            wv.append((ev, ev_done[half][i - 1]))
                    rec.add("tensor", lambda e, xn_sb=xn_sb, c=c, pb=pb, j=j: e.transpose(
                        pb[:, j * 128:(j + 1) * 128], xn_sb[:, c * 128:(c + 1) * 128], idn[:]),
                        waits=wv, incs=[(tdone, 1)] if j == 3 else [])
                td = tdone.inc()
                wv = [(tdone, td)]
                if i >= 2 and half == 0:
                    wv.append((st, fm_st[i - 2]))
                rec.add("vector", lambda e, pb=pb, fm_sb=fm_sb, half=half: e.tensor_copy(
                    fm_sb[:, half * 512:(half + 1) * 512], pb[:]),
                    waits=wv, incs=[(ev, 1)])
                ev_done[half][i] = ev.inc()
            xn_tr[i] = tdone.n
            rec.add("sync", lambda e, i=i, fm_sb=fm_sb: e.dma_start(
                out=xn_fm_h[:, i * 128:(i + 1) * 128].rearrange("(c f) t -> f c t", f=128),
                in_=fm_sb[:].rearrange("f (c t) -> f c t", t=128)),
                waits=[(ev, ev_done[1][i])], incs=[(st, 16)])
            fm_st[i] = st.inc(16)
        rec.add("sync", lambda e: e.nop(), waits=[(st, st.n)])

def build_ln_phase():
    ctx = Ctx("emit_ln")
    nc = ctx.nc
    nc = ctx.nc
    x_h = nc.dram_tensor("x", [TC, D], F32, kind="ExternalInput").ap()
    xn_tm_h = nc.dram_tensor("xn_tm", [TC, D], BF16, kind="ExternalOutput").ap()
    xn_fm_h = nc.dram_tensor("xn_fm", [D, TC], BF16, kind="ExternalOutput").ap()

    io = {
        "x": x_h,
        "xn_tm": xn_tm_h,
        "xn_fm": xn_fm_h,
    }
    with ctx.es:
        emit_ln(ctx, io)
        finish(ctx.nc, ctx.rec, ctx.es, ctx)
    return ctx.nc


def _ttiles(total, span=512):
    out = []
    t = 0
    while t < total:
        s = min(span, total - t)
        out.append((t, s))
        t += s
    return out


def emit_slot_qkv(ctx, io):
    nc = ctx.nc
    rec = ctx.rec
    xn_fm_h = io["xn_fm"]
    w_h = io["Wqkv"]
    b_h = io["bqkv"]
    q_fm_h = io["q_fm"]
    k_fm_h = io["k_fm"]
    v65_h = io["v65"]
    if True:
        ctx.setup_consts()
        rec = ctx.rec
        wq = ctx.sbuf("wq", [128, 8, 3 * D], BF16)
        bqc = ctx.sbuf("bqc", [128, 24], F32)
        bqrow = ctx.sbuf("bqrow", [1, D], F32)
        ones_row = ctx.sbuf("ones_row", [1, 128], F32)
        xn_sb = [ctx.sbuf(f"xn{i}", [128, 8, 512], BF16) for i in range(2)]
        stage = [ctx.sbuf(f"stg{i}", [128, 512], BF16) for i in range(2)]
        v_sb = [ctx.sbuf(f"vsb{i}", [128, SLOT_H, 65], BF16) for i in range(2)]
        ps = [ctx.psum(f"ps{i}", [128, 512]) for i in range(2)]

        ld = ctx.sem("ld"); wrdy = ctx.sem("wrdy"); st = ctx.sem("st")
        td = ctx.sem("td"); ev = ctx.sem("ev")

        rec.add("gpsimd", lambda e: e.dma_start(
            out=wq[:], in_=w_h.rearrange("(c p) e -> p c e", p=128)), incs=[(wrdy, 16)])
        rec.add("sync", lambda e: e.dma_start(
            out=bqc[:], in_=b_h.rearrange("(c p) -> p c", p=128),
            allow_slow_non_contiguous=True), incs=[(wrdy, 16)])
        rec.add("sync", lambda e: e.dma_start(
            out=bqrow[:], in_=b_h[2048:3072].rearrange("(o e) -> o e", o=1)), incs=[(wrdy, 16)])
        wv = wrdy.inc(48)
        rec.add("vector", lambda e: e.memset(ones_row[:], 1.0))
        rec.add("vector", lambda e: e.tensor_scalar(
            wq[:, :, 0:1024], wq[:, :, 0:1024], ctx.c("eighth"), None, ALU.mult),
            waits=[(wrdy, wv), (ctx.sem("cst_ready"), ctx.cst_target)])
        rec.add("vector", lambda e: e.tensor_scalar(
            bqc[:, 0:8], bqc[:, 0:8], ctx.c("eighth"), None, ALU.mult), incs=[(wrdy, 1)])
        wv = wrdy.inc()

        tt = _ttiles(TC, 512)
        ev_after = {}
        ld_after = {}
        for it, (t0, sp) in enumerate(tt):
            p = it % 2
            xs = xn_sb[p]
            w = [(wrdy, wv)]
            if it >= 2:
                w.append((td, ld_after[it - 2]))
            rec.add("sync", lambda e, t0=t0, sp=sp, xs=xs: e.dma_start(
                out=xs[:, :, 0:sp],
                in_=xn_fm_h[:, t0:t0 + sp].rearrange("(c p) t -> p c t", p=128)),
                waits=w, incs=[(ld, 16)])
            lv = ld.inc(16)
            for m in range(16):
                pp = ps[m % 2]
                wm = [(ld, lv)]
                if it > 0 or m >= 2:
                    wm.append((ev, ev_after[(it, m - 2) if m >= 2 else (it - 1, 14 + m)]))
                for c in range(8):
                    rec.add("tensor", lambda e, pp=pp, m=m, c=c, xs=xs, sp=sp: e.matmul(
                        pp[:, 0:sp], wq[:, c, m * 128:(m + 1) * 128], xs[:, c, 0:sp],
                        start=(c == 0), stop=(c == 7)),
                        waits=wm if c == 0 else [], incs=[(td, 1)] if c == 7 else [])
                tv = td.inc()
                if m == 15:
                    ld_after[it] = tv
                stg = stage[m % 2]
                rec.add("vector", lambda e, pp=pp, m=m, stg=stg, sp=sp: e.tensor_scalar(
                    stg[:, 0:sp], pp[:, 0:sp], bqc[:, m:m + 1], None, ALU.add),
                    waits=[(td, tv)] + ([(st, ev_after.get(("st", it, m - 2),
                           ev_after.get(("st", it - 1, 14 + m), 0)))] if (it > 0 or m >= 2) else []),
                    incs=[(ev, 1)])
                evv = ev.inc()
                ev_after[(it, m)] = evv
                dst = q_fm_h if m < 8 else k_fm_h
                mm = m % 8
                rec.add("sync", lambda e, dst=dst, mm=mm, t0=t0, sp=sp, stg=stg: e.dma_start(
                    out=dst[mm * 128:(mm + 1) * 128, t0:t0 + sp], in_=stg[:, 0:sp]),
                    waits=[(ev, evv)], incs=[(st, 16)])
                ev_after[("st", it, m)] = st.inc(16)

        for i in range(TC // 128):
            p = i % 2
            xs, vs = xn_sb[p], v_sb[p]
            w = []
            if i >= 2:
                w.append((td, ld_after[("v", i - 2)]))
            else:
                w.append((td, ld_after[len(_ttiles(TC, 512)) - 1]))
            rec.add("sync", lambda e, i=i, xs=xs: e.dma_start(
                out=xs[:, :, 0:128],
                in_=xn_fm_h[:, i * 128:(i + 1) * 128].rearrange("(c p) t -> p c t", p=128)),
                waits=w, incs=[(ld, 16)])
            lv = ld.inc(16)
            if i < 2:
                rec.add("vector", lambda e, vs=vs: e.memset(vs[:, :, 64:65], 1.0))
            for half in range(2):
                pp = ps[half]
                wm = [(ld, lv)]
                if i > 0:
                    wm.append((ev, ev_after[("v", i - 1, half)]))
                else:
                    last_it = len(_ttiles(TC, 512)) - 1
                    wm.append((ev, ev_after[(last_it, 14 + half)]))
                for c in range(8):
                    rec.add("tensor", lambda e, pp=pp, c=c, half=half, xs=xs: e.matmul(
                        pp[:], xs[:, c, 0:128], wq[:, c, 2048 + half * 512: 2048 + (half + 1) * 512],
                        start=(c == 0), stop=False), waits=wm if c == 0 else [])
                rec.add("tensor", lambda e, pp=pp, half=half: e.matmul(
                    pp[:], ones_row[:], bqrow[:, half * 512:(half + 1) * 512],
                    start=False, stop=True), incs=[(td, 1)])
                tv = td.inc()
                if half == 1:
                    ld_after[("v", i)] = tv
                wv3 = [(td, tv)]
                if i >= 2:
                    wv3.append((st, ev_after[("vst", i - 2)]))
                rec.add("vector", lambda e, pp=pp, vs=vs, half=half: e.tensor_copy(
                    vs[:, half * 8:(half + 1) * 8, 0:64],
                    pp[:].rearrange("p (h e) -> p h e", e=64)), waits=wv3, incs=[(ev, 1)])
                ev_after[("v", i, half)] = ev.inc()
            rec.add("sync", lambda e, i=i, vs=vs: e.dma_start(
                out=v65_h[i * 128:(i + 1) * 128, :], in_=vs[:].rearrange("p h e -> p (h e)")),
                waits=[(ev, ev_after[("v", i, 1)])], incs=[(st, 16)])
            ev_after[("vst", i)] = st.inc(16)
        rec.add("sync", lambda e: e.nop(), waits=[(st, st.n)])

def build_slot_qkv_phase():
    ctx = Ctx("emit_slot_qkv")
    nc = ctx.nc
    nc = ctx.nc
    xn_fm_h = nc.dram_tensor("xn_fm", [D, TC], BF16, kind="ExternalInput").ap()
    w_h = nc.dram_tensor("Wqkv", [D, 3 * D], F32, kind="ExternalInput").ap()
    b_h = nc.dram_tensor("bqkv", [3 * D], F32, kind="ExternalInput").ap()
    q_fm_h = nc.dram_tensor("q_fm", [D, TC], BF16, kind="ExternalOutput").ap()
    k_fm_h = nc.dram_tensor("k_fm", [D, TC], BF16, kind="ExternalOutput").ap()
    v65_h = nc.dram_tensor("v65", [TC, SLOT_H * 65], BF16, kind="ExternalOutput").ap()

    io = {
        "xn_fm": xn_fm_h,
        "Wqkv": w_h,
        "bqkv": b_h,
        "q_fm": q_fm_h,
        "k_fm": k_fm_h,
        "v65": v65_h,
    }
    with ctx.es:
        emit_slot_qkv(ctx, io)
        finish(ctx.nc, ctx.rec, ctx.es, ctx)
    return ctx.nc


NPOS_NEAR = 128 * 384
NPOS = NPOS_NEAR + 4096
FAR_KS = [3, 4, 6, 8, 12]


def make_slot_mconst():
    oi = {d: i for i, d in enumerate(OFFSETS)}
    M = np.zeros((44, NPOS), dtype=np.float32)
    j = np.arange(128)[:, None]
    c = np.arange(128)[None, :]
    for kidx in range(3):
        delta = 128 * kidx + c - j
        for d, i in oi.items():
            jj, cc = np.nonzero(delta == d)
            M[i, jj * 384 + kidx * 128 + cc] = 1.0
    for f, k in enumerate(FAR_KS):
        M[oi[128 * k], NPOS_NEAR + f] = 1.0
    return M.astype(BF)


def emit_slot_attn(ctx, io, dbg_tile=None, dbg_head=0):
    nc = ctx.nc
    rec = ctx.rec
    q_h = io["q_fm"]
    k_h = io["k_fm"]
    v_h = io["v65"]
    pb_h = io["pos_bias"]
    mc_h = io["mconst"]
    msk_h = io["kvmask"]
    flat_h = io["flat"]
    ball_h = io["ball_scratch"]
    dbgS_h = io.get("dbgS")
    dbgq_h = io.get("dbgq")
    dbgk_h = io.get("dbgk")
    dbgE_h = io.get("dbgE")
    dbgG_h = io.get("dbgG")
    dbgO_h = io.get("dbgO")
    dbgB_h = io.get("dbgB")
    if True:
        ctx.setup_consts()
        idn = build_identity_bf16(ctx, io.get("ident"))
        rec = ctx.rec
        ld = ctx.sem("ld"); st = ctx.sem("st"); td = ctx.sem("td")
        vd = ctx.sem("vd"); sd = ctx.sem("sd"); ev = ctx.sem("ev")
        gld = ctx.sem("gld")

        pb_sb = ctx.sbuf("pb", [44, 16], F32)
        epb = ctx.sbuf("epb", [44, 16], BF16)
        MCW = 4096
        mc_sb = [ctx.sbuf(f"mc{j}", [44, MCW], BF16) for j in range(2)]
        bst = ctx.sbuf("bst", [16, 512], F32)
        bps = ctx.psum("bps", [16, 512])
        rec.add("sync", lambda e: e.dma_start(out=pb_sb[:], in_=pb_h), incs=[(ld, 16)])
        lv0 = ld.inc(16)
        rec.add("scalar", lambda e: e.activation(epb[:], pb_sb[:], AF.Exp),
                waits=[(ld, lv0)], incs=[(sd, 1)])
        sd_epb = sd.inc()
        mm_done = {}
        for j in range(NPOS // MCW):
            mcb = mc_sb[j % 2]
            w = [(td, mm_done[j - 2])] if j >= 2 else []
            rec.add("gpsimd", lambda e, j=j, mcb=mcb: e.dma_start(
                out=mcb[:], in_=mc_h[:, j * MCW:(j + 1) * MCW]), waits=w, incs=[(gld, 16)])
            lvj = gld.inc(16)
            for i2 in range(MCW // 512):
                i = j * (MCW // 512) + i2
                rec.add("tensor", lambda e, i2=i2, mcb=mcb: e.matmul(
                    bps[:], epb[:], mcb[:, i2 * 512:(i2 + 1) * 512], start=True, stop=True),
                    waits=[(sd, sd_epb), (gld, lvj)] + ([(ev, ctx.sems["ev"].n)] if i >= 1 else []),
                    incs=[(td, 1)])
                tv = td.inc()
                mm_done[j] = tv
                rec.add("vector", lambda e: e.tensor_copy(bst[:], bps[:]),
                        waits=[(td, tv)] + ([(st, ctx.sems["st"].n)] if i >= 1 else []),
                        incs=[(ev, 1)])
                evv = ev.inc()
                rec.add("sync", lambda e, i=i: e.dma_start(
                    out=ball_h[:, i * 512:(i + 1) * 512], in_=bst[:]),
                    waits=[(ev, evv)], incs=[(st, 16)])
                st.inc(16)
        bnear = ctx.sbuf("bnear", [128, 16, 1024], BF16)
        pbf = ctx.sbuf("pbf", [128, 16, len(FAR_KS)], F32)
        wb = [(st, st.n)]
        for h in range(16):
            rec.add("gpsimd", lambda e, h=h: e.dma_start(
                out=bnear[:, h, 0:384],
                in_=ball_h[h, 0:NPOS_NEAR].rearrange("(j c) -> j c", j=128)),
                waits=wb if h == 0 else [], incs=[(gld, 16)])
            gld.inc(16)
        rec.add("sync", lambda e: e.dma_start(
            out=pbf[:],
            in_=ball_h[:, NPOS_NEAR:NPOS_NEAR + len(FAR_KS)].unsqueeze(0).broadcast_to(
                [128, 16, len(FAR_KS)])), waits=wb, incs=[(ld, 16)])
        ld.inc(16)
        bf_gld = ctx.sems["gld"].n
        bf_ld = ctx.sems["ld"].n
        for h in range(16):
            for f in range(len(FAR_KS)):
                rec.add("vector", lambda e, h=h, f=f: e.tensor_scalar(
                    bnear[:, h, 384 + f * 128:384 + (f + 1) * 128], idn[:],
                    pbf[:, h, f:f + 1], None, ALU.mult),
                    waits=[(ld, bf_ld), (gld, bf_gld),
                           (ctx.sem("ident_rdy"), 16)] if (h == 0 and f == 0) else [],
                    incs=[(vd, 1)] if (h == 15 and f == 4) else [])
        bfar_vd = ctx.sem("vd").inc()
        mask_sb = ctx.sbuf("masks", [128, TC // 128], F32)
        rec.add("sync", lambda e: e.dma_start(
            out=mask_sb[:], in_=msk_h.rearrange("(i p) -> p i", p=128),
            allow_slow_non_contiguous=True), incs=[(ld, 16)])
        prep_ld = ld.inc(16)
        prep_gld = gld.n

        RING = 16
        q_sb = [ctx.sbuf(f"qsb{i}", [64, 16, 128], BF16) for i in range(2)]
        k_sb = ctx.sbuf("kring", [64, 16, RING * 128], BF16)
        v_sb = ctx.sbuf("vring", [128, RING, SLOT_H * 65], BF16)
        E_sb = [ctx.sbuf(f"esb{i}", [128, 1024], BF16) for i in range(2)]
        G_sb = [ctx.sbuf(f"gsb{i}", [128, 1024], BF16) for i in range(2)]
        o_stage = ctx.sbuf("ostg", [128, SLOT_H, 65], F32)
        zcol = ctx.sbuf("zcol", [128, 2 * SLOT_H], F32)
        tmpc = ctx.sbuf("tmpc", [128, 8], F32)
        flat_sb = [ctx.sbuf(f"flt{i}", [128, D], BF16) for i in range(2)]
        ps_e = [ctx.psum(f"pse{i}", [128, 1024]) for i in range(2)]
        o_ps = [ctx.psum(f"ops{i}", [128, 512]) for i in range(2)]

        tiles = list(range(HALO // 128, TC // 128))
        dbg_barrier = [0]
        tile_mm_done = {}
        tile_pv_done = {}
        exp_done = {}
        g_done = {}
        oev_done = {}
        flat_st = {}
        first_i = tiles[0]
        pre0 = first_i - 12
        assert (pre0 % RING) + 12 <= RING
        rec.add("sync", lambda e: e.dma_start(
            out=k_sb[:, :, (pre0 % RING) * 128:((pre0 % RING) + 12) * 128],
            in_=k_h[:, pre0 * 128:(pre0 + 12) * 128].rearrange(
                "(h p) t -> p h t", p=64)), incs=[(ld, 16)])
        ld.inc(16)
        rec.add("gpsimd", lambda e: e.dma_start(
            out=v_sb[:, pre0 % RING:(pre0 % RING) + 12, :],
            in_=v_h[pre0 * 128:(pre0 + 12) * 128, :].rearrange(
                "(b p) e -> p b e", p=128)), incs=[(gld, 16)])
        gld.inc(16)
        for ti, i in enumerate(tiles):
            p = ti % 2
            qs, ks, vs = q_sb[p], k_sb, v_sb
            blocks = [i - k for k in SLOT_BLOCKS]
            w = []
            if ti >= 2:
                w = [(td, tile_pv_done[ti - 2])]
            rec.add("sync", lambda e, i=i, qs=qs: e.dma_start(
                out=qs[:], in_=q_h[:, i * 128:(i + 1) * 128].rearrange(
                    "(h p) t -> p h t", p=64)), waits=w, incs=[(ld, 16)])
            wr = []
            if ti - 4 >= 0:
                wr = [(td, tile_pv_done[ti - 4])]
            rec.add("sync", lambda e, i=i: e.dma_start(
                out=k_sb[:, :, (i % RING) * 128:((i % RING) + 1) * 128],
                in_=k_h[:, i * 128:(i + 1) * 128].rearrange(
                    "(h p) t -> p h t", p=64)), waits=wr, incs=[(ld, 16)])
            rec.add("gpsimd", lambda e, i=i: e.dma_start(
                out=v_sb[:, i % RING, :], in_=v_h[i * 128:(i + 1) * 128, :]),
                waits=wr, incs=[(gld, 16)])
            lv = ld.inc(32)
            lvg = gld.inc(16)
            for h in range(16):
                hp = h % 2
                pe = ps_e[hp]
            
                wm = [(ld, lv), (gld, lvg)]
                if (ti, h - 2) in exp_done:
                    wm.append((sd, exp_done[(ti, h - 2)]))
                elif ti > 0 and (ti - 1, 14 + h) in exp_done:
                    wm.append((sd, exp_done[(ti - 1, 14 + h)]))
                for kidx in range(8):
                    rs = blocks[kidx] % RING
                    rec.add("tensor", lambda e, pe=pe, kidx=kidx, rs=rs, ks=ks, qs=qs, h=h: e.matmul(
                        pe[:, kidx * 128:(kidx + 1) * 128],
                        ks[:, h, rs * 128:(rs + 1) * 128],
                        qs[:, h, :], start=True, stop=True),
                        waits=wm if kidx == 0 else [], incs=[(td, 1)] if kidx == 7 else [])
                t_mm = td.inc()
                if h == 15:
                    tile_mm_done[ti] = t_mm
                Eb = E_sb[hp]
                we = [(td, t_mm)]
                if dbg_barrier[0]:
                    we.append((st, dbg_barrier[0]))
                if (ti, h - 2) in g_done:
                    we.append((vd, g_done[(ti, h - 2)]))
                elif ti > 0 and (ti - 1, 14 + h) in g_done:
                    we.append((vd, g_done[(ti - 1, 14 + h)]))
                rec.add("scalar", lambda e, pe=pe, Eb=Eb: e.activation(
                    Eb[:], pe[:], AF.Exp), waits=we, incs=[(sd, 1)])
                exp_done[(ti, h)] = sd.inc()
                Gb = G_sb[hp]
                wg = [(sd, exp_done[(ti, h)])]
                if (ti, h - 2) in tile_pv_done:
                    wg.append((td, tile_pv_done[(ti, h - 2)]))
                elif ti > 0 and (ti - 1, 14 + h) in tile_pv_done:
                    wg.append((td, tile_pv_done[(ti - 1, 14 + h)]))
                gops = []
                if ti == 0 and h == 0:
                    wg = wg + [(vd, bfar_vd)]
                gops.append((lambda e, Eb=Eb, Gb=Gb, h=h: e.tensor_tensor(
                    Gb[:, 0:1024], Eb[:, 0:1024], bnear[:, h, :], ALU.mult), wg))
                for kidx in range(8):
                    bu = blocks[kidx]
                    if bu < HALO // 128:
                        gops.append((lambda e, Gb=Gb, kidx=kidx, bu=bu: e.tensor_scalar(
                            Gb[:, kidx * 128:(kidx + 1) * 128],
                            Gb[:, kidx * 128:(kidx + 1) * 128],
                            mask_sb[:, bu:bu + 1], None, ALU.mult), []))
                for oi_, (fn_, w_) in enumerate(gops):
                    last = oi_ == len(gops) - 1
                    rec.add("vector", fn_, waits=w_, incs=[(vd, 1)] if last else [])
                g_done[(ti, h)] = vd.inc()
                if dbg_tile is not None and i == dbg_tile and h == dbg_head:
                    scrS = ctx.sbuf("scrS", [128, 1024], F32)
                    rec.add("vector", lambda e, pe=pe: e.tensor_copy(scrS[:], pe[:]),
                            waits=[(td, t_mm)], incs=[(vd, 1)])
                    v_ps = vd.inc()
                    rec.add("sync", lambda e: e.dma_start(out=dbgS_h, in_=scrS[:]),
                            waits=[(vd, v_ps)], incs=[(st, 16)])
                    st.inc(16)
                    rec.add("sync", lambda e, qs=qs: e.dma_start(out=dbgq_h, in_=qs[:]),
                            waits=[(td, t_mm)], incs=[(st, 16)])
                    st.inc(16)
                    rec.add("sync", lambda e, ks=ks: e.dma_start(out=dbgk_h, in_=ks[:]),
                            waits=[(td, t_mm)], incs=[(st, 16)])
                    st.inc(16)
                    rec.add("sync", lambda e, Eb=Eb: e.dma_start(out=dbgE_h, in_=Eb[:]),
                            waits=[(vd, g_done[(ti, h)])], incs=[(st, 16)])
                    st.inc(16)
                    rec.add("sync", lambda e, Gb=Gb: e.dma_start(out=dbgG_h, in_=Gb[:]),
                            waits=[(vd, g_done[(ti, h)])], incs=[(st, 16)])
                    st.inc(16)
                    rec.add("sync", lambda e: e.dma_start(out=dbgB_h, in_=bnear[:]),
                            incs=[(st, 16)])
                    st.inc(16)
                    dbg_barrier[0] = st.n
                def emit_pv(hh, Gbb):
                    po = o_ps[hh % 2]
                    wp = [(vd, g_done[(ti, hh)])]
                    if (ti, hh - 2) in oev_done:
                        wp.append((ev, oev_done[(ti, hh - 2)]))
                    elif ti > 0 and (ti - 1, 14 + hh) in oev_done:
                        wp.append((ev, oev_done[(ti - 1, 14 + hh)]))
                    for kidx in range(8):
                        rs = blocks[kidx] % RING
                        rec.add("tensor", lambda e, po=po, Gbb=Gbb, kidx=kidx, rs=rs, vs=vs, hh=hh: e.matmul(
                            po[:, 0:65], Gbb[:, kidx * 128:(kidx + 1) * 128],
                            vs[:, rs, 65 * hh:65 * hh + 65],
                            start=(kidx == 0), stop=(kidx == 7)),
                            waits=wp if kidx == 0 else [], incs=[(td, 1)] if kidx == 7 else [])
                    tile_pv_done[(ti, hh)] = td.inc()
                    if hh == 15:
                        tile_pv_done[ti] = tile_pv_done[(ti, hh)]
                    rec.add("vector", lambda e, po=po, hh=hh: e.tensor_copy(
                        o_stage[:, hh, :], po[:, 0:65]),
                        waits=[(td, tile_pv_done[(ti, hh)])], incs=[(ev, 1)])
                    oev_done[(ti, hh)] = ev.inc()

                if h >= 1:
                    emit_pv(h - 1, G_sb[(h - 1) % 2])
                if h == 15:
                    emit_pv(15, Gb)
            if dbg_tile is not None and i == dbg_tile:
                rec.add("sync", lambda e: e.dma_start(
                    out=dbgO_h, in_=o_stage[:].rearrange("p h e -> p (h e)")),
                    waits=[(ev, oev_done[(ti, 15)])], incs=[(st, 16)])
                st.inc(16)
                dbg_barrier[0] = st.n
            fb = flat_sb[p]
            wn = [(ev, oev_done[(ti, 15)])]
            if ti >= 2:
                wn.append((st, flat_st[ti - 2]))
            rec.add("vector", lambda e: e.tensor_scalar(
                zcol[:, 0:16], o_stage[:, :, 64], ctx.c("tiny"), None, ALU.max),
                waits=wn, incs=[(vd, 1)])
            v_z = vd.inc()
            rec.add("vector", lambda e: e.reciprocal(
                zcol[:, 16:32], zcol[:, 0:16]), waits=[(vd, v_z)], incs=[(vd, 1)])
            v_rz = vd.inc()
            rec.add("vector", lambda e, fb=fb: e.tensor_tensor(
                fb[:].rearrange("p (h d) -> p h d", d=64),
                o_stage[:, :, 0:64],
                zcol[:, 16:32].unsqueeze(2).broadcast_to([128, 16, 64]),
                ALU.mult), waits=[(vd, v_rz)], incs=[(vd, 1)])
            v_fl = vd.inc()
            rec.add("sync", lambda e, i=i, fb=fb: e.dma_start(
                out=flat_h[i * 128:(i + 1) * 128, :], in_=fb[:]),
                waits=[(vd, v_fl)], incs=[(st, 16)])
            flat_st[ti] = st.inc(16)
        rec.add("sync", lambda e: e.nop(), waits=[(st, st.n)])

def build_slot_attn_phase(dbg_tile=None, dbg_head=0):
    ctx = Ctx("emit_slot_attn")
    nc = ctx.nc
    nc = ctx.nc
    q_h = nc.dram_tensor("q_fm", [D, TC], BF16, kind="ExternalInput").ap()
    k_h = nc.dram_tensor("k_fm", [D, TC], BF16, kind="ExternalInput").ap()
    v_h = nc.dram_tensor("v65", [TC, SLOT_H * 65], BF16, kind="ExternalInput").ap()
    pb_h = nc.dram_tensor("pos_bias", [44, 16], F32, kind="ExternalInput").ap()
    mc_h = nc.dram_tensor("mconst", [44, NPOS], BF16, kind="ExternalInput").ap()
    msk_h = nc.dram_tensor("kvmask", [TC], F32, kind="ExternalInput").ap()
    flat_h = nc.dram_tensor("flat", [TC, D], BF16, kind="ExternalOutput").ap()
    ball_h = nc.dram_tensor("ball_scratch", [16, NPOS], F32, kind="Internal").ap()
    if dbg_tile is not None:
        dbgS_h = nc.dram_tensor("dbgS", [128, 1024], F32, kind="ExternalOutput").ap()
        dbgq_h = nc.dram_tensor("dbgq", [64, 16, 128], BF16, kind="ExternalOutput").ap()
        dbgk_h = nc.dram_tensor("dbgk", [64, 16, 1024], BF16, kind="ExternalOutput").ap()
        dbgE_h = nc.dram_tensor("dbgE", [128, 1024], BF16, kind="ExternalOutput").ap()
        dbgG_h = nc.dram_tensor("dbgG", [128, 1024], BF16, kind="ExternalOutput").ap()
        dbgO_h = nc.dram_tensor("dbgO", [128, SLOT_H * 65], F32, kind="ExternalOutput").ap()
        dbgB_h = nc.dram_tensor("dbgB", [128, 16, 384], BF16, kind="ExternalOutput").ap()

    io = {
        "q_fm": q_h,
        "k_fm": k_h,
        "v65": v_h,
        "pos_bias": pb_h,
        "mconst": mc_h,
        "kvmask": msk_h,
        "flat": flat_h,
        "ball_scratch": ball_h,
    }
    if dbg_tile is not None:
        io.update({"dbgS": dbgS_h, "dbgq": dbgq_h, "dbgk": dbgk_h,
                   "dbgE": dbgE_h, "dbgG": dbgG_h, "dbgO": dbgO_h, "dbgB": dbgB_h})
    with ctx.es:
        emit_slot_attn(ctx, io, dbg_tile=dbg_tile, dbg_head=dbg_head)
        finish(ctx.nc, ctx.rec, ctx.es, ctx)
    return ctx.nc


def emit_gate(ctx, io):
    nc = ctx.nc
    rec = ctx.rec
    xn_fm_h = io["xn_fm"]
    flat_h = io["flat"]
    wg_h = io["Wgate"]
    bg_h = io["bgate"]
    wo_h = io["Wout"]
    bo_h = io["bout"]
    s_h = io["s_tm"]
    if True:
        ctx.setup_consts()
        idn = build_identity_bf16(ctx, io.get("ident"))
        rec = ctx.rec
        ld = ctx.sem("ld"); st = ctx.sem("st"); td = ctx.sem("td")
        vd = ctx.sem("vd"); sd = ctx.sem("sd"); ev = ctx.sem("ev")

        wg = ctx.sbuf("wg", [128, 8, D], BF16)
        wo = ctx.sbuf("wo", [128, 8, D], BF16)
        bgrow = ctx.sbuf("bgrow", [1, D], F32)
        borow = ctx.sbuf("borow", [1, D], F32)
        ones_row = ctx.sbuf("ones_row", [1, 128], F32)
        rec.add("gpsimd", lambda e: e.dma_start(
            out=wg[:], in_=wg_h.rearrange("(c p) e -> p c e", p=128)), incs=[(ld, 16)])
        rec.add("gpsimd", lambda e: e.dma_start(
            out=wo[:], in_=wo_h.rearrange("(c p) e -> p c e", p=128)), incs=[(ld, 16)])
        rec.add("sync", lambda e: e.dma_start(
            out=bgrow[:], in_=bg_h.rearrange("(o e) -> o e", o=1)), incs=[(ld, 16)])
        rec.add("sync", lambda e: e.dma_start(
            out=borow[:], in_=bo_h.rearrange("(o e) -> o e", o=1)), incs=[(ld, 16)])
        rec.add("vector", lambda e: e.memset(ones_row[:], 1.0), incs=[(vd, 1)])
        prep_ld = ld.inc(64)
        prep_vd = vd.inc()

        xn_sb = [ctx.sbuf(f"xn{i}", [128, 8, 128], BF16) for i in range(2)]
        fl_sb = [ctx.sbuf(f"fl{i}", [128, D], BF16) for i in range(2)]
        gt_sb = [ctx.sbuf(f"gt{i}", [128, D], BF16) for i in range(2)]
        fg_sb = [ctx.sbuf(f"fg{i}", [128, 8, 128], BF16) for i in range(2)]
        s_sb = [ctx.sbuf(f"ssb{i}", [128, D], BF16) for i in range(2)]
        gps = [ctx.psum(f"gps{i}", [128, 512]) for i in range(2)]
        tps = ctx.psum("tps", [128, 1024], BF16)
        sps = [ctx.psum(f"sps{i}", [128, 512]) for i in range(2)]

        tiles = list(range(HALO // 128, TC // 128))
        trk = {}
        for ti, i in enumerate(tiles):
            p = ti % 2
            xs, fl, gt, fg, ss = xn_sb[p], fl_sb[p], gt_sb[p], fg_sb[p], s_sb[p]
            w = [(ld, prep_ld)] if ti == 0 else []
            if ti >= 2:
                w = [(td, trk[("smm", ti - 2)])]
            rec.add("sync", lambda e, i=i, xs=xs: e.dma_start(
                out=xs[:], in_=xn_fm_h[:, i * 128:(i + 1) * 128].rearrange(
                    "(c p) t -> p c t", p=128)), waits=w, incs=[(ld, 16)])
            rec.add("sync", lambda e, i=i, fl=fl: e.dma_start(
                out=fl[:], in_=flat_h[i * 128:(i + 1) * 128, :]), incs=[(ld, 16)])
            lv = ld.inc(32)
            for half in range(2):
                pg = gps[half]
                wm = [(ld, lv)]
                if ti >= 1:
                    wm.append((sd, trk[("sig", ti - 1, half)]))
                for c in range(8):
                    rec.add("tensor", lambda e, pg=pg, xs=xs, c=c, half=half: e.matmul(
                        pg[:], xs[:, c, :], wg[:, c, half * 512:(half + 1) * 512],
                        start=(c == 0), stop=False), waits=wm if c == 0 else [])
                rec.add("tensor", lambda e, pg=pg, half=half: e.matmul(
                    pg[:], ones_row[:], bgrow[:, half * 512:(half + 1) * 512],
                    start=False, stop=True), incs=[(td, 1)])
                tv = td.inc()
                rec.add("scalar", lambda e, pg=pg, gt=gt, half=half: e.activation(
                    gt[:, half * 512:(half + 1) * 512], pg[:], AF.Sigmoid),
                    waits=[(td, tv)], incs=[(sd, 1)])
                trk[("sig", ti, half)] = sd.inc()
            wfg = [(sd, trk[("sig", ti, 1)])]
            if ti >= 1:
                wfg.append((td, trk[("tp", ti - 1)]))
            rec.add("vector", lambda e, fl=fl, gt=gt, fg=fg: e.tensor_tensor(
                fg[:].rearrange("p c t -> p (c t)"), fl[:], gt[:], ALU.mult),
                waits=wfg, incs=[(vd, 1)])
            v_fg = vd.inc()
            wtr = [(vd, v_fg)]
            if ti >= 1:
                wtr.append((ev, trk[("fgev", ti - 1)]))
            for c in range(8):
                rec.add("tensor", lambda e, fg=fg, c=c: e.transpose(
                    tps[:, c * 128:(c + 1) * 128], fg[:, c, :], idn[:]),
                    waits=wtr if c == 0 else [], incs=[(td, 1)] if c == 7 else [])
            trk[("tp", ti)] = td.inc()
            rec.add("vector", lambda e, fg=fg: e.tensor_copy(
                fg[:].rearrange("p c t -> p (c t)"), tps[:]),
                waits=[(td, trk[("tp", ti)])], incs=[(ev, 1)])
            trk[("fgev", ti)] = ev.inc()
            for half in range(2):
                pq = sps[half]
                wm = [(ev, trk[("fgev", ti)])]
                if ti >= 1:
                    wm.append((ev, trk[("sev", ti - 1, half)]))
                for c in range(8):
                    rec.add("tensor", lambda e, pq=pq, fg=fg, c=c, half=half: e.matmul(
                        pq[:], fg[:, c, :], wo[:, c, half * 512:(half + 1) * 512],
                        start=(c == 0), stop=False), waits=wm if c == 0 else [])
                rec.add("tensor", lambda e, pq=pq, half=half: e.matmul(
                    pq[:], ones_row[:], borow[:, half * 512:(half + 1) * 512],
                    start=False, stop=True), incs=[(td, 1)])
                tvs = td.inc()
                if half == 1:
                    trk[("smm", ti)] = tvs
                wv = [(td, tvs)]
                if ti >= 2:
                    wv.append((st, trk[("sst", ti - 2)]))
                rec.add("vector", lambda e, pq=pq, ss=ss, half=half: e.tensor_copy(
                    ss[:, half * 512:(half + 1) * 512], pq[:]),
                    waits=wv, incs=[(ev, 1)])
                trk[("sev", ti, half)] = ev.inc()
            rec.add("sync", lambda e, i=i, ss=ss: e.dma_start(
                out=s_h[i * 128:(i + 1) * 128, :], in_=ss[:]),
                waits=[(ev, trk[("sev", ti, 1)])], incs=[(st, 16)])
            trk[("sst", ti)] = st.inc(16)
        rec.add("sync", lambda e: e.nop(), waits=[(st, st.n)])

def build_gate_phase():
    ctx = Ctx("emit_gate")
    nc = ctx.nc
    nc = ctx.nc
    xn_fm_h = nc.dram_tensor("xn_fm", [D, TC], BF16, kind="ExternalInput").ap()
    flat_h = nc.dram_tensor("flat", [TC, D], BF16, kind="ExternalInput").ap()
    wg_h = nc.dram_tensor("Wgate", [D, D], F32, kind="ExternalInput").ap()
    bg_h = nc.dram_tensor("bgate", [D], F32, kind="ExternalInput").ap()
    wo_h = nc.dram_tensor("Wout", [D, D], F32, kind="ExternalInput").ap()
    bo_h = nc.dram_tensor("bout", [D], F32, kind="ExternalInput").ap()
    s_h = nc.dram_tensor("s_tm", [TC, D], BF16, kind="ExternalOutput").ap()

    io = {
        "xn_fm": xn_fm_h,
        "flat": flat_h,
        "Wgate": wg_h,
        "bgate": bg_h,
        "Wout": wo_h,
        "bout": bo_h,
        "s_tm": s_h,
    }
    with ctx.es:
        emit_gate(ctx, io)
        finish(ctx.nc, ctx.rec, ctx.es, ctx)
    return ctx.nc


def make_rotary_tables(t0):
    t = np.arange(t0 - HALO, t0 + T_OWN, dtype=np.float64)
    half = np.arange(0, 64, 2, dtype=np.float64)
    inv_freq = 1.0 / 10000.0 ** (half / 64.0)
    fr = t[:, None] * inv_freq[None, :]
    fr = np.concatenate([fr, fr], -1)
    base = (half + 0.4 * 64) / (1.4 * 64)
    power = (t - N // 2) / float(WIN // 2)
    scale = base[None, :] ** power[:, None]
    scale = np.concatenate([scale, scale], -1)
    sign = np.concatenate([-np.ones(32), np.ones(32)])[None, :]
    cq = (np.cos(fr) * scale * 8.0).astype(np.float32)
    sq = (np.sin(fr) * scale * 8.0 * sign).astype(np.float32)
    ck = (np.cos(fr) / scale).astype(np.float32)
    sk = (np.sin(fr) / scale * sign).astype(np.float32)
    return cq, sq, ck, sk


FQ_T0 = 4
FA_T0 = 8


def emit_ffn_qkv(ctx, io, t0tile=FQ_T0):
    nc = ctx.nc
    rec = ctx.rec
    tab_h = {n: io[n] for n in ("cq", "sq", "ck", "sk")}
    xin_h = io["xin_fm"]
    w_h = io["wqkv"]
    qs_h = io["q_scale"]
    ks_h = io["k_scale"]
    q_out = io["q_fm"]
    k_out = io["k_fm"]
    v_out = io["v65"]
    if True:
        ctx.setup_consts()
        idn = build_identity_bf16(ctx, io.get("ident"))
        rec = ctx.rec
        ld = ctx.sem("ld"); gld = ctx.sem("gld"); st = ctx.sem("st")
        td = ctx.sem("td"); vd = ctx.sem("vd"); sd = ctx.sem("sd"); ev = ctx.sem("ev")

        wq = ctx.sbuf("wq", [128, 8, 1536], BF16)
        scl = ctx.sbuf("scl", [128, 4, 64], F32)
        rec.add("gpsimd", lambda e: e.dma_start(
            out=wq[:], in_=w_h.rearrange("(c p) e -> p c e", p=128)), incs=[(gld, 16)])
        for j, hh in enumerate((qs_h, ks_h)):
            rec.add("sync", lambda e, j=j, hh=hh: e.dma_start(
                out=scl[:, 2 * j, :], in_=hh.unsqueeze(0).broadcast_to([128, 64])),
                incs=[(ld, 16)])
            rec.add("sync", lambda e, j=j, hh=hh: e.dma_start(
                out=scl[:, 2 * j + 1, 0:32],
                in_=hh[32:64].unsqueeze(0).broadcast_to([128, 32])), incs=[(ld, 16)])
            rec.add("sync", lambda e, j=j, hh=hh: e.dma_start(
                out=scl[:, 2 * j + 1, 32:64],
                in_=hh[0:32].unsqueeze(0).broadcast_to([128, 32])), incs=[(ld, 16)])
        prep_ld = ld.inc(16 * 6)
        w_gld = gld.inc(16)

        xs = [ctx.sbuf(f"xs{i}", [128, 8, 128], BF16) for i in range(2)]
        tabs = [ctx.sbuf(f"tab{i}", [128, 4, 64], F32) for i in range(2)]
        fold = ctx.sbuf("fold", [128, 4, 64], F32)
        qkt = [ctx.sbuf(f"qkt{i}", [128, 2, 512], BF16) for i in range(2)]
        v_sb = [ctx.sbuf(f"vsb{i}", [128, FFN_H, 65], BF16) for i in range(2)]
        rns = ctx.sbuf("rns", [128, 2, 8], F32)
        scr = ctx.sbuf("scr", [128, 512], F32)
        tmp1 = ctx.sbuf("tmp1", [128, 512], F32)
        tmp2 = ctx.sbuf("tmp2", [128, 512], F32)
        fmst = [ctx.sbuf(f"fmst{i}", [128, 512], BF16) for i in range(2)]
        ps = [ctx.psum(f"ps{i}", [128, 512]) for i in range(6)]
        tp = ctx.psum("tp", [128, 512], BF16)

        trk = {}
        tiles = list(range(t0tile, TC // 128))
        for ti, i in enumerate(tiles):
            p = ti % 2
            x_sb, tb, qk, vs = xs[p], tabs[p], qkt[p], v_sb[p]
            w = [(ld, prep_ld)] if ti == 0 else []
            if ti >= 2:
                w = [(td, trk[("mmv", ti - 2)]), (vd, trk[("fold", ti - 2)])]
            rec.add("sync", lambda e, i=i, x_sb=x_sb: e.dma_start(
                out=x_sb[:, 0:4, :],
                in_=xin_h[0:512, i * 128:(i + 1) * 128].rearrange(
                    "(c p) t -> p c t", p=128)), waits=w, incs=[(ld, 16)])
            rec.add("sync", lambda e, i=i, x_sb=x_sb: e.dma_start(
                out=x_sb[:, 4:8, :],
                in_=xin_h[512:1024, i * 128 - 1:(i + 1) * 128 - 1].rearrange(
                    "(c p) t -> p c t", p=128)), incs=[(ld, 16)])
            for j, nmn in enumerate(("cq", "sq", "ck", "sk")):
                rec.add("sync", lambda e, i=i, j=j, hh=tab_h[nmn], tb=tb: e.dma_start(
                    out=tb[:, j, :], in_=hh[i * 128:(i + 1) * 128, :]), incs=[(ld, 16)])
            lv = ld.inc(16 * 6)
            wf = [(ld, lv)]
            if ti >= 1:
                wf.append((vd, trk[("rotdone", ti - 1)]))
            for j in range(4):
                rec.add("vector", lambda e, j=j, tb=tb: e.tensor_tensor(
                    fold[:, j, :], tb[:, j, :], scl[:, j, :], ALU.mult),
                    waits=wf if j == 0 else [], incs=[(vd, 1)] if j == 3 else [])
            trk[("fold", ti)] = vd.inc()
            for part in range(3):
                pp = ps[part + 3 * (ti % 2)]
                wm = [(ld, lv), (gld, w_gld)]
                if ti >= 2:
                    wm.append(trk[("pfree", ti - 2, part)])
                for c in range(8):
                    rec.add("tensor", lambda e, pp=pp, x_sb=x_sb, c=c, part=part: e.matmul(
                        pp[:], x_sb[:, c, :], wq[:, c, part * 512:(part + 1) * 512],
                        start=(c == 0), stop=(c == 7)),
                        waits=wm if c == 0 else [], incs=[(td, 1)] if c == 7 else [])
                tv = td.inc()
                if part == 2:
                    trk[("mmv", ti)] = tv
                    if ti < 2:
                        rec.add("vector", lambda e, vs=vs: e.memset(vs[:, :, 64:65], 1.0))
                    wv = [(td, tv)]
                    if ti >= 2:
                        wv.append((st, trk[("vst", ti - 2)]))
                    rec.add("vector", lambda e, pp=pp, vs=vs: e.tensor_copy(
                        vs[:, :, 0:64], pp[:].rearrange("p (h d) -> p h d", d=64)),
                        waits=wv, incs=[(ev, 1)])
                    evv = ev.inc()
                    trk[("pfree", ti, part)] = (ev, evv)
                    rec.add("sync", lambda e, i=i, vs=vs: e.dma_start(
                        out=v_out[i * 128:(i + 1) * 128, :],
                        in_=vs[:].rearrange("p h e -> p (h e)")),
                        waits=[(ev, evv)], incs=[(st, 16)])
                    trk[("vst", ti)] = st.inc(16)
                    continue
                wv = [(td, tv)]
                prevk = (ti, part - 1) if part == 1 else (ti - 1, 1)
                if ("rot", prevk[0], prevk[1]) in trk:
                    wv.append((vd, trk[("rot", prevk[0], prevk[1])]))
                rec.add("scalar", lambda e, pp=pp: e.activation(
                    scr[:], pp[:], AF.Square), waits=wv, incs=[(sd, 1)])
                s_sq = sd.inc()
                rec.add("vector", lambda e, part=part: e.reduce_sum(
                    rns[:, part, :], scr[:].rearrange("p (h d) -> p h d", d=64),
                    AX.X), waits=[(sd, s_sq)])
                rec.add("vector", lambda e, part=part: e.tensor_scalar(
                    rns[:, part, :], rns[:, part, :], ctx.c("clip2"), None,
                    ALU.max), incs=[(vd, 1)])
                v_ss = vd.inc()
                rec.add("scalar", lambda e, part=part: e.activation(
                    rns[:, part, :], rns[:, part, :], AF.Sqrt),
                    waits=[(vd, v_ss)], incs=[(sd, 1)])
                s_rt = sd.inc()
                rec.add("vector", lambda e, part=part: e.reciprocal(
                    rns[:, part, :], rns[:, part, :]), waits=[(sd, s_rt)], incs=[(vd, 1)])
                v_rn = vd.inc()
                rec.add("vector", lambda e, pp=pp, part=part: e.tensor_tensor(
                    tmp1[:].rearrange("p (h d) -> p h d", d=64),
                    pp[:].rearrange("p (h d) -> p h d", d=64),
                    rns[:, part, :].unsqueeze(2).broadcast_to([128, 8, 64]),
                    ALU.mult), waits=[(vd, v_rn)], incs=[(vd, 1)])
                trk[("pfree", ti, part)] = (vd, vd.inc())
                so = 2 * part + 1
                for hside in range(2):
                    dst = (slice(0, 32), slice(32, 64))[hside]
                    srcs = (slice(32, 64), slice(0, 32))[hside]
                    rec.add("vector", lambda e, dst=dst, srcs=srcs, so=so: e.tensor_tensor(
                        tmp2[:].rearrange("p (h d) -> p h d", d=64)[:, :, dst],
                        tmp1[:].rearrange("p (h d) -> p h d", d=64)[:, :, srcs],
                        fold[:, so, dst].unsqueeze(1).broadcast_to([128, 8, 32]),
                        ALU.mult))
                rec.add("vector", lambda e, so=so: e.tensor_tensor(
                    scr[:].rearrange("p (h d) -> p h d", d=64),
                    tmp1[:].rearrange("p (h d) -> p h d", d=64),
                    fold[:, so - 1, :].unsqueeze(1).broadcast_to([128, 8, 64]),
                    ALU.mult))
                wv2 = []
                if ti >= 2:
                    wv2.append((td, trk[("tpd", ti - 2, part)]))
                rec.add("vector", lambda e, qk=qk, part=part: e.tensor_tensor(
                    qk[:, part, :], scr[:], tmp2[:], ALU.add),
                    waits=wv2, incs=[(vd, 1)])
                v_rot = vd.inc()
                trk[("rot", ti, part)] = v_rot
                if part == 1:
                    trk[("rotdone", ti)] = v_rot
                wt = [(vd, v_rot)]
                if (ti, part) != (0, 0):
                    prevkey = (ti, part - 1) if part == 1 else (ti - 1, 1)
                    wt.append((ev, trk[("fmev", prevkey)]))
                for c in range(4):
                    rec.add("tensor", lambda e, qk=qk, part=part, c=c: e.transpose(
                        tp[:, c * 128:(c + 1) * 128], qk[:, part, c * 128:(c + 1) * 128],
                        idn[:]), waits=wt if c == 0 else [],
                        incs=[(td, 1)] if c == 3 else [])
                trk[("tpd", ti, part)] = td.inc()
                fs = fmst[part % 2]
                wv3 = [(td, trk[("tpd", ti, part)])]
                if ti >= 1:
                    wv3.append((st, trk[("fmst", ti - 1, part)]))
                rec.add("vector", lambda e, fs=fs: e.tensor_copy(fs[:], tp[:]),
                        waits=wv3, incs=[(ev, 1)])
                trk[("fmev", (ti, part))] = ev.inc()
                dst_h = (q_out, k_out)[part]
                rec.add("sync", lambda e, dst_h=dst_h, i=i, fs=fs: e.dma_start(
                    out=dst_h[:, i * 128:(i + 1) * 128].rearrange(
                        "(c f) t -> f c t", f=128),
                    in_=fs[:].rearrange("f (c t) -> f c t", t=128)),
                    waits=[(ev, trk[("fmev", (ti, part))])], incs=[(st, 16)])
                trk[("fmst", ti, part)] = st.inc(16)
        rec.add("sync", lambda e: e.nop(), waits=[(st, st.n)])

def build_ffn_qkv_phase():
    ctx = Ctx("emit_ffn_qkv")
    nc = ctx.nc
    nc = ctx.nc
    xin_h = nc.dram_tensor("xin_fm", [D, TC], BF16, kind="ExternalInput").ap()
    w_h = nc.dram_tensor("wqkv", [D, 1536], F32, kind="ExternalInput").ap()
    qs_h = nc.dram_tensor("q_scale", [64], F32, kind="ExternalInput").ap()
    ks_h = nc.dram_tensor("k_scale", [64], F32, kind="ExternalInput").ap()
    tab_h = {n: nc.dram_tensor(n, [TC, 64], F32, kind="ExternalInput").ap()
             for n in ("cq", "sq", "ck", "sk")}
    q_out = nc.dram_tensor("q_fm", [512, TC], BF16, kind="ExternalOutput").ap()
    k_out = nc.dram_tensor("k_fm", [512, TC], BF16, kind="ExternalOutput").ap()
    v_out = nc.dram_tensor("v65", [TC, FFN_H * 65], BF16, kind="ExternalOutput").ap()

    io = {
        "xin_fm": xin_h,
        "wqkv": w_h,
        "q_scale": qs_h,
        "k_scale": ks_h,
        "q_fm": q_out,
        "k_fm": k_out,
        "v65": v_out,
        **tab_h,
    }
    with ctx.es:
        emit_ffn_qkv(ctx, io)
        finish(ctx.nc, ctx.rec, ctx.es, ctx)
    return ctx.nc


def make_w01():
    j = np.arange(128)[:, None]
    c = np.arange(128)[None, :]
    a = (c <= j).astype(np.float32)
    b = (c >= j).astype(np.float32)
    return a.astype(BF), b.astype(BF)


def emit_ffn_attn(ctx, io, t0tile=FA_T0, dbg_tile=None, dbg_head=0):
    nc = ctx.nc
    rec = ctx.rec
    q_h = io["q_fm"]
    k_h = io["k_fm"]
    v_h = io["v65"]
    w01a_h = io["w01a"]
    w01b_h = io["w01b"]
    msk_h = io["kvmask"]
    o_h = io["o_tm"]
    dbgE_h = io.get("dbgE")
    dbgG_h = io.get("dbgG")
    dbgO_h = io.get("dbgO")
    if True:
        ctx.setup_consts()
        rec = ctx.rec
        ld = ctx.sem("ld"); gld = ctx.sem("gld"); st = ctx.sem("st")
        td = ctx.sem("td"); vd = ctx.sem("vd"); sd = ctx.sem("sd"); ev = ctx.sem("ev")

        w01a = ctx.sbuf("w01a_sb", [128, 128], BF16)
        w01b = ctx.sbuf("w01b_sb", [128, 128], BF16)
        mask_sb = ctx.sbuf("masks", [128, TC // 128], F32)
        rec.add("sync", lambda e: e.dma_start(out=w01a[:], in_=w01a_h), incs=[(ld, 16)])
        rec.add("sync", lambda e: e.dma_start(out=w01b[:], in_=w01b_h), incs=[(ld, 16)])
        rec.add("sync", lambda e: e.dma_start(
            out=mask_sb[:], in_=msk_h.rearrange("(i p) -> p i", p=128),
            allow_slow_non_contiguous=True), incs=[(ld, 16)])
        prep_ld = ld.inc(48)

        RINGF = 8
        q_sb = [ctx.sbuf(f"qsb{i}", [64, 8, 128], BF16) for i in range(2)]
        k_sb = ctx.sbuf("kringf", [64, 8, RINGF * 128], BF16)
        v_sb = ctx.sbuf("vringf", [128, RINGF, FFN_H * 65], BF16)
        E_sb = [ctx.sbuf(f"esb{i}", [128, 640], BF16) for i in range(2)]
        G_sb = [ctx.sbuf(f"gsb{i}", [128, 640], BF16) for i in range(2)]
        o_stage = ctx.sbuf("ostg", [128, FFN_H, 65], F32)
        zcol = ctx.sbuf("zcol", [128, 2 * FFN_H], F32)
        o_tm = [ctx.sbuf(f"otm{i}", [128, 512], BF16) for i in range(2)]
        ps_e = [ctx.psum(f"pse{i}", [128, 1024]) for i in range(2)]
        o_ps = [ctx.psum(f"ops{i}", [128, 512]) for i in range(2)]

        tiles = list(range(t0tile, TC // 128))
        dbgbar = [0]
        exp_done = {}; g_done = {}; pv_done = {}; oev_done = {}; o_st = {}
        first_i = tiles[0]
        pre0 = first_i - 4
        assert (pre0 % RINGF) + 4 <= RINGF
        rec.add("sync", lambda e: e.dma_start(
            out=k_sb[:, :, (pre0 % RINGF) * 128:((pre0 % RINGF) + 4) * 128],
            in_=k_h[:, pre0 * 128:(pre0 + 4) * 128].rearrange(
                "(h p) t -> p h t", p=64)), incs=[(ld, 16)])
        ld.inc(16)
        rec.add("gpsimd", lambda e: e.dma_start(
            out=v_sb[:, pre0 % RINGF:(pre0 % RINGF) + 4, :],
            in_=v_h[pre0 * 128:(pre0 + 4) * 128, :].rearrange(
                "(b p) e -> p b e", p=128)), incs=[(gld, 16)])
        gld.inc(16)
        use_g_all = [None, None]
        for ti, i in enumerate(tiles):
            p = ti % 2
            qs, ks, vs = q_sb[p], k_sb, v_sb
            blocks = [i - 4 + j for j in range(5)]
            w = []
            if ti >= 2:
                w = [(td, pv_done[ti - 2])]
            rec.add("sync", lambda e, i=i, qs=qs: e.dma_start(
                out=qs[:], in_=q_h[:, i * 128:(i + 1) * 128].rearrange(
                    "(h p) t -> p h t", p=64)), waits=w, incs=[(ld, 16)])
            wr = []
            if ti - 3 >= 0:
                wr = [(td, pv_done[ti - 3])]
            rec.add("sync", lambda e, i=i: e.dma_start(
                out=k_sb[:, :, (i % RINGF) * 128:((i % RINGF) + 1) * 128],
                in_=k_h[:, i * 128:(i + 1) * 128].rearrange(
                    "(h p) t -> p h t", p=64)), waits=wr, incs=[(ld, 16)])
            rec.add("gpsimd", lambda e, i=i: e.dma_start(
                out=v_sb[:, i % RINGF, :], in_=v_h[i * 128:(i + 1) * 128, :]),
                waits=wr, incs=[(gld, 16)])
            lv = ld.inc(32)
            lvg = gld.inc(16)
            for h in range(FFN_H):
                hp = h % 2
                pe = ps_e[hp]
                wm = [(ld, lv), (gld, lvg)]
                key = (ti, h - 2) if h >= 2 else (ti - 1, FFN_H - 2 + h)
                if key in exp_done:
                    wm.append((sd, exp_done[key]))
                for j in range(5):
                    rs = blocks[j] % RINGF
                    rec.add("tensor", lambda e, pe=pe, j=j, rs=rs, ks=ks, qs=qs, h=h: e.matmul(
                        pe[:, j * 128:(j + 1) * 128], ks[:, h, rs * 128:(rs + 1) * 128],
                        qs[:, h, :], start=True, stop=True),
                        waits=wm if j == 0 else [], incs=[(td, 1)] if j == 4 else [])
                t_mm = td.inc()
                Eb = E_sb[hp]
                we = [(td, t_mm)]
                if key in g_done:
                    we.append((vd, g_done[key]))
                if key in pv_done:
                    we.append((td, pv_done[key]))
                if dbgbar[0]:
                    we.append((st, dbgbar[0]))
                rec.add("scalar", lambda e, pe=pe, Eb=Eb: e.activation(
                    Eb[:], pe[:, 0:640], AF.Exp), waits=we, incs=[(sd, 1)])
                exp_done[(ti, h)] = sd.inc()
                Gb = G_sb[hp]
                wg = [(sd, exp_done[(ti, h)])]
                if key in pv_done:
                    wg.append((td, pv_done[key]))
                use_g = [False] * 5
                gops = []
                gops.append((lambda e, Eb=Eb, Gb=Gb: e.tensor_tensor(
                    Gb[:, 0:128], Eb[:, 0:128], w01a[:], ALU.mult), wg))
                use_g[0] = True
                gops.append((lambda e, Eb=Eb, Gb=Gb: e.tensor_tensor(
                    Gb[:, 512:640], Eb[:, 512:640], w01b[:], ALU.mult), []))
                use_g[4] = True
                for j, bu in enumerate(blocks):
                    if bu < HALO // 128:
                        if j in (0, 4):
                            gops.append((lambda e, Gb=Gb, j=j, bu=bu: e.tensor_scalar(
                                Gb[:, j * 128:(j + 1) * 128],
                                Gb[:, j * 128:(j + 1) * 128],
                                mask_sb[:, bu:bu + 1], None, ALU.mult), []))
                        else:
                            gops.append((lambda e, Gb=Gb, Eb=Eb, j=j, bu=bu: e.tensor_scalar(
                                Gb[:, j * 128:(j + 1) * 128],
                                Eb[:, j * 128:(j + 1) * 128],
                                mask_sb[:, bu:bu + 1], None, ALU.mult), []))
                            use_g[j] = True
                for oi_, (fn_, w_) in enumerate(gops):
                    rec.add("vector", fn_, waits=w_,
                            incs=[(vd, 1)] if oi_ == len(gops) - 1 else [])
                g_done[(ti, h)] = vd.inc()
                use_g_all[h % 2] = list(use_g)

                def emit_pv(hh):
                    po = o_ps[hh % 2]
                    wp = [(vd, g_done[(ti, hh)])]
                    kk = (ti, hh - 2) if hh >= 2 else (ti - 1, FFN_H - 2 + hh)
                    if kk in oev_done:
                        wp.append((ev, oev_done[kk]))
                    for j in range(5):
                        opnd = G_sb[hh % 2] if use_g_all[hh % 2][j] else E_sb[hh % 2]
                        rs = blocks[j] % RINGF
                        rec.add("tensor", lambda e, po=po, opnd=opnd, j=j, rs=rs, hh=hh: e.matmul(
                            po[:, 0:65], opnd[:, j * 128:(j + 1) * 128],
                            v_sb[:, rs, 65 * hh:65 * hh + 65],
                            start=(j == 0), stop=(j == 4)),
                            waits=wp if j == 0 else [], incs=[(td, 1)] if j == 4 else [])
                    pv_done[(ti, hh)] = td.inc()
                    if hh == FFN_H - 1:
                        pv_done[ti] = pv_done[(ti, hh)]
                    rec.add("vector", lambda e, po=po, hh=hh: e.tensor_copy(
                        o_stage[:, hh, :], po[:, 0:65]),
                        waits=[(td, pv_done[(ti, hh)])], incs=[(ev, 1)])
                    oev_done[(ti, hh)] = ev.inc()

                if h >= 1:
                    emit_pv(h - 1)
                if h == FFN_H - 1:
                    emit_pv(h)
                if dbg_tile is not None and i == dbg_tile and h == dbg_head:
                    rec.add("sync", lambda e, Eb=Eb: e.dma_start(out=dbgE_h, in_=Eb[:]),
                            waits=[(vd, g_done[(ti, h)])], incs=[(st, 16)])
                    st.inc(16)
                    rec.add("sync", lambda e, Gb=Gb: e.dma_start(out=dbgG_h, in_=Gb[:]),
                            waits=[(vd, g_done[(ti, h)])], incs=[(st, 16)])
                    dbgbar[0] = st.inc(16)
            if dbg_tile is not None and i == dbg_tile:
                rec.add("sync", lambda e: e.dma_start(
                    out=dbgO_h, in_=o_stage[:].rearrange("p h e -> p (h e)")),
                    waits=[(ev, oev_done[(ti, FFN_H - 1)])], incs=[(st, 16)])
                dbgbar[0] = st.inc(16)
            ob = o_tm[p]
            wn = [(ev, oev_done[(ti, FFN_H - 1)])]
            if ti >= 2:
                wn.append((st, o_st[ti - 2]))
            rec.add("vector", lambda e: e.tensor_scalar(
                zcol[:, 0:8], o_stage[:, :, 64], ctx.c("tiny"), None, ALU.max),
                waits=wn, incs=[(vd, 1)])
            v_z = vd.inc()
            rec.add("vector", lambda e: e.reciprocal(
                zcol[:, 8:16], zcol[:, 0:8]), waits=[(vd, v_z)], incs=[(vd, 1)])
            v_rz = vd.inc()
            rec.add("vector", lambda e, ob=ob: e.tensor_tensor(
                ob[:].rearrange("p (h d) -> p h d", d=64),
                o_stage[:, :, 0:64],
                zcol[:, 8:16].unsqueeze(2).broadcast_to([128, FFN_H, 64]),
                ALU.mult), waits=[(vd, v_rz)], incs=[(vd, 1)])
            v_fl = vd.inc()
            rec.add("sync", lambda e, i=i, ob=ob: e.dma_start(
                out=o_h[i * 128:(i + 1) * 128, :], in_=ob[:]),
                waits=[(vd, v_fl)], incs=[(st, 16)])
            o_st[ti] = st.inc(16)
        rec.add("sync", lambda e: e.nop(), waits=[(st, st.n)])

def build_ffn_attn_phase(dbg_tile=None, dbg_head=0):
    ctx = Ctx("emit_ffn_attn")
    nc = ctx.nc
    nc = ctx.nc
    q_h = nc.dram_tensor("q_fm", [512, TC], BF16, kind="ExternalInput").ap()
    k_h = nc.dram_tensor("k_fm", [512, TC], BF16, kind="ExternalInput").ap()
    v_h = nc.dram_tensor("v65", [TC, FFN_H * 65], BF16, kind="ExternalInput").ap()
    w01a_h = nc.dram_tensor("w01a", [128, 128], BF16, kind="ExternalInput").ap()
    w01b_h = nc.dram_tensor("w01b", [128, 128], BF16, kind="ExternalInput").ap()
    msk_h = nc.dram_tensor("kvmask", [TC], F32, kind="ExternalInput").ap()
    o_h = nc.dram_tensor("o_tm", [TC, 512], BF16, kind="ExternalOutput").ap()
    if dbg_tile is not None:
        dbgE_h = nc.dram_tensor("dbgE", [128, 640], BF16, kind="ExternalOutput").ap()
        dbgG_h = nc.dram_tensor("dbgG", [128, 640], BF16, kind="ExternalOutput").ap()
        dbgO_h = nc.dram_tensor("dbgO", [128, FFN_H * 65], F32, kind="ExternalOutput").ap()

    io = {
        "q_fm": q_h,
        "k_fm": k_h,
        "v65": v_h,
        "w01a": w01a_h,
        "w01b": w01b_h,
        "kvmask": msk_h,
        "o_tm": o_h,
    }
    if dbg_tile is not None:
        io.update({"dbgE": dbgE_h, "dbgG": dbgG_h, "dbgO": dbgO_h})
    with ctx.es:
        emit_ffn_attn(ctx, io, dbg_tile=dbg_tile, dbg_head=dbg_head)
        finish(ctx.nc, ctx.rec, ctx.es, ctx)
    return ctx.nc


def emit_ffn_wo(ctx, io, t0tile=FA_T0):
    nc = ctx.nc
    rec = ctx.rec
    o_h = io["o_tm"]
    wo_h = io["wo"]
    xr_h = io["xres_tm"]
    msk_h = io["kvmask"]
    x1_h = io["x1_tm"]
    if True:
        ctx.setup_consts()
        idn = build_identity_bf16(ctx, io.get("ident"))
        rec = ctx.rec
        ld = ctx.sem("ld"); gld = ctx.sem("gld"); st = ctx.sem("st")
        td = ctx.sem("td"); vd = ctx.sem("vd"); ev = ctx.sem("ev")

        wo = ctx.sbuf("wo_sb", [128, 4, D], BF16)
        mask_sb = ctx.sbuf("masks", [128, TC // 128], F32)
        rec.add("gpsimd", lambda e: e.dma_start(
            out=wo[:], in_=wo_h.rearrange("(c p) e -> p c e", p=128)), incs=[(gld, 16)])
        w_gld = gld.inc(16)
        rec.add("sync", lambda e: e.dma_start(
            out=mask_sb[:], in_=msk_h.rearrange("(i p) -> p i", p=128),
            allow_slow_non_contiguous=True), incs=[(ld, 16)])
        prep_ld = ld.inc(16)

        o_sb = [ctx.sbuf(f"osb{i}", [128, 512], BF16) for i in range(2)]
        xr_sb = [ctx.sbuf(f"xrsb{i}", [128, D], BF16) for i in range(2)]
        ofm = [ctx.sbuf(f"ofm{i}", [128, 4, 128], BF16) for i in range(2)]
        x1s = [ctx.sbuf(f"x1s{i}", [128, D], BF16) for i in range(2)]
        tp = ctx.psum("tp", [128, 512], BF16)
        ps = [ctx.psum(f"ps{i}", [128, 512]) for i in range(2)]

        trk = {}
        tiles = list(range(t0tile, TC // 128))
        zt = ctx.sbuf("ztile", [128, D], BF16)
        rec.add("vector", lambda e: e.memset(zt[:], 0.0), incs=[(vd, 1)])
        z_v = vd.inc()
        rec.add("sync", lambda e: e.dma_start(
            out=x1_h[(t0tile - 1) * 128:t0tile * 128, :], in_=zt[:]),
            waits=[(vd, z_v)], incs=[(st, 16)])
        st.inc(16)
        for ti, i in enumerate(tiles):
            p = ti % 2
            ob, xr, of, x1 = o_sb[p], xr_sb[p], ofm[p], x1s[p]
            w = [(ld, prep_ld)] if ti == 0 else []
            if ti >= 2:
                w = [(td, trk[("tp", ti - 2)])]
            rec.add("sync", lambda e, i=i, ob=ob: e.dma_start(
                out=ob[:], in_=o_h[i * 128:(i + 1) * 128, :]), waits=w, incs=[(ld, 16)])
            rec.add("sync", lambda e, i=i, xr=xr: e.dma_start(
                out=xr[:], in_=xr_h[i * 128:(i + 1) * 128, :]), incs=[(ld, 16)])
            lv = ld.inc(32)
            wt = [(ld, lv)]
            if ti >= 1:
                wt.append((ev, trk[("ofev", ti - 1)]))
            for c in range(4):
                rec.add("tensor", lambda e, ob=ob, c=c: e.transpose(
                    tp[:, c * 128:(c + 1) * 128], ob[:, c * 128:(c + 1) * 128], idn[:]),
                    waits=wt if c == 0 else [], incs=[(td, 1)] if c == 3 else [])
            trk[("tp", ti)] = td.inc()
            rec.add("vector", lambda e, of=of: e.tensor_copy(
                of[:].rearrange("p c t -> p (c t)"), tp[:]),
                waits=[(td, trk[("tp", ti)])], incs=[(ev, 1)])
            trk[("ofev", ti)] = ev.inc()
            for half in range(2):
                pp = ps[half]
                wm = [(ev, trk[("ofev", ti)]), (gld, w_gld)]
                if ti >= 1:
                    wm.append((ev, trk[("x1ev", ti - 1, half)]))
                for c in range(4):
                    rec.add("tensor", lambda e, pp=pp, of=of, c=c, half=half: e.matmul(
                        pp[:], of[:, c, :], wo[:, c, half * 512:(half + 1) * 512],
                        start=(c == 0), stop=(c == 3)),
                        waits=wm if c == 0 else [], incs=[(td, 1)] if c == 3 else [])
                tv = td.inc()
                wv = [(td, tv)]
                if ti >= 2:
                    wv.append((st, trk[("x1st", ti - 2)]))
                rec.add("vector", lambda e, pp=pp, x1=x1, xr=xr, half=half: e.scalar_tensor_tensor(
                    x1[:, half * 512:(half + 1) * 512], pp[:], ctx.c("one"),
                    xr[:, half * 512:(half + 1) * 512], ALU.mult, ALU.add),
                    waits=wv, incs=[(ev, 1)] if i >= HALO // 128 else [])
                if i < HALO // 128:
                    rec.add("vector", lambda e, x1=x1, half=half, i=i: e.tensor_scalar(
                        x1[:, half * 512:(half + 1) * 512],
                        x1[:, half * 512:(half + 1) * 512],
                        mask_sb[:, i:i + 1], None, ALU.mult), incs=[(ev, 1)])
                trk[("x1ev", ti, half)] = ev.inc()
            rec.add("sync", lambda e, i=i, x1=x1: e.dma_start(
                out=x1_h[i * 128:(i + 1) * 128, :], in_=x1[:]),
                waits=[(ev, trk[("x1ev", ti, 1)])], incs=[(st, 16)])
            trk[("x1st", ti)] = st.inc(16)
        rec.add("sync", lambda e: e.nop(), waits=[(st, st.n)])

def build_ffn_wo_phase():
    ctx = Ctx("emit_ffn_wo")
    nc = ctx.nc
    nc = ctx.nc
    o_h = nc.dram_tensor("o_tm", [TC, 512], BF16, kind="ExternalInput").ap()
    wo_h = nc.dram_tensor("wo", [512, D], F32, kind="ExternalInput").ap()
    xr_h = nc.dram_tensor("xres_tm", [TC, D], BF16, kind="ExternalInput").ap()
    msk_h = nc.dram_tensor("kvmask", [TC], F32, kind="ExternalInput").ap()
    x1_h = nc.dram_tensor("x1_tm", [TC, D], BF16, kind="ExternalOutput").ap()

    io = {
        "o_tm": o_h,
        "wo": wo_h,
        "xres_tm": xr_h,
        "kvmask": msk_h,
        "x1_tm": x1_h,
    }
    with ctx.es:
        emit_ffn_wo(ctx, io)
        finish(ctx.nc, ctx.rec, ctx.es, ctx)
    return ctx.nc


def emit_ffn_mid(ctx, io, fm_out, t0tile=FA_T0, dbg_group=None):
    nc = ctx.nc
    rec = ctx.rec
    NCH = 2 * HIDP // 128
    NMID = HIDP // 128
    x1_h = io["x1_tm"]
    lg_h = io["ln_g"]
    lb_h = io["ln_b"]
    w1_h = io["w1p"]
    b1_h = io["b1p"]
    w2_h = io["w2p"]
    b2_h = io["b2"]
    msk_h = io["kvmask"]
    x2_h = io["x2_tm"]
    x2f_h = io.get("x2_fm")
    dbgh_h = io.get("dbgH")
    dbgm_h = io.get("dbgM")
    if True:
        ctx.setup_consts()
        idn = build_identity_bf16(ctx, io.get("ident"))
        rec = ctx.rec
        ld = ctx.sem("ld"); gld = ctx.sem("gld"); st = ctx.sem("st")
        td = ctx.sem("td"); vd = ctx.sem("vd"); sd = ctx.sem("sd"); ev = ctx.sem("ev")

        w1 = ctx.sbuf("w1_sb", [128, 8, 2 * HIDP], BF16)
        w2 = ctx.sbuf("w2_sb", [128, NMID, D], BF16)
        cols = ctx.sbuf("cols_sb", [128, 8 + 8 + NCH], F32)
        b2row = ctx.sbuf("b2row", [1, D], F32)
        ones_row = ctx.sbuf("ones_row2", [1, 128], F32)
        mask_sb = ctx.sbuf("masks", [128, TC // 128], F32)
        rec.add("gpsimd", lambda e: e.dma_start(
            out=w1[:], in_=w1_h.rearrange("(c p) e -> p c e", p=128)), incs=[(gld, 16)])
        rec.add("gpsimd", lambda e: e.dma_start(
            out=w2[:], in_=w2_h.rearrange("(c p) e -> p c e", p=128)), incs=[(gld, 16)])
        w_gld = gld.inc(32)
        rec.add("sync", lambda e: e.dma_start(
            out=cols[:, 0:8], in_=lg_h.rearrange("(c p) -> p c", p=128),
            allow_slow_non_contiguous=True), incs=[(ld, 16)])
        rec.add("sync", lambda e: e.dma_start(
            out=cols[:, 8:16], in_=lb_h.rearrange("(c p) -> p c", p=128),
            allow_slow_non_contiguous=True), incs=[(ld, 16)])
        rec.add("sync", lambda e: e.dma_start(
            out=cols[:, 16:16 + NCH], in_=b1_h.rearrange("(c p) -> p c", p=128),
            allow_slow_non_contiguous=True), incs=[(ld, 16)])
        rec.add("sync", lambda e: e.dma_start(
            out=b2row[:], in_=b2_h.rearrange("(o e) -> o e", o=1)), incs=[(ld, 16)])
        rec.add("sync", lambda e: e.dma_start(
            out=mask_sb[:], in_=msk_h.rearrange("(i p) -> p i", p=128),
            allow_slow_non_contiguous=True), incs=[(ld, 16)])
        rec.add("vector", lambda e: e.memset(ones_row[:], 1.0), incs=[(vd, 1)])
        prep_ld = ld.inc(16 * 5)
        prep_vd = vd.inc()

        x1r = [ctx.sbuf(f"x1r{i}", [128, D], BF16) for i in range(4)]
        h2b = [ctx.sbuf(f"h2b{i}", [128, 512], BF16) for i in range(2)]
        h2n = ctx.sbuf("h2n", [128, D], BF16)
        h2nf = ctx.sbuf("h2nf", [128, 8, 512], BF16)
        mid = ctx.sbuf("mid_sb", [128, NMID, 512], BF16)
        Gg = ctx.sbuf("Gg", [128, 512], BF16)
        stats = ctx.sbuf("stats2", [128, 12], F32)
        scr = ctx.sbuf("scr2", [128, D], F32)
        x2s = [ctx.sbuf(f"x2s{i}", [128, D], BF16) for i in range(2)]
        fmst = ctx.sbuf("fmst2", [128, D], BF16)
        tp = ctx.psum("tp2", [128, 512], BF16)
        hp1 = [ctx.psum(f"hp{i}", [128, 512]) for i in range(2)]
        xp = [ctx.psum(f"xp{i}", [128, 512]) for i in range(2)]

        trk = {}
        ntile = TC // 128
        groups = []
        if fm_out:
            zt2 = ctx.sbuf("ztile2", [128, D], BF16)
            rec.add("vector", lambda e: e.memset(zt2[:], 0.0), incs=[(vd, 1)])
            z_v2 = vd.inc()
            rec.add("sync", lambda e: e.dma_start(
                out=x2f_h[:, (t0tile - 1) * 128:t0tile * 128].rearrange(
                    "(c f) t -> f c t", f=128),
                in_=zt2[:].rearrange("f (c t) -> f c t", t=128)),
                waits=[(vd, z_v2)], incs=[(st, 16)])
            st.inc(16)
        t = t0tile * 128
        while t < TC:
            spn = min(512, TC - t)
            groups.append((t, spn))
            t += spn
        for gi, (g0, spn) in enumerate(groups):
            nt = spn // 128
            for tt in range(nt):
                i = g0 // 128 + tt
                p = i % 2
                xr, hb = x1r[i % 4], h2b[p]
                w = [(ld, prep_ld), (vd, prep_vd)] if gi == 0 and tt == 0 else []
                if ("x2ev", i - 4, 1) in trk:
                    w = [(ev, trk[("x2ev", i - 4, 1)])]
                elif (i - 2) * 128 >= t0tile * 128:
                    w = [(vd, trk[("h2n", i - 2)])]
                if ("sh2n", i - 2) in trk:
                    w.append((sd, trk[("sh2n", i - 2)]))
                rec.add("sync", lambda e, i=i, xr=xr: e.dma_start(
                    out=xr[:], in_=x1_h[i * 128:(i + 1) * 128, :]),
                    waits=w, incs=[(ld, 16)])
                rec.add("sync", lambda e, i=i, hb=hb: e.dma_start(
                    out=hb[:], in_=x1_h[i * 128 - 1:(i + 1) * 128 - 1, 512:1024]),
                    incs=[(ld, 16)])
                lv = ld.inc(32)
                rec.add("vector", lambda e, xr=xr: e.reduce_sum(
                    stats[:, 0:1], xr[:, 0:512], AX.X), waits=[(ld, lv)])
                rec.add("vector", lambda e, hb=hb: e.reduce_sum(
                    stats[:, 1:2], hb[:], AX.X))
                rec.add("scalar", lambda e, xr=xr: e.activation(
                    scr[:, 0:512], xr[:, 0:512], AF.Square), waits=[(ld, lv)])
                rec.add("scalar", lambda e, hb=hb: e.activation(
                    scr[:, 512:1024], hb[:], AF.Square), incs=[(sd, 1)])
                s_sq = sd.inc()
                rec.add("vector", lambda e: e.reduce_sum(
                    stats[:, 2:3], scr[:], AX.X), waits=[(sd, s_sq)])
                rec.add("vector", lambda e: e.tensor_tensor(
                    stats[:, 3:4], stats[:, 0:1], stats[:, 1:2], ALU.add), incs=[(vd, 1)])
                v_s = vd.inc()
                rec.add("vector", lambda e: e.tensor_scalar(
                    stats[:, 4:5], stats[:, 3:4], ctx.c("inv1024"), None, ALU.mult))
                rec.add("vector", lambda e: e.tensor_scalar(
                    stats[:, 5:6], stats[:, 2:3], ctx.c("inv1024"), None, ALU.mult),
                    incs=[(vd, 1)])
                v_m = vd.inc()
                rec.add("vector", lambda e: e.scalar_tensor_tensor(
                    stats[:, 6:7], stats[:, 4:5], stats[:, 4:5], stats[:, 5:6],
                    ALU.mult, ALU.subtract), waits=[(vd, v_m)], incs=[(vd, 1)])
                v_t = vd.inc()
                rec.add("scalar", lambda e: e.activation(
                    stats[:, 7:8], stats[:, 6:7], AF.Sqrt, bias=ctx.c("eps"),
                    scale=ctx.c("neg1")), waits=[(vd, v_t)], incs=[(sd, 1)])
                s_std = sd.inc()
                rec.add("vector", lambda e: e.reciprocal(
                    stats[:, 8:9], stats[:, 7:8]), waits=[(sd, s_std)], incs=[(vd, 1)])
                v_rs = vd.inc()
                rec.add("vector", lambda e: e.tensor_scalar(
                    stats[:, 9:10], stats[:, 4:5], stats[:, 8:9], ctx.c("neg1"),
                    ALU.mult, ALU.mult), waits=[(vd, v_rs)], incs=[(vd, 1)])
                v_mb = vd.inc()
                wa = [(vd, v_mb)]
                if (i - 1) * 128 >= t0tile * 128:
                    wa.append((td, trk[("h2tp", i - 1)]))
                rec.add("scalar", lambda e, xr=xr: e.activation(
                    h2n[:, 0:512], xr[:, 0:512], AF.Identity, bias=stats[:, 9:10],
                    scale=stats[:, 8:9]), waits=wa)
                rec.add("scalar", lambda e, hb=hb: e.activation(
                    h2n[:, 512:1024], hb[:], AF.Identity, bias=stats[:, 9:10],
                    scale=stats[:, 8:9]), incs=[(sd, 1)])
                s_h2n = sd.inc()
                trk[("sh2n", i)] = s_h2n
                for c in range(8):
                    wt = [(sd, s_h2n)] if c % 4 == 0 else []
                    if c % 4 == 0 and (gi > 0 or tt > 0 or c > 0):
                        wt.append((ev, ctx.sems["ev"].n))
                    rec.add("tensor", lambda e, c=c: e.transpose(
                        tp[:, (c % 4) * 128:((c % 4) + 1) * 128],
                        h2n[:, c * 128:(c + 1) * 128], idn[:]),
                        waits=wt, incs=[(td, 1)] if c % 4 == 3 else [])
                    if c % 4 == 3:
                        tv = td.inc()
                        for cc in range(c - 3, c + 1):
                            lastcc = cc == c
                            rec.add("vector", lambda e, cc=cc, tt=tt: e.scalar_tensor_tensor(
                                h2nf[:, cc, tt * 128:(tt + 1) * 128],
                                tp[:, (cc % 4) * 128:((cc % 4) + 1) * 128],
                                cols[:, cc:cc + 1],
                                cols[:, 8 + cc:9 + cc].broadcast_to([128, 128]),
                                ALU.mult, ALU.add),
                                waits=[(td, tv)], incs=[(ev, 1)])
                            ev.inc()
                trk[("h2tp", i)] = td.n
                rec.add("vector", lambda e: e.tensor_copy(
                    stats[:, 10:11], stats[:, 8:9]), incs=[(vd, 1)])
                trk[("h2n", i)] = vd.inc()
            h2nf_ev = ctx.sems["ev"].n
            for m in range(NMID):
                pg = hp1[0]
                pa = hp1[1]
                wmg = [(ev, h2nf_ev), (gld, w_gld)]
                if (gi, m - 1) in trk:
                    wmg.append((sd, trk[(gi, m - 1)]))
                    wmg.append((vd, trk[(gi, m - 1, "a")]))
                elif gi > 0:
                    wmg.append((sd, trk[(gi - 1, NMID - 1)]))
                    wmg.append((vd, trk[(gi - 1, NMID - 1, "a")]))
                for c in range(8):
                    rec.add("tensor", lambda e, pg=pg, c=c, m=m, spn=spn: e.matmul(
                        pg[:, 0:spn], w1[:, c, (NMID + m) * 128:(NMID + m + 1) * 128],
                        h2nf[:, c, 0:spn], start=(c == 0), stop=(c == 7)),
                        waits=wmg if c == 0 else [], incs=[(td, 1)] if c == 7 else [])
                tvg = td.inc()
                rec.add("scalar", lambda e, pg=pg, m=m, spn=spn: e.activation(
                    Gg[:, 0:spn], pg[:, 0:spn], AF.Gelu,
                    bias=cols[:, 16 + NMID + m:17 + NMID + m]),
                    waits=[(td, tvg)], incs=[(sd, 1)])
                trk[(gi, m)] = sd.inc()
                wma = [(td, tvg)]
                if (gi, m - 1, "a") in trk:
                    wma.append((vd, trk[(gi, m - 1, "a")]))
                elif gi > 0:
                    wma.append((vd, trk[(gi - 1, NMID - 1, "a")]))
                for c in range(8):
                    rec.add("tensor", lambda e, pa=pa, c=c, m=m, spn=spn: e.matmul(
                        pa[:, 0:spn], w1[:, c, m * 128:(m + 1) * 128],
                        h2nf[:, c, 0:spn], start=(c == 0), stop=(c == 7)),
                        waits=wma if c == 0 else [], incs=[(td, 1)] if c == 7 else [])
                tva = td.inc()
                rec.add("vector", lambda e, pa=pa, m=m, spn=spn: e.scalar_tensor_tensor(
                    mid[:, m, 0:spn], pa[:, 0:spn], cols[:, 16 + m:17 + m],
                    Gg[:, 0:spn], ALU.add, ALU.mult),
                    waits=[(td, tva), (sd, trk[(gi, m)])], incs=[(vd, 1)])
                trk[(gi, m, "a")] = vd.inc()
            mid_vd = trk[(gi, NMID - 1, "a")]
            if dbg_group is not None and gi == dbg_group:
                rec.add("sync", lambda e: e.dma_start(
                    out=dbgh_h, in_=h2nf[:].rearrange("p c t -> p (c t)")),
                    waits=[(vd, mid_vd)], incs=[(st, 16)])
                st.inc(16)
                rec.add("sync", lambda e: e.dma_start(
                    out=dbgm_h, in_=mid[:].rearrange("p c t -> p (c t)")),
                    waits=[(vd, mid_vd)], incs=[(st, 16)])
                st.inc(16)
            for tt in range(nt):
                i = g0 // 128 + tt
                p = i % 2
                x2b = x2s[p]
                for half in range(2):
                    pp = xp[half]
                    wm = [(vd, mid_vd)]
                    if ("x2ev", i - 2, half) in trk:
                        wm.append((ev, trk[("x2ev", i - 2, half)]))
                    if fm_out and ("xf", i - 2) in trk:
                        wm.append((st, trk[("xf", i - 2)]))
                    for m in range(NMID):
                        rec.add("tensor", lambda e, pp=pp, m=m, tt=tt, half=half: e.matmul(
                            pp[:], mid[:, m, tt * 128:(tt + 1) * 128],
                            w2[:, m, half * 512:(half + 1) * 512],
                            start=(m == 0), stop=False),
                            waits=wm if m == 0 else [])
                    rec.add("tensor", lambda e, pp=pp, half=half: e.matmul(
                        pp[:], ones_row[:], b2row[:, half * 512:(half + 1) * 512],
                        start=False, stop=True), incs=[(td, 1)])
                    tvx = td.inc()
                    wv = [(td, tvx)]
                    if ("x2st", i - 2) in trk:
                        wv.append((st, trk[("x2st", i - 2)]))
                    xr = x1r[i % 4]
                    rec.add("vector", lambda e, pp=pp, x2b=x2b, xr=xr, half=half: e.scalar_tensor_tensor(
                        x2b[:, half * 512:(half + 1) * 512], pp[:], ctx.c("one"),
                        xr[:, half * 512:(half + 1) * 512], ALU.mult, ALU.add),
                        waits=wv, incs=[(ev, 1)] if i >= HALO // 128 else [])
                    if i < HALO // 128:
                        rec.add("vector", lambda e, x2b=x2b, half=half, i=i: e.tensor_scalar(
                            x2b[:, half * 512:(half + 1) * 512],
                            x2b[:, half * 512:(half + 1) * 512],
                            mask_sb[:, i:i + 1], None, ALU.mult), incs=[(ev, 1)])
                    trk[("x2ev", i, half)] = ev.inc()
                rec.add("sync", lambda e, i=i, x2b=x2b: e.dma_start(
                    out=x2_h[i * 128:(i + 1) * 128, :], in_=x2b[:]),
                    waits=[(ev, trk[("x2ev", i, 1)])], incs=[(st, 16)])
                trk[("x2st", i)] = st.inc(16)
                if fm_out:
                    for c in range(8):
                        wt = []
                        if c % 4 == 0:
                            wt = [(ev, trk[("x2ev", i, 1)]), (ev, ctx.sems["ev"].n)]
                        rec.add("tensor", lambda e, x2b=x2b, c=c: e.transpose(
                            tp[:, (c % 4) * 128:((c % 4) + 1) * 128],
                            x2b[:, c * 128:(c + 1) * 128], idn[:]),
                            waits=wt, incs=[(td, 1)] if c % 4 == 3 else [])
                        if c % 4 == 3:
                            tvf = td.inc()
                            wvf = [(td, tvf)]
                            if ("xf", i - 1) in trk:
                                wvf.append((st, trk[("xf", i - 1)]))
                            rec.add("vector", lambda e, c=c: e.tensor_copy(
                                fmst[:, (c - 3) * 128:(c + 1) * 128],
                                tp[:]), waits=wvf, incs=[(ev, 1)])
                            ev.inc()
                    fm_ev = ctx.sems["ev"].n
                    rec.add("sync", lambda e, i=i: e.dma_start(
                        out=x2f_h[:, i * 128:(i + 1) * 128].rearrange(
                            "(c f) t -> f c t", f=128),
                        in_=fmst[:].rearrange("f (c t) -> f c t", t=128)),
                        waits=[(ev, fm_ev)], incs=[(st, 16)])
                    trk[("xf", i)] = st.inc(16)
            trk[("gx2", gi)] = trk[("x2ev", g0 // 128 + nt - 1, 1)]
        rec.add("sync", lambda e: e.nop(), waits=[(st, st.n)])

def build_ffn_mid_phase(fm_out, dbg_group=None):
    ctx = Ctx("emit_ffn_mid")
    nc = ctx.nc
    nc = ctx.nc
    x1_h = nc.dram_tensor("x1_tm", [TC, D], BF16, kind="ExternalInput").ap()
    lg_h = nc.dram_tensor("ln_g", [D], F32, kind="ExternalInput").ap()
    lb_h = nc.dram_tensor("ln_b", [D], F32, kind="ExternalInput").ap()
    w1_h = nc.dram_tensor("w1p", [D, 2 * HIDP], F32, kind="ExternalInput").ap()
    b1_h = nc.dram_tensor("b1p", [2 * HIDP], F32, kind="ExternalInput").ap()
    w2_h = nc.dram_tensor("w2p", [HIDP, D], F32, kind="ExternalInput").ap()
    b2_h = nc.dram_tensor("b2", [D], F32, kind="ExternalInput").ap()
    msk_h = nc.dram_tensor("kvmask", [TC], F32, kind="ExternalInput").ap()
    x2_h = nc.dram_tensor("x2_tm", [TC, D], BF16, kind="ExternalOutput").ap()
    if fm_out:
        x2f_h = nc.dram_tensor("x2_fm", [D, TC], BF16, kind="ExternalOutput").ap()
    if dbg_group is not None:
        dbgh_h = nc.dram_tensor("dbgH", [128, 8 * 512], BF16, kind="ExternalOutput").ap()
        dbgm_h = nc.dram_tensor("dbgM", [128, 22 * 512], BF16, kind="ExternalOutput").ap()

    NCH = 2 * HIDP // 128
    NMID = HIDP // 128
    io = {
        "x1_tm": x1_h,
        "ln_g": lg_h,
        "ln_b": lb_h,
        "w1p": w1_h,
        "b1p": b1_h,
        "w2p": w2_h,
        "b2": b2_h,
        "kvmask": msk_h,
        "x2_tm": x2_h,
    }
    if fm_out:
        io["x2_fm"] = x2f_h
    if dbg_group is not None:
        io.update({"dbgH": dbgh_h, "dbgM": dbgm_h})
    with ctx.es:
        emit_ffn_mid(ctx, io, fm_out=fm_out, dbg_group=dbg_group)
        finish(ctx.nc, ctx.rec, ctx.es, ctx)
    return ctx.nc


def emit_combine(ctx, io):
    nc = ctx.nc
    rec = ctx.rec
    x_h = io["x"]
    s_h = io["s_tm"]
    f_h = io["f_tm"]
    o_h = io["out"]
    if True:
        ctx.setup_consts()
        rec = ctx.rec
        ld = ctx.sem("ld"); st = ctx.sem("st")
        vd = ctx.sem("vd"); sd = ctx.sem("sd")
        xb = [ctx.sbuf(f"xb{i}", [128, D], F32) for i in range(2)]
        sb = [ctx.sbuf(f"sb{i}", [128, D], BF16) for i in range(2)]
        fb = [ctx.sbuf(f"fb{i}", [128, D], BF16) for i in range(2)]
        gb = ctx.sbuf("gb", [128, D], BF16)
        tb = ctx.sbuf("tb", [128, D], F32)
        ob = [ctx.sbuf(f"ob{i}", [128, D], F32) for i in range(2)]
        trk = {}
        ntile = T_OWN // 128
        for ti in range(ntile):
            i = HALO // 128 + ti
            p = ti % 2
            w = []
            if ti >= 2:
                w = [(vd, trk[("mul", ti - 2)])]
            rec.add("sync", lambda e, i=i, p=p: e.dma_start(
                out=xb[p][:], in_=x_h[i * 128:(i + 1) * 128, :]), waits=w, incs=[(ld, 16)])
            rec.add("sync", lambda e, i=i, p=p: e.dma_start(
                out=sb[p][:], in_=s_h[i * 128:(i + 1) * 128, :]), incs=[(ld, 16)])
            rec.add("sync", lambda e, i=i, p=p: e.dma_start(
                out=fb[p][:], in_=f_h[i * 128:(i + 1) * 128, :]), incs=[(ld, 16)])
            lv = ld.inc(48)
            ws = [(ld, lv)]
            if ti >= 1:
                ws.append((vd, trk[("mul", ti - 1)]))
            rec.add("scalar", lambda e, p=p: e.activation(
                gb[:], fb[p][:], AF.Silu), waits=ws, incs=[(sd, 1)])
            s_g = sd.inc()
            wv = [(sd, s_g)]
            if ti >= 1:
                wv.append((vd, trk[("add", ti - 1)]))
            rec.add("vector", lambda e, p=p: e.tensor_tensor(
                tb[:], sb[p][:], gb[:], ALU.mult), waits=wv, incs=[(vd, 1)])
            trk[("mul", ti)] = vd.inc()
            wv2 = []
            if ti >= 2:
                wv2 = [(st, trk[("ost", ti - 2)])]
            rec.add("vector", lambda e, p=p: e.tensor_tensor(
                ob[p][:], xb[p][:], tb[:], ALU.add), waits=wv2, incs=[(vd, 1)])
            trk[("add", ti)] = vd.inc()
            rec.add("sync", lambda e, ti=ti, p=p: e.dma_start(
                out=o_h[ti * 128:(ti + 1) * 128, :], in_=ob[p][:]),
                waits=[(vd, trk[("add", ti)])], incs=[(st, 16)])
            trk[("ost", ti)] = st.inc(16)
        rec.add("sync", lambda e: e.nop(), waits=[(st, st.n)])

def build_combine_phase():
    ctx = Ctx("emit_combine")
    nc = ctx.nc
    nc = ctx.nc
    x_h = nc.dram_tensor("x", [TC, D], F32, kind="ExternalInput").ap()
    s_h = nc.dram_tensor("s_tm", [TC, D], BF16, kind="ExternalInput").ap()
    f_h = nc.dram_tensor("f_tm", [TC, D], BF16, kind="ExternalInput").ap()
    o_h = nc.dram_tensor("out", [T_OWN, D], F32, kind="ExternalOutput").ap()

    io = {
        "x": x_h,
        "s_tm": s_h,
        "f_tm": f_h,
        "out": o_h,
    }
    with ctx.es:
        emit_combine(ctx, io)
        finish(ctx.nc, ctx.rec, ctx.es, ctx)
    return ctx.nc


def _pad_slice(full, lo, hi, axis=0, dtype=None):
    n = full.shape[axis]
    lo2, hi2 = max(lo, 0), min(hi, n)
    sl = [slice(None)] * full.ndim
    sl[axis] = slice(lo2, hi2)
    part = full[tuple(sl)]
    pads = [(0, 0)] * full.ndim
    pads[axis] = (lo2 - lo, hi - hi2)
    out = np.pad(part, pads) if (lo2 - lo or hi - hi2) else part
    return out.astype(dtype) if dtype is not None else out


def _core_ranges():
    return [(c // 2, (c % 2) * T_OWN) for c in range(NCORES)]


_IDENT = np.eye(128, dtype=np.float32)


def run_ln(x_full):
    nc = build_ln_phase()
    in_maps = []
    for b, t0 in _core_ranges():
        xs = _pad_slice(x_full[b], t0 - HALO, t0 + T_OWN, axis=0).astype(np.float32)
        in_maps.append({"x": np.ascontiguousarray(xs), "ident": _IDENT})
    res = run_bass_kernel_spmd(nc, in_maps, core_ids=list(range(NCORES)))
    return [r["xn_tm"] for r in res.results], [r["xn_fm"] for r in res.results]


_MCONST = None


def run_slot(xn_fm_list, inputs):
    global _MCONST
    if _MCONST is None:
        _MCONST = make_slot_mconst()
    nc = build_slot_qkv_phase()
    im = [{"xn_fm": xn_fm_list[c], "Wqkv": inputs["L0_attn_Wqkv"],
           "bqkv": inputs["L0_attn_bqkv"]} for c in range(NCORES)]
    res = run_bass_kernel_spmd(nc, im, core_ids=list(range(NCORES)))
    q = [r["q_fm"] for r in res.results]
    k = [r["k_fm"] for r in res.results]
    v = [r["v65"] for r in res.results]
    nc = build_slot_attn_phase()
    masks = []
    for b, t0 in _core_ranges():
        m = np.zeros(TC, dtype=np.float32)
        m[max(0, HALO - t0):] = 1.0
        masks.append(m)
    im = [{"q_fm": q[c], "k_fm": k[c], "v65": v[c],
           "pos_bias": inputs["L0_attn_pos_bias"], "mconst": _MCONST,
           "kvmask": masks[c], "ident": _IDENT} for c in range(NCORES)]
    res = run_bass_kernel_spmd(nc, im, core_ids=list(range(NCORES)))
    flat = [r["flat"] for r in res.results]
    nc = build_gate_phase()
    im = [{"xn_fm": xn_fm_list[c], "flat": flat[c],
           "Wgate": inputs["L0_attn_Wgate"], "bgate": inputs["L0_attn_bgate"],
           "Wout": inputs["L0_attn_Wout"], "bout": inputs["L0_attn_bout"],
           "ident": _IDENT} for c in range(NCORES)]
    res = run_bass_kernel_spmd(nc, im, core_ids=list(range(NCORES)))
    s = [r["s_tm"] for r in res.results]
    return q, k, v, flat, s


def _prep_ffn_weights(inputs, l):
    w1 = np.asarray(inputs["L0_ffn_w1"][l])
    b1 = np.asarray(inputs["L0_ffn_b1"][l])
    w2 = np.asarray(inputs["L0_ffn_w2"][l])
    w1p = np.zeros((D, 2 * HIDP), np.float32)
    w1p[:, 0:HID] = w1[:, 0:HID]
    w1p[:, HIDP:HIDP + HID] = w1[:, HID:2 * HID]
    b1p = np.zeros(2 * HIDP, np.float32)
    b1p[0:HID] = b1[0:HID]
    b1p[HIDP:HIDP + HID] = b1[HID:2 * HID]
    w2p = np.zeros((HIDP, D), np.float32)
    w2p[0:HID] = w2
    return w1p, b1p, w2p


def _masks():
    ms = []
    for b, t0 in _core_ranges():
        m = np.zeros(TC, dtype=np.float32)
        m[max(0, HALO - t0):] = 1.0
        ms.append(m)
    return ms


def _spmd(nc, im):
    return run_bass_kernel_spmd(nc, im, core_ids=list(range(NCORES))).results


def run_ffn_layer(inputs, l, xin_fm, xres_tm, fm_out):
    masks = _masks()
    tabs = [make_rotary_tables(t0) for _, t0 in _core_ranges()]
    nc = build_ffn_qkv_phase()
    im = [{"xin_fm": xin_fm[c], "wqkv": np.asarray(inputs["L0_ffn_wqkv"][l]),
           "q_scale": np.asarray(inputs["L0_ffn_q_scale"][l]),
           "k_scale": np.asarray(inputs["L0_ffn_k_scale"][l]),
           "cq": tabs[c][0], "sq": tabs[c][1], "ck": tabs[c][2], "sk": tabs[c][3],
           "ident": _IDENT} for c in range(NCORES)]
    r = _spmd(nc, im)
    q = [x["q_fm"] for x in r]; k = [x["k_fm"] for x in r]; v = [x["v65"] for x in r]

    w01a, w01b = make_w01()
    nc = build_ffn_attn_phase()
    im = [{"q_fm": q[c], "k_fm": k[c], "v65": v[c], "w01a": w01a, "w01b": w01b,
           "kvmask": masks[c]} for c in range(NCORES)]
    r = _spmd(nc, im)
    o = [x["o_tm"] for x in r]

    nc = build_ffn_wo_phase()
    im = [{"o_tm": o[c], "wo": np.asarray(inputs["L0_ffn_wo"][l]),
           "xres_tm": xres_tm[c], "kvmask": masks[c], "ident": _IDENT}
          for c in range(NCORES)]
    r = _spmd(nc, im)
    x1 = [x["x1_tm"] for x in r]

    w1p, b1p, w2p = _prep_ffn_weights(inputs, l)
    nc = build_ffn_mid_phase(fm_out)
    im = [{"x1_tm": x1[c], "ln_g": np.asarray(inputs["L0_ffn_ln_g"][l]),
           "ln_b": np.asarray(inputs["L0_ffn_ln_b"][l]), "w1p": w1p, "b1p": b1p,
           "w2p": w2p, "b2": np.asarray(inputs["L0_ffn_b2"][l]),
           "kvmask": masks[c], "ident": _IDENT} for c in range(NCORES)]
    r = _spmd(nc, im)
    x2 = [x["x2_tm"] for x in r]
    x2f = [x["x2_fm"] for x in r] if fm_out else None
    return x1, x2, x2f


def build_mega(dbg=False):
    nc = bass.Bass("TRN2", target_bir_lowering=False)
    EI, IN, EO = "ExternalInput", "Internal", "ExternalOutput"
    if dbg:
        IN = EO

    def d(nm, shp, dt, kind):
        return nc.dram_tensor(nm, list(shp), dt, kind=kind).ap()

    g = {}
    g["x"] = d("x", [TC, D], F32, EI)
    g["ident"] = d("ident", [128, 128], F32, EI)
    g["kvmask"] = d("kvmask", [TC], F32, EI)
    g["Wqkv"] = d("Wqkv", [D, 3 * D], F32, EI)
    g["bqkv"] = d("bqkv", [3 * D], F32, EI)
    g["Wgate"] = d("Wgate", [D, D], F32, EI)
    g["bgate"] = d("bgate", [D], F32, EI)
    g["Wout"] = d("Wout", [D, D], F32, EI)
    g["bout"] = d("bout", [D], F32, EI)
    g["pos_bias"] = d("pos_bias", [44, 16], F32, EI)
    g["mconst"] = d("mconst", [44, NPOS], BF16, EI)
    g["w01a"] = d("w01a", [128, 128], BF16, EI)
    g["w01b"] = d("w01b", [128, 128], BF16, EI)
    for n in ("cq", "sq", "ck", "sk"):
        g[n] = d(n, [TC, 64], F32, EI)
    for l in range(2):
        g[f"wqkv{l}"] = d(f"wqkv{l}", [D, 1536], F32, EI)
        g[f"q_scale{l}"] = d(f"q_scale{l}", [64], F32, EI)
        g[f"k_scale{l}"] = d(f"k_scale{l}", [64], F32, EI)
        g[f"wo{l}"] = d(f"wo{l}", [512, D], F32, EI)
        g[f"ln_g{l}"] = d(f"ln_g{l}", [D], F32, EI)
        g[f"ln_b{l}"] = d(f"ln_b{l}", [D], F32, EI)
        g[f"w1p{l}"] = d(f"w1p{l}", [D, 2 * HIDP], F32, EI)
        g[f"b1p{l}"] = d(f"b1p{l}", [2 * HIDP], F32, EI)
        g[f"w2p{l}"] = d(f"w2p{l}", [HIDP, D], F32, EI)
        g[f"b2{l}"] = d(f"b2{l}", [D], F32, EI)
    g["xn_tm"] = d("xn_tm", [TC, D], BF16, IN)
    g["xn_fm"] = d("xn_fm", [D, TC], BF16, IN)
    g["sq_fm"] = d("sq_fm", [D, TC], BF16, IN)
    g["sk_fm"] = d("sk_fm", [D, TC], BF16, IN)
    g["sv65"] = d("sv65", [TC, SLOT_H * 65], BF16, IN)
    g["ball"] = d("ball", [16, NPOS], F32, IN)
    g["flat"] = d("flat", [TC, D], BF16, IN)
    g["s_tm"] = d("s_tm", [TC, D], BF16, IN)
    for l in range(2):
        g[f"fq{l}"] = d(f"fq{l}", [512, TC], BF16, IN)
        g[f"fk{l}"] = d(f"fk{l}", [512, TC], BF16, IN)
        g[f"fv{l}"] = d(f"fv{l}", [TC, FFN_H * 65], BF16, IN)
        g[f"fo{l}"] = d(f"fo{l}", [TC, 512], BF16, IN)
        g[f"fx1{l}"] = d(f"fx1{l}", [TC, D], BF16, IN)
        g[f"fx2{l}"] = d(f"fx2{l}", [TC, D], BF16, IN)
    g["x2_fm"] = d("x2_fm", [D, TC], BF16, IN)
    g["out"] = d("out", [T_OWN, D], F32, EO)

    mega_es = contextlib.ExitStack()
    barrier = [()]

    shared_sems = {}

    def phase(name, emitfn, io, *args, **kw):
        ctx = Ctx(name, nc, sem_es=mega_es, sems=shared_sems)
        with ctx.es:
            emitfn(ctx, io, *args, **kw)
            finish(nc, ctx.rec, ctx.es, ctx, barrier[0])
        barrier[0] = ctx.final_sems()

    with mega_es:
        phase("ln", emit_ln, {"x": g["x"], "xn_tm": g["xn_tm"], "xn_fm": g["xn_fm"],
                              "ident": g["ident"]})
        phase("sqkv", emit_slot_qkv, {"xn_fm": g["xn_fm"], "Wqkv": g["Wqkv"],
                                      "bqkv": g["bqkv"], "q_fm": g["sq_fm"],
                                      "k_fm": g["sk_fm"], "v65": g["sv65"]})
        phase("sattn", emit_slot_attn, {"q_fm": g["sq_fm"], "k_fm": g["sk_fm"],
                                        "v65": g["sv65"], "pos_bias": g["pos_bias"],
                                        "mconst": g["mconst"], "kvmask": g["kvmask"],
                                        "flat": g["flat"], "ball_scratch": g["ball"],
                                        "ident": g["ident"]})
        phase("gate", emit_gate, {"xn_fm": g["xn_fm"], "flat": g["flat"],
                                  "Wgate": g["Wgate"], "bgate": g["bgate"],
                                  "Wout": g["Wout"], "bout": g["bout"],
                                  "s_tm": g["s_tm"], "ident": g["ident"]})
        for l in range(2):
            xin_fm = g["xn_fm"] if l == 0 else g["x2_fm"]
            xres = g["xn_tm"] if l == 0 else g["fx2" + "0"]
            tq = FQ_T0 if l == 0 else FA_T0
            ta = FA_T0 if l == 0 else 12
            phase(f"fqkv{l}", emit_ffn_qkv,
                  {"xin_fm": xin_fm, "wqkv": g[f"wqkv{l}"],
                   "q_scale": g[f"q_scale{l}"], "k_scale": g[f"k_scale{l}"],
                   "cq": g["cq"], "sq": g["sq"], "ck": g["ck"], "sk": g["sk"],
                   "q_fm": g[f"fq{l}"], "k_fm": g[f"fk{l}"], "v65": g[f"fv{l}"],
                   "ident": g["ident"]}, tq)
            phase(f"fattn{l}", emit_ffn_attn,
                  {"q_fm": g[f"fq{l}"], "k_fm": g[f"fk{l}"], "v65": g[f"fv{l}"],
                   "w01a": g["w01a"], "w01b": g["w01b"], "kvmask": g["kvmask"],
                   "o_tm": g[f"fo{l}"]}, ta)
            phase(f"fwo{l}", emit_ffn_wo,
                  {"o_tm": g[f"fo{l}"], "wo": g[f"wo{l}"], "xres_tm": xres,
                   "kvmask": g["kvmask"], "x1_tm": g[f"fx1{l}"], "ident": g["ident"]}, ta)
            io_mid = {"x1_tm": g[f"fx1{l}"], "ln_g": g[f"ln_g{l}"],
                      "ln_b": g[f"ln_b{l}"], "w1p": g[f"w1p{l}"], "b1p": g[f"b1p{l}"],
                      "w2p": g[f"w2p{l}"], "b2": g[f"b2{l}"], "kvmask": g["kvmask"],
                      "x2_tm": g[f"fx2{l}"], "ident": g["ident"]}
            if l == 0:
                io_mid["x2_fm"] = g["x2_fm"]
            phase(f"fmid{l}", emit_ffn_mid, io_mid, l == 0, ta)
        phase("comb", emit_combine, {"x": g["x"], "s_tm": g["s_tm"],
                                     "f_tm": g["fx21"], "out": g["out"]})
    return nc


_MEGA_NC = None


def kernel(**inputs):
    global _MEGA_NC, _MCONST
    if _MCONST is None:
        _MCONST = make_slot_mconst()
    if _MEGA_NC is None:
        _MEGA_NC = build_mega()
    x_full = np.asarray(inputs["L0_attn_x"])
    masks = _masks()
    w01a, w01b = make_w01()
    preps = [_prep_ffn_weights(inputs, l) for l in range(2)]
    in_maps = []
    for c, (b, t0) in enumerate(_core_ranges()):
        xs = _pad_slice(x_full[b], t0 - HALO, t0 + T_OWN, axis=0).astype(np.float32)
        cq, sq, ck, sk = make_rotary_tables(t0)
        im = {"x": np.ascontiguousarray(xs), "ident": _IDENT, "kvmask": masks[c],
              "Wqkv": np.asarray(inputs["L0_attn_Wqkv"]),
              "bqkv": np.asarray(inputs["L0_attn_bqkv"]),
              "Wgate": np.asarray(inputs["L0_attn_Wgate"]),
              "bgate": np.asarray(inputs["L0_attn_bgate"]),
              "Wout": np.asarray(inputs["L0_attn_Wout"]),
              "bout": np.asarray(inputs["L0_attn_bout"]),
              "pos_bias": np.asarray(inputs["L0_attn_pos_bias"]),
              "mconst": _MCONST, "w01a": w01a, "w01b": w01b,
              "cq": cq, "sq": sq, "ck": ck, "sk": sk}
        for l in range(2):
            im[f"wqkv{l}"] = np.asarray(inputs["L0_ffn_wqkv"][l])
            im[f"q_scale{l}"] = np.asarray(inputs["L0_ffn_q_scale"][l])
            im[f"k_scale{l}"] = np.asarray(inputs["L0_ffn_k_scale"][l])
            im[f"wo{l}"] = np.asarray(inputs["L0_ffn_wo"][l])
            im[f"ln_g{l}"] = np.asarray(inputs["L0_ffn_ln_g"][l])
            im[f"ln_b{l}"] = np.asarray(inputs["L0_ffn_ln_b"][l])
            im[f"w1p{l}"], im[f"b1p{l}"], im[f"w2p{l}"] = preps[l]
            im[f"b2{l}"] = np.asarray(inputs["L0_ffn_b2"][l])
        in_maps.append(im)
    res = run_bass_kernel_spmd(_MEGA_NC, in_maps, core_ids=list(range(NCORES)))
    out = np.empty((B, N, D), np.float32)
    for c, (b, t0) in enumerate(_core_ranges()):
        out[b, t0:t0 + T_OWN] = res.results[c]["out"]
    return out


def kernel_unfused(**inputs):
    x_full = np.asarray(inputs["L0_attn_x"])
    xn_tm, xn_fm = run_ln(x_full)
    _, _, _, _, s = run_slot(xn_fm, inputs)
    _, x2, x2f = run_ffn_layer(inputs, 0, xn_fm, xn_tm, True)
    x2p, f, _ = run_ffn_layer(inputs, 1, x2f, x2, False)
    nc = build_combine_phase()
    im = []
    for c, (b, t0) in enumerate(_core_ranges()):
        xs = _pad_slice(x_full[b], t0 - HALO, t0 + T_OWN, axis=0).astype(np.float32)
        im.append({"x": np.ascontiguousarray(xs), "s_tm": s[c], "f_tm": f[c]})
    r = _spmd(nc, im)
    out = np.empty((B, N, D), np.float32)
    for c, (b, t0) in enumerate(_core_ranges()):
        out[b, t0:t0 + T_OWN] = r[c]["out"]
    return out
